# version 14
# baseline (speedup 1.0000x reference)
import math
import numpy as np
import ml_dtypes
import concourse.bass as bass
import concourse.mybir as mybir
from concourse.bass_utils import run_bass_kernel_spmd

F32 = mybir.dt.float32
BF16 = mybir.dt.bfloat16
I32 = mybir.dt.int32
AF = mybir.ActivationFunctionType
ALU = mybir.AluOpType
AX = mybir.AxisListType


ENGINES = ("pe", "act", "dve", "pool", "sp")
N_DMA_SEMS = 6
DMA_SEMS = {q: [f"d{q}{i}" for i in range(N_DMA_SEMS)] for q in ("sp", "act", "pool")}
ALL_DMA_SEMS = [k for v in DMA_SEMS.values() for k in v]


class Buf:
    __slots__ = ("name", "w", "r")

    def __init__(self, name):
        self.name = name
        self.w = None
        self.r = {}


class Op:
    __slots__ = ("eng", "fn", "waits", "inc", "tag")

    def __init__(self, eng, fn, waits, inc, tag):
        self.eng = eng
        self.fn = fn
        self.waits = waits
        self.inc = inc
        self.tag = tag


class Ctx:
    def __init__(self, nc, stack=None):
        self.nc = nc
        self.stack = stack
        self.ops = []
        self.counts = {}
        self.waited = {}
        for e in ENGINES:
            self.counts[e] = 0
        for ks in DMA_SEMS.values():
            for k in ks:
                self.counts[k] = 0
        self._dma_rot = {q: 0 for q in DMA_SEMS}

    def _collect_waits(self, eng, reads, writes):
        need = {}
        def add(tok):
            if tok is None:
                return
            k, v = tok
            if need.get(k, -1) < v:
                need[k] = v
        for b in reads:
            add(b.w)
        for b in writes:
            add(b.w)
            for k, v in b.r.items():
                add((k, v))
        waits = []
        for k, v in need.items():
            if k == eng == "pe":
                continue
            done = self.waited.get((eng, k), 0)
            if k == eng and v <= self.counts[eng]:
                pass
            if v > done:
                waits.append((k, v))
                self.waited[(eng, k)] = v
        return waits

    def op(self, eng, fn, reads=(), writes=(), tag="", inc=True):
        waits = self._collect_waits(eng, reads, writes)
        inc_tok = None
        if inc:
            self.counts[eng] += 1
            inc_tok = (eng, 1)
        self.ops.append(Op(eng, fn, waits, inc_tok, tag))
        tok = (eng, self.counts[eng])
        for b in writes:
            b.w = tok
            b.r = {}
        for b in reads:
            if b.r.get(eng, 0) < tok[1]:
                b.r[eng] = tok[1]
        return tok

    def mm(self, fn, reads, write, acc_first=True, acc_last=True, tag="mm"):
        waits = self._collect_waits("pe", reads, (write,) if acc_first else ())
        inc_tok = None
        if acc_last:
            self.counts["pe"] += 1
            inc_tok = ("pe", 1)
        self.ops.append(Op("pe", fn, waits, inc_tok, tag))
        if acc_last:
            tok = ("pe", self.counts["pe"])
            write.w = tok
            write.r = {}
        for b in reads:
            b.r["pe"] = self.counts["pe"] + (0 if acc_last else 1)

    def dma(self, queue, fn, reads=(), writes=(), tag="dma", sem=None):
        if sem is None:
            sem = self._dma_rot[queue]
            self._dma_rot[queue] = (sem + 1) % N_DMA_SEMS
        dkey = DMA_SEMS[queue][sem]
        waits = self._collect_waits(queue, reads, writes)
        prev = self.counts[dkey]
        if prev > 0 and self.waited.get((queue, dkey), 0) < prev:
            waits.append((dkey, prev))
            self.waited[(queue, dkey)] = prev
        self.counts[dkey] += 16
        self.ops.append(Op(queue, fn, waits, (dkey, 16), tag))
        tok = (dkey, self.counts[dkey])
        for b in writes:
            b.w = tok
            b.r = {}
        for b in reads:
            if b.r.get(dkey, 0) < tok[1]:
                b.r[dkey] = tok[1]

    def barrier(self):
        snapshot = [(k, v) for k, v in self.counts.items() if v > 0]
        for e in ENGINES:
            for k, v in snapshot:
                if k == e:
                    continue
                if self.waited.get((e, k), 0) < v:
                    self.waited[(e, k)] = v
                    self.ops.append(Op(e, None, [(k, v)], None, "barrier"))

    def finalize(self, blk, sems):
        per_eng = {e: [] for e in ENGINES}
        for o in self.ops:
            per_eng[o.eng].append(o)
        attr = {"pe": "tensor", "act": "scalar", "dve": "vector",
                "pool": "gpsimd", "sp": "sync"}

        def run_engine(ops_list):
            def body(eng):
                for o in ops_list:
                    for (k, v) in o.waits:
                        eng.wait_ge(sems[k], v)
                    if o.fn is None:
                        continue
                    ins = o.fn(eng)
                    if o.inc is not None:
                        ins.then_inc(sems[o.inc[0]], o.inc[1])
            return body

        for e in ENGINES:
            if per_eng[e]:
                getattr(blk, attr[e])(run_engine(per_eng[e]))



class Tile:
    __slots__ = ("t", "buf")

    def __init__(self, t, name):
        self.t = t
        self.buf = Buf(name)

    def __getitem__(self, idx):
        return self.t[idx]

    @property
    def ap(self):
        return self.t[:]


def sbuf(ctx, name, shape, dtype):
    return Tile(ctx.stack.enter_context(ctx.nc.sbuf_tensor(name, shape, dtype)), name)


def psum(ctx, name, shape, dtype=F32):
    return Tile(ctx.stack.enter_context(ctx.nc.psum_tensor(name, shape, dtype)), name)


def dram(nc, name, shape, dtype, kind="Internal"):
    return Tile(nc.dram_tensor(name, shape, dtype, kind=kind).ap(), name)



D = 1024
NH0 = 16
HD0 = 64
FF0 = 3072
QK1 = 1024
V1 = 2048
NH1 = 8
FF1 = 4096
EPS = 1e-5

BF = ml_dtypes.bfloat16



def host_constants(T):
    c = {}
    c["identb"] = np.eye(128, dtype=BF)
    c["onescol"] = np.ones((128, 1), np.float32)
    c["onesrow"] = np.ones((1, 128), np.float32)
    c["onesrow_b"] = np.ones((1, 128), BF)
    hd2 = HD0 // 2
    inv = 1.0 / 10000.0 ** (np.arange(0, HD0, 2, dtype=np.float64) / HD0)
    t = np.arange(T, dtype=np.float64)
    fr = np.outer(inv, t)
    cos = np.cos(fr); sin = np.sin(fr)
    C = np.concatenate([cos, cos], axis=0)
    S = np.concatenate([sin, sin], axis=0)
    C2 = np.concatenate([C, C], axis=0)
    S2 = np.concatenate([S, S], axis=0)
    sc = 1.0 / math.sqrt(HD0)
    c["r0cq"] = (C2 * sc).astype(BF); c["r0sq"] = (S2 * sc).astype(BF)
    c["r0ck"] = C2.astype(BF); c["r0sk"] = S2.astype(BF)
    P0 = np.zeros((128, 128), np.float32)
    for base in (0, 64):
        for r in range(hd2):
            P0[base + r, base + hd2 + r] = -1.0
            P0[base + hd2 + r, base + r] = 1.0
    c["p0t"] = P0.T.copy().astype(BF)
    dim_half = QK1 // 2
    base_scale = (np.arange(0, QK1, 2, dtype=np.float64) + 0.4 * QK1) / (1.4 * QK1)
    pos = np.arange(T, dtype=np.float64)
    scale = base_scale[:, None] ** (pos[None, :] / 512.0)
    inv1 = 1.0 / 10000.0 ** (np.arange(dim_half, dtype=np.float64) / dim_half)
    sinu = inv1[:, None] * pos[None, :]
    sn = np.sin(sinu); cs = np.cos(sinu)
    rep = lambda a: np.repeat(a, 2, axis=0)
    c["x1cq"] = rep(cs * scale).astype(BF)
    c["x1sq"] = rep(sn * scale).astype(BF)
    c["x1ck"] = rep(cs / scale).astype(BF)
    c["x1sk"] = rep(sn / scale).astype(BF)
    P1 = np.zeros((128, 128), np.float32)
    for i in range(64):
        P1[2 * i, 2 * i + 1] = -1.0
        P1[2 * i + 1, 2 * i] = 1.0
    c["p1t"] = P1.T.copy().astype(BF)
    cz = np.triu(np.ones((128, 128), np.float32)).astype(BF)
    c["causal"] = cz
    c["iotar"] = np.arange(128, dtype=np.float32)[None, :]
    return c



def emit_decayprep(ctx, nc, g, T):
    dk = sbuf(ctx, "dp_dk", [1, 8], F32)
    ln = sbuf(ctx, "dp_ln", [1, 8], F32)
    nln = sbuf(ctx, "dp_nln", [1, 8], F32)
    g128 = sbuf(ctx, "dp_g128", [1, 8], F32)
    rowq = sbuf(ctx, "dp_rowq", [1, 128], F32)
    rowk = sbuf(ctx, "dp_rowk", [1, 128], F32)
    rows_bq = sbuf(ctx, "dp_rbq", [1, 1024], BF16)
    rows_bk = sbuf(ctx, "dp_rbk", [1, 1024], BF16)
    iot = sbuf(ctx, "dp_iot", [1, 128], F32)
    onr = sbuf(ctx, "dp_onr", [1, 128], BF16)
    evac = sbuf(ctx, "dp_evac", [128, 128], BF16)
    evf = sbuf(ctx, "dp_evf", [128, 8], F32)
    ps = psum(ctx, "dp_ps", [128, 128], F32)

    ctx.dma("sp", lambda e: e.dma_start(out=dk[0:1, :], in_=g["L1_attn_decay"].ap),
            writes=[dk.buf])
    ctx.dma("sp", lambda e: e.dma_start(out=iot[:], in_=g["c_iotar"].ap), writes=[iot.buf])
    ctx.dma("sp", lambda e: e.dma_start(out=onr[:], in_=g["c_onesrow_b"].ap), writes=[onr.buf])
    ctx.op("act", lambda s: s.activation(ln[:], dk[:], AF.Ln), reads=[dk.buf], writes=[ln.buf])
    ctx.op("dve", lambda v: v.tensor_scalar(out=nln[:], in0=ln[:], scalar1=-1.0, scalar2=None, op0=ALU.mult),
           reads=[ln.buf], writes=[nln.buf])
    ctx.op("act", lambda s: s.activation(g128[:], ln[:], AF.Exp, scale=128.0),
           reads=[ln.buf], writes=[g128.buf])
    for h in range(NH1):
        ctx.op("act", lambda s, h=h: s.activation(rowq[:], iot[:], AF.Exp, scale=ln[0:1, h:h + 1]),
               reads=[iot.buf, ln.buf], writes=[rowq.buf])
        ctx.op("dve", lambda v: v.tensor_scalar(out=rowq[:], in0=rowq[:], scalar1=1.0 / math.sqrt(128.0),
                                                scalar2=None, op0=ALU.mult),
               reads=[rowq.buf], writes=[rowq.buf])
        ctx.op("act", lambda s, h=h: s.activation(rowk[:], iot[:], AF.Exp, scale=nln[0:1, h:h + 1]),
               reads=[iot.buf, nln.buf], writes=[rowk.buf])
        ctx.op("dve", lambda v, h=h: v.tensor_copy(rows_bq[0:1, 128 * h:128 * (h + 1)], rowq[:]),
               reads=[rowq.buf], writes=[rows_bq.buf])
        ctx.op("dve", lambda v, h=h: v.tensor_copy(rows_bk[0:1, 128 * h:128 * (h + 1)], rowk[:]),
               reads=[rowk.buf], writes=[rows_bk.buf])
    for (rows, out) in ((rows_bq, "dk_bq"), (rows_bk, "dk_bk")):
        for h in range(NH1):
            ctx.mm(lambda t, rows=rows, h=h: t.matmul(ps[:], onr[:], rows[0:1, 128 * h:128 * (h + 1)],
                                                      start=True, stop=True),
                   [onr.buf, rows.buf], ps.buf)
            ctx.op("dve", lambda v: v.tensor_copy(evac[:], ps[:]), reads=[ps.buf], writes=[evac.buf])
            ctx.dma("sp", lambda e, out=out, h=h: e.dma_start(out=g[out].ap[:, 128 * h:128 * (h + 1)],
                                                              in_=evac[:]),
                    reads=[evac.buf])
    onrf = sbuf(ctx, "dp_onrf", [1, 128], F32)
    ps8 = psum(ctx, "dp_ps8", [128, 8], F32)
    ctx.dma("sp", lambda e: e.dma_start(out=onrf[:], in_=g["c_onesrow"].ap), writes=[onrf.buf])
    ctx.mm(lambda t: t.matmul(ps8[:], onrf[:], g128[:], start=True, stop=True),
           [onrf.buf, g128.buf], ps8.buf)
    ctx.op("dve", lambda v: v.tensor_copy(evf[:], ps8[:]), reads=[ps8.buf], writes=[evf.buf])
    ctx.dma("sp", lambda e: e.dma_start(out=g["dk_g128"].ap, in_=evf[:]), reads=[evf.buf])



def emit_pre0(ctx, nc, g, T):
    nt5 = T // 512
    wq = [sbuf(ctx, f"p1_wq{c}", [128, 3072], BF16) for c in range(8)]
    for c in range(8):
        ctx.dma("pool", lambda e, c=c: e.dma_start(out=wq[c][:], in_=g["L0_attn_Wqkv"].ap[128 * c:128 * (c + 1), :]),
                writes=[wq[c].buf], tag="wq")
    bq = sbuf(ctx, "p1_bq", [128, 24], F32)
    ctx.dma("sp", lambda e: e.dma_start(out=bq[:], in_=g["L0_attn_bqkv"].ap),
            writes=[bq.buf])
    tabs = {}
    for nm in ("r0cq", "r0sq", "r0ck", "r0sk"):
        tabs[nm] = sbuf(ctx, f"p1_{nm}", [128, T], BF16)
        ctx.dma("sp", lambda e, nm=nm: e.dma_start(out=tabs[nm][:], in_=g["c_" + nm].ap),
                writes=[tabs[nm].buf])
    p0t = sbuf(ctx, "p1_p0t", [128, 128], BF16)
    idb = sbuf(ctx, "p1_idb", [128, 128], BF16)
    ctx.dma("sp", lambda e: e.dma_start(out=p0t[:], in_=g["c_p0t"].ap), writes=[p0t.buf])
    ctx.dma("sp", lambda e: e.dma_start(out=idb[:], in_=g["c_identb"].ap), writes=[idb.buf])
    onescol = sbuf(ctx, "p1_ones", [128, 1], F32)
    onesrow = sbuf(ctx, "p1_onesr", [1, 128], F32)
    ctx.dma("sp", lambda e: e.dma_start(out=onescol[:], in_=g["c_onescol"].ap), writes=[onescol.buf])
    ctx.dma("sp", lambda e: e.dma_start(out=onesrow[:], in_=g["c_onesrow"].ap), writes=[onesrow.buf])

    xt2 = [sbuf(ctx, f"p1_xt{i}", [128, 1024], F32) for i in range(2)]
    xsq2 = [sbuf(ctx, f"p1_xsq{i}", [128, 1024], F32) for i in range(2)]
    mvec2 = [sbuf(ctx, f"p1_m{i}", [128, 1], F32) for i in range(2)]
    ssq2 = [sbuf(ctx, f"p1_ssq{i}", [128, 1], F32) for i in range(2)]
    var2 = [sbuf(ctx, f"p1_var{i}", [128, 1], F32) for i in range(2)]
    rstd2 = [sbuf(ctx, f"p1_rstd{i}", [128, 1], F32) for i in range(2)]
    xnb2 = [sbuf(ctx, f"p1_xnb{i}", [128, 1024], BF16) for i in range(2)]
    xnf = [sbuf(ctx, f"p1_xnf{c}", [128, 512], BF16) for c in range(8)]
    qraw2 = [sbuf(ctx, f"p1_qraw{i}", [128, 512], BF16) for i in range(2)]
    tmp12 = [sbuf(ctx, f"p1_tmp1{i}", [128, 512], F32) for i in range(2)]
    qhat2 = [sbuf(ctx, f"p1_qhat{i}", [128, 512], BF16) for i in range(2)]
    sqv2 = [sbuf(ctx, f"p1_sqv{i}", [128, 512], F32) for i in range(2)]
    vout2 = [sbuf(ctx, f"p1_vout{i}", [128, 1040], BF16) for i in range(2)]
    mq = sbuf(ctx, "p1_mq", [1, 16], F32)
    mk = sbuf(ctx, "p1_mk", [1, 16], F32)
    mtmp = sbuf(ctx, "p1_mtmp", [1, 1], F32)
    pacc2 = [psum(ctx, f"p1_pacc{i}", [128, 512], F32) for i in range(2)]
    prot2 = [psum(ctx, f"p1_prot{i}", [128, 512], F32) for i in range(2)]
    ptr = psum(ctx, "p1_ptr", [128, 128], BF16)
    pn1 = psum(ctx, "p1_pn1", [1, 512], F32)
    pn2 = psum(ctx, "p1_pn2", [1, 512], F32)
    pv = psum(ctx, "p1_pv", [128, 512], F32)

    ctx.op("dve", lambda v: v.memset(mq[:], 0.0), writes=[mq.buf])
    ctx.op("dve", lambda v: v.memset(mk[:], 0.0), writes=[mk.buf])

    def norms_update(src, acc, base, psn, hidx):
        ctx.mm(lambda t: t.matmul(psn[:], onescol[base:base + 64, :], src[base:base + 64, :],
                                  start=True, stop=True),
               [onescol.buf, src.buf], psn.buf, tag="norm")
        ctx.op("dve", lambda v: v.reduce_max(mtmp[:], psn[:], axis=AX.X),
               reads=[psn.buf], writes=[mtmp.buf])
        ctx.op("dve", lambda v: v.tensor_tensor(acc[0:1, hidx:hidx + 1], acc[0:1, hidx:hidx + 1],
                                                mtmp[:], op=ALU.max),
               reads=[acc.buf, mtmp.buf], writes=[acc.buf])

    for it in range(nt5):
        ts = slice(512 * it, 512 * (it + 1))
        for sub in range(4):
            p = sub % 2
            xt = xt2[p]; xsq = xsq2[p]; mvec = mvec2[p]; ssq = ssq2[p]
            var = var2[p]; rstd = rstd2[p]; xnb = xnb2[p]
            t0 = 512 * it + 128 * sub
            ctx.dma("sp", lambda e, t0=t0, xt=xt: e.dma_start(out=xt[:], in_=g["x"].ap[t0:t0 + 128, :]),
                    writes=[xt.buf], tag="ldx")
            ctx.op("dve", lambda v, xt=xt, mvec=mvec: v.reduce_sum(mvec[:], xt[:], axis=AX.X),
                   reads=[xt.buf], writes=[mvec.buf])
            ctx.op("dve", lambda v, mvec=mvec: v.tensor_scalar(out=mvec[:], in0=mvec[:], scalar1=1.0 / D,
                                                    scalar2=None, op0=ALU.mult),
                   reads=[mvec.buf], writes=[mvec.buf])
            ctx.op("dve", lambda v, xt=xt, xsq=xsq: v.tensor_tensor(xsq[:], xt[:], xt[:], op=ALU.mult),
                   reads=[xt.buf], writes=[xsq.buf])
            ctx.op("dve", lambda v, xsq=xsq, ssq=ssq: v.reduce_sum(ssq[:], xsq[:], axis=AX.X),
                   reads=[xsq.buf], writes=[ssq.buf])
            ctx.op("dve", lambda v, mvec=mvec, var=var: v.tensor_tensor(var[:], mvec[:], mvec[:], op=ALU.mult),
                   reads=[mvec.buf], writes=[var.buf])
            ctx.op("dve", lambda v, ssq=ssq, var=var: v.tensor_scalar(out=var[:], in0=ssq[:], scalar1=1.0 / D,
                                                    scalar2=var[:], op0=ALU.mult, op1=ALU.subtract),
                   reads=[ssq.buf, var.buf], writes=[var.buf])
            ctx.op("dve", lambda v, var=var: v.tensor_scalar(out=var[:], in0=var[:], scalar1=EPS,
                                                    scalar2=None, op0=ALU.add),
                   reads=[var.buf], writes=[var.buf])
            ctx.op("act", lambda s, var=var, rstd=rstd: s.activation(rstd[:], var[:], AF.Ln),
                   reads=[var.buf], writes=[rstd.buf])
            ctx.op("act", lambda s, rstd=rstd: s.activation(rstd[:], rstd[:], AF.Exp, scale=-0.5),
                   reads=[rstd.buf], writes=[rstd.buf])
            ctx.op("dve", lambda v, xt=xt, mvec=mvec, rstd=rstd, xnb=xnb: v.tensor_scalar(out=xnb[:], in0=xt[:], scalar1=mvec[:],
                                                    scalar2=rstd[:], op0=ALU.subtract, op1=ALU.mult),
                   reads=[xt.buf, mvec.buf, rstd.buf], writes=[xnb.buf])
            for c in range(8):
                ctx.op("pe", lambda t, c=c, xnb=xnb: t.transpose(ptr[:], xnb[:, 128 * c:128 * (c + 1)], idb[:]),
                       reads=[xnb.buf, idb.buf], writes=[ptr.buf])
                ctx.op("dve", lambda v, c=c, sub=sub: v.tensor_copy(xnf[c][:, 128 * sub:128 * (sub + 1)], ptr[:]),
                       reads=[ptr.buf], writes=[xnf[c].buf])
        for c in range(8):
            ctx.dma("act", lambda e, c=c, ts=ts: e.dma_start(out=g["xn0"].ap[128 * c:128 * (c + 1), ts],
                                                             in_=xnf[c][:]),
                    reads=[xnf[c].buf], tag="stxn")
        for f in range(16):
            is_q = f < 8
            pacc = pacc2[f % 2]; prot = prot2[f % 2]
            qraw = qraw2[f % 2]; tmp1 = tmp12[f % 2]; sqv = sqv2[f % 2]; qhat = qhat2[f % 2]
            for c in range(8):
                ctx.mm(lambda t, f=f, c=c, pacc=pacc: t.matmul(pacc[:], wq[c][:, 128 * f:128 * (f + 1)], xnf[c][:],
                                                    start=(c == 0), stop=(c == 7)),
                       [wq[c].buf, xnf[c].buf], pacc.buf, acc_first=(c == 0), acc_last=(c == 7), tag="qk_mm")
            ctx.op("dve", lambda v, f=f, pacc=pacc, qraw=qraw: v.tensor_scalar(out=qraw[:], in0=pacc[:], scalar1=bq[:, f:f + 1],
                                                         scalar2=None, op0=ALU.add),
                   reads=[pacc.buf, bq.buf], writes=[qraw.buf])
            ctx.mm(lambda t, prot=prot, qraw=qraw: t.matmul(prot[:], p0t[:], qraw[:], start=True, stop=True),
                   [p0t.buf, qraw.buf], prot.buf, tag="rot")
            ct = tabs["r0cq" if is_q else "r0ck"]
            stab = tabs["r0sq" if is_q else "r0sk"]
            ctx.op("dve", lambda v, ct=ct, ts=ts, qraw=qraw, tmp1=tmp1: v.tensor_tensor(tmp1[:], qraw[:], ct[:, ts], op=ALU.mult),
                   reads=[qraw.buf, ct.buf], writes=[tmp1.buf])
            ctx.op("dve", lambda v, stab=stab, ts=ts, prot=prot, sqv=sqv: v.tensor_tensor(sqv[:], prot[:], stab[:, ts], op=ALU.mult),
                   reads=[prot.buf, stab.buf], writes=[sqv.buf])
            ctx.op("dve", lambda v, tmp1=tmp1, sqv=sqv, qhat=qhat: v.tensor_tensor(qhat[:], tmp1[:], sqv[:], op=ALU.add),
                   reads=[tmp1.buf, sqv.buf], writes=[qhat.buf])
            dst = "q0h" if is_q else "k0r"
            fc = f if is_q else f - 8
            ctx.dma("act", lambda e, dst=dst, fc=fc, ts=ts, qhat=qhat: e.dma_start(
                out=g[dst].ap[128 * fc:128 * (fc + 1), ts], in_=qhat[:]),
                reads=[qhat.buf], tag="stqk")
            ctx.op("dve", lambda v, qhat=qhat, sqv=sqv: v.tensor_tensor(sqv[:], qhat[:], qhat[:], op=ALU.mult),
                   reads=[qhat.buf], writes=[sqv.buf])
            acc = mq if is_q else mk
            norms_update(sqv, acc, 0, pn1, 2 * fc)
            norms_update(sqv, acc, 64, pn2, 2 * fc + 1)
        for sub in range(4):
            vout = vout2[sub % 2]
            for vh in range(2):
                for c in range(8):
                    ctx.mm(lambda t, sub=sub, vh=vh, c=c: t.matmul(
                        pv[:], xnf[c][:, 128 * sub:128 * (sub + 1)],
                        wq[c][:, 2048 + 512 * vh:2048 + 512 * (vh + 1)],
                        start=(c == 0), stop=(c == 7)),
                        [xnf[c].buf, wq[c].buf], pv.buf, acc_first=(c == 0), acc_last=(c == 7), tag="v_mm")
                ctx.op("dve", lambda v, vh=vh, vout=vout: v.tensor_copy(
                    bass.AP(vout[:].tensor, vout[:].offset + 65 * 8 * vh,
                            [list(vout[:].ap[0]), [65, 8], [1, 64]]),
                    bass.AP(pv[:].tensor, pv[:].offset, [list(pv[:].ap[0]), [64, 8], [1, 64]])),
                    reads=[pv.buf], writes=[vout.buf], tag="v_evac")
            ctx.op("dve", lambda v, vout=vout: v.memset(
                bass.AP(vout[:].tensor, vout[:].offset + 64, [list(vout[:].ap[0]), [65, 16]]), 1.0),
                writes=[vout.buf])
            t0 = 512 * it + 128 * sub
            ctx.dma("sp", lambda e, t0=t0, vout=vout: e.dma_start(out=g["v0p"].ap[t0:t0 + 128, :], in_=vout[:]),
                    reads=[vout.buf], tag="stv")
    prod = sbuf(ctx, "p1_prod", [1, 16], F32)
    negc = sbuf(ctx, "p1_negc", [1, 16], F32)
    chb = sbuf(ctx, "p1_chb", [128, 16], F32)
    pch = pv
    ctx.op("dve", lambda v: v.tensor_tensor(prod[:], mq[:], mk[:], op=ALU.mult),
           reads=[mq.buf, mk.buf], writes=[prod.buf])
    ctx.op("dve", lambda v: v.tensor_scalar(out=prod[:], in0=prod[:], scalar1=1e-30,
                                            scalar2=None, op0=ALU.add),
           reads=[prod.buf], writes=[prod.buf])
    ctx.op("act", lambda s: s.activation(prod[:], prod[:], AF.Ln),
           reads=[prod.buf], writes=[prod.buf])
    ctx.op("act", lambda s: s.activation(negc[:], prod[:], AF.Exp, scale=0.5),
           reads=[prod.buf], writes=[negc.buf])
    ctx.op("dve", lambda v: v.tensor_scalar(out=negc[:], in0=negc[:], scalar1=-1.0, scalar2=None,
                                            op0=ALU.mult),
           reads=[negc.buf], writes=[negc.buf])
    ctx.mm(lambda t: t.matmul(pch[:, 0:16], onesrow[:], negc[:], start=True, stop=True),
           [onesrow.buf, negc.buf], pch.buf)
    ctx.op("dve", lambda v: v.tensor_copy(chb[:], pch[:, 0:16]), reads=[pch.buf], writes=[chb.buf])
    ctx.dma("sp", lambda e: e.dma_start(out=g["ch0"].ap, in_=chb[:]), reads=[chb.buf])


def fm_block_ap(dr, T, ts0, width, nchunk=8, rows=128):
    a = dr.ap
    return bass.AP(a.tensor, a.offset + ts0, [[T, rows], [rows * T, nchunk], [1, width]])


def blk_slice(tile, c, width):
    return tile[:, c * width:(c + 1) * width]



def emit_attn0(ctx, nc, g, T):
    ntk = T // 128
    nq = T // 512
    kf = [sbuf(ctx, f"p2_kf{c}", [128, T], BF16) for c in range(8)]
    for c in range(8):
        ctx.dma("sp", lambda e, c=c: e.dma_start(out=kf[c][:], in_=g["k0r"].ap[128 * c:128 * (c + 1), :]),
                writes=[kf[c].buf], tag="ldk")
    vp = [sbuf(ctx, f"p2_vp{i}", [128, 1040], BF16) for i in range(ntk)]
    for i in range(ntk):
        ctx.dma("act", lambda e, i=i: e.dma_start(out=vp[i][:], in_=g["v0p"].ap[128 * i:128 * (i + 1), :]),
                writes=[vp[i].buf], tag="ldv")
    wo = [sbuf(ctx, f"p2_wo{c}", [128, 1024], BF16) for c in range(8)]
    for c in range(8):
        ctx.dma("pool", lambda e, c=c: e.dma_start(out=wo[c][:], in_=g["L0_attn_Wout"].ap[128 * c:128 * (c + 1), :]),
                writes=[wo[c].buf], tag="ldwo")
    ch = sbuf(ctx, "p2_ch", [128, 16], F32)
    ctx.dma("sp", lambda e: e.dma_start(out=ch[:], in_=g["ch0"].ap), writes=[ch.buf])
    bq = sbuf(ctx, "p2_bq", [128, 24], F32)
    ctx.dma("sp", lambda e: e.dma_start(out=bq[:], in_=g["L0_attn_bqkv"].ap),
            writes=[bq.buf])
    bo = sbuf(ctx, "p2_bo", [128, 8], F32)
    ctx.dma("sp", lambda e: e.dma_start(out=bo[:], in_=g["L0_attn_bout"].ap),
            writes=[bo.buf])
    onesrb = sbuf(ctx, "p2_onesrb", [1, 128], BF16)
    ctx.dma("sp", lambda e: e.dma_start(out=onesrb[:], in_=g["c_onesrow_b"].ap), writes=[onesrb.buf])

    bqvb = sbuf(ctx, "p2_bqvb", [128, 8], BF16)
    ctx.op("dve", lambda v: v.tensor_copy(bqvb[:], bq[:, 16:24]), reads=[bq.buf], writes=[bqvb.buf])
    boutp = sbuf(ctx, "p2_boutp", [128, 8], F32)

    qf2 = [sbuf(ctx, f"p2_qf{i}", [128, 8 * 512], BF16) for i in range(2)]
    aTA2 = [sbuf(ctx, f"p2_aTA{i}", [128, 1024], BF16) for i in range(2)]
    aTB2 = [sbuf(ctx, f"p2_aTB{i}", [128, 1024], BF16) for i in range(2)]
    pvsb2 = [[sbuf(ctx, f"p2_pvsb{p}_{i}", [128, 512], F32) for i in range(2)] for p in range(2)]
    dclq = [[sbuf(ctx, f"p2_dcl{p}_{i}", [1, 512], F32) for i in range(2)] for p in range(2)]
    recbq = [[sbuf(ctx, f"p2_recb{p}_{i}", [1, 512], BF16) for i in range(2)] for p in range(2)]
    bcf2 = [sbuf(ctx, f"p2_bcf{i}", [64, 512], F32) for i in range(2)]
    atf = [sbuf(ctx, f"p2_atf{c}", [128, 512], BF16) for c in range(8)]
    s0t2 = [sbuf(ctx, f"p2_s0t{i}", [128, 512], BF16) for i in range(2)]
    pscA = psum(ctx, "p2_pscA", [128, 1024], F32)
    pscB = psum(ctx, "p2_pscB", [128, 1024], F32)
    pvA = psum(ctx, "p2_pvA", [128, 512], F32)
    pvB = psum(ctx, "p2_pvB", [128, 512], F32)
    pbc = psum(ctx, "p2_pbc", [64, 512], F32)
    po = psum(ctx, "p2_po", [128, 512], F32)

    for dc in range(8):
        for c in range(8):
            ctx.mm(lambda t, dc=dc, c=c: t.matmul(po[:, dc:dc + 1], wo[c][:, 128 * dc:128 * (dc + 1)],
                                                  bqvb[:, c:c + 1], start=(c == 0), stop=(c == 7)),
                   [wo[c].buf, bqvb.buf], po.buf, acc_first=(c == 0), acc_last=(c == 7), tag="boutp")
    ctx.op("dve", lambda v: v.tensor_tensor(boutp[:], bo[:], po[:, 0:8], op=ALU.add),
           reads=[bo.buf, po.buf], writes=[boutp.buf])

    wup = sbuf(ctx, "p2_wup", [128, 512], BF16)
    ctx.op("dve", lambda v: v.memset(wup[:], 0.0), writes=[wup.buf])
    def pe_warmup(n):
        for i in range(n):
            ctx.mm(lambda t: t.matmul(po[:], kf[0][:, 0:128], wup[:], start=True, stop=True),
                   [kf[0].buf, wup.buf], po.buf, tag="warm")

    nhalf = ntk // 2

    def pair_main(hp, qf):
        for tkp in range(nhalf + 1):
            if tkp < nhalf:
                aTA = aTA2[tkp % 2]; aTB = aTB2[tkp % 2]
                for half in range(2):
                    tk = 2 * tkp + half
                    ctx.mm(lambda t, hp=hp, tk=tk, half=half, qf=qf: t.matmul(
                        pscA[:, 512 * half:512 * (half + 1)],
                        kf[hp][0:64, 128 * tk:128 * (tk + 1)],
                        qf[0:64, 512 * hp:512 * (hp + 1)], start=True, stop=True),
                        [kf[hp].buf, qf.buf], pscA.buf,
                        acc_first=(half == 0), acc_last=(half == 1), tag="scoreA")
                    ctx.mm(lambda t, hp=hp, tk=tk, half=half, qf=qf: t.matmul(
                        pscB[:, 512 * half:512 * (half + 1)],
                        kf[hp][64:128, 128 * tk:128 * (tk + 1)],
                        qf[64:128, 512 * hp:512 * (hp + 1)], start=True, stop=True),
                        [kf[hp].buf, qf.buf], pscB.buf,
                        acc_first=(half == 0), acc_last=(half == 1), tag="scoreB")
                ctx.op("act", lambda s, hp=hp, aTA=aTA: s.activation(aTA[:], pscA[:], AF.Exp,
                                                                    bias=ch[:, 2 * hp:2 * hp + 1]),
                       reads=[pscA.buf, ch.buf], writes=[aTA.buf], tag="expA")
                ctx.op("act", lambda s, hp=hp, aTB=aTB: s.activation(aTB[:], pscB[:], AF.Exp,
                                                                    bias=ch[:, 2 * hp + 1:2 * hp + 2]),
                       reads=[pscB.buf, ch.buf], writes=[aTB.buf], tag="expB")
            if tkp >= 1:
                kp = tkp - 1
                aTA = aTA2[kp % 2]; aTB = aTB2[kp % 2]
                for half in range(2):
                    tk = 2 * kp + half
                    ctx.mm(lambda t, hp=hp, tk=tk, half=half, aTA=aTA: t.matmul(
                        pvA[0:65, :], vp[tk][:, 65 * (2 * hp):65 * (2 * hp) + 65],
                        aTA[:, 512 * half:512 * (half + 1)],
                        start=(tk == 0), stop=(tk == ntk - 1)),
                        [vp[tk].buf, aTA.buf], pvA.buf,
                        acc_first=(tk == 0), acc_last=(tk == ntk - 1), tag="avA")
                    ctx.mm(lambda t, hp=hp, tk=tk, half=half, aTB=aTB: t.matmul(
                        pvB[0:65, :], vp[tk][:, 65 * (2 * hp + 1):65 * (2 * hp + 1) + 65],
                        aTB[:, 512 * half:512 * (half + 1)],
                        start=(tk == 0), stop=(tk == ntk - 1)),
                        [vp[tk].buf, aTB.buf], pvB.buf,
                        acc_first=(tk == 0), acc_last=(tk == ntk - 1), tag="avB")
                ctx.mm(lambda t: t.matmul(po[:], kf[0][:, 0:128], wup[:], start=True, stop=True),
                       [kf[0].buf, wup.buf], po.buf, tag="warm2")
        pvsb = pvsb2[hp % 2]
        ctx.op("dve", lambda v, pvsb=pvsb: v.tensor_copy(pvsb[0][0:65, :], pvA[0:65, :]), reads=[pvA.buf], writes=[pvsb[0].buf])
        ctx.op("dve", lambda v, pvsb=pvsb: v.tensor_copy(pvsb[1][0:65, :], pvB[0:65, :]), reads=[pvB.buf], writes=[pvsb[1].buf])
        for i in range(2):
            dcl = dclq[hp % 2][i]; recb = recbq[hp % 2][i]
            ctx.op("dve", lambda v, i=i, dcl=dcl, pvsb=pvsb: v.tensor_scalar(out=dcl[:], in0=pvsb[i][64:65, :], scalar1=1e-35,
                                                    scalar2=None, op0=ALU.max),
                   reads=[pvsb[i].buf], writes=[dcl.buf])
            ctx.op("dve", lambda v, dcl=dcl: v.reciprocal(dcl[:], dcl[:]), reads=[dcl.buf], writes=[dcl.buf])
            ctx.op("dve", lambda v, dcl=dcl, recb=recb: v.tensor_copy(recb[:], dcl[:]), reads=[dcl.buf], writes=[recb.buf])

    def pair_tail(hp):
        pvsb = pvsb2[hp % 2]
        for i in range(2):
            hb = i * 64
            recb = recbq[hp % 2][i]; bcf = bcf2[i]
            ctx.mm(lambda t, recb=recb: t.matmul(pbc[:], onesrb[0:1, 0:64], recb[:], start=True, stop=True),
                   [onesrb.buf, recb.buf], pbc.buf, tag="bcast")
            ctx.op("dve", lambda v, bcf=bcf: v.tensor_copy(bcf[:], pbc[:]), reads=[pbc.buf], writes=[bcf.buf])
            ctx.op("dve", lambda v, hp=hp, hb=hb, i=i, bcf=bcf, pvsb=pvsb: v.tensor_tensor(
                atf[hp][hb:hb + 64, :], pvsb[i][0:64, :], bcf[:], op=ALU.mult),
                reads=[pvsb[i].buf, bcf.buf], writes=[atf[hp].buf], tag="norm")

    for iq in range(nq):
        ts0 = 512 * iq
        qf = qf2[iq % 2]
        ctx.dma("sp", lambda e, ts0=ts0, qf=qf: e.dma_start(out=qf[:], in_=fm_block_ap(g["q0h"], T, ts0, 512)),
                writes=[qf.buf], tag="ldq")
        pe_warmup(24)
        for hp in range(9):
            if hp < 8:
                pair_main(hp, qf)
            if hp >= 1:
                pair_tail(hp - 1)
        for dc in range(8):
            s0t = s0t2[dc % 2]
            for c in range(8):
                ctx.mm(lambda t, dc=dc, c=c: t.matmul(po[:], wo[c][:, 128 * dc:128 * (dc + 1)], atf[c][:],
                                                      start=(c == 0), stop=(c == 7)),
                       [wo[c].buf, atf[c].buf], po.buf, acc_first=(c == 0), acc_last=(c == 7), tag="out0")
            ctx.op("dve", lambda v, dc=dc, s0t=s0t: v.tensor_scalar(out=s0t[:], in0=po[:], scalar1=boutp[:, dc:dc + 1],
                                                           scalar2=None, op0=ALU.add),
                   reads=[po.buf, boutp.buf], writes=[s0t.buf])
            ctx.dma("act", lambda e, dc=dc, ts0=ts0, s0t=s0t: e.dma_start(
                out=g["s0"].ap[128 * dc:128 * (dc + 1), ts0:ts0 + 512], in_=s0t[:]),
                reads=[s0t.buf], tag="sts0")



def emit_ffn0(ctx, nc, g, T):
    nt2 = T // 256
    idb = sbuf(ctx, "p3_idb", [128, 128], BF16)
    ctx.dma("sp", lambda e: e.dma_start(out=idb[:], in_=g["c_identb"].ap), writes=[idb.buf])
    wgT = [sbuf(ctx, f"p3_wgT{c}", [128, 3072], BF16) for c in range(8)]
    wuT = [sbuf(ctx, f"p3_wuT{c}", [128, 3072], BF16) for c in range(8)]
    wdT = [sbuf(ctx, f"p3_wdT{h}", [128, 1024], BF16) for h in range(24)]
    wtmp = [sbuf(ctx, f"p3_wtmp{i}", [128, 3072], BF16) for i in range(2)]
    ptw2 = [psum(ctx, f"p3_ptw{i}", [128, 128], BF16) for i in range(2)]
    for (wname, dstT) in (("L0_ffn_w_gate", wgT), ("L0_ffn_w_up", wuT)):
        for hc in range(24):
            w = wtmp[hc % 2]
            ctx.dma("pool", lambda e, wname=wname, hc=hc, w=w: e.dma_start(
                out=w[:, 0:1024], in_=g[wname].ap[128 * hc:128 * (hc + 1), :]),
                writes=[w.buf], tag="ldwg")
            for c in range(8):
                ptw = ptw2[c % 2]
                ctx.op("pe", lambda t, w=w, c=c, ptw=ptw: t.transpose(ptw[:], w[:, 128 * c:128 * (c + 1)], idb[:]),
                       reads=[w.buf, idb.buf], writes=[ptw.buf])
                ctx.op("dve", lambda v, c=c, hc=hc, dstT=dstT, ptw=ptw: v.tensor_copy(
                    dstT[c][:, 128 * hc:128 * (hc + 1)], ptw[:]),
                    reads=[ptw.buf], writes=[dstT[c].buf])
    for dc in range(8):
        w = wtmp[dc % 2]
        ctx.dma("pool", lambda e, dc=dc, w=w: e.dma_start(
            out=w[:], in_=g["L0_ffn_w_down"].ap[128 * dc:128 * (dc + 1), :]),
            writes=[w.buf], tag="ldwd")
        for hc in range(24):
            ptw = ptw2[hc % 2]
            ctx.op("pe", lambda t, w=w, hc=hc, ptw=ptw: t.transpose(ptw[:], w[:, 128 * hc:128 * (hc + 1)], idb[:]),
                   reads=[w.buf, idb.buf], writes=[ptw.buf])
            ctx.op("dve", lambda v, dc=dc, hc=hc, ptw=ptw: v.tensor_copy(wdT[hc][:, 128 * dc:128 * (dc + 1)], ptw[:]),
                   reads=[ptw.buf], writes=[wdT[hc].buf])

    xf2 = [sbuf(ctx, f"p3_xf{i}", [128, 8 * 256], BF16) for i in range(2)]
    hsg2 = [sbuf(ctx, f"p3_hsg{i}", [128, 256], BF16) for i in range(2)]
    hup2 = [sbuf(ctx, f"p3_hup{i}", [128, 256], BF16) for i in range(2)]
    hhb = sbuf(ctx, "p3_hhb", [128, 24 * 256], BF16)
    g0b2 = [sbuf(ctx, f"p3_g0b{i}", [128, 256], BF16) for i in range(2)]
    s0t1 = sbuf(ctx, "p3_s0t", [128, 8 * 256], BF16)
    prodb2 = [sbuf(ctx, f"p3_prodb{i}", [128, 256], BF16) for i in range(2)]
    xrow = [sbuf(ctx, f"p3_xrow{s}", [128, 1024], BF16) for s in range(2)]
    xt1 = sbuf(ctx, "p3_xt", [128, 1024], F32)
    x1f1 = sbuf(ctx, "p3_x1f", [128, 1024], F32)
    psg2 = [psum(ctx, f"p3_psg{i}", [128, 256], F32) for i in range(2)]
    psu2 = [psum(ctx, f"p3_psu{i}", [128, 256], F32) for i in range(2)]
    pd2 = [psum(ctx, f"p3_pd{i}", [128, 256], F32) for i in range(2)]

    for it in range(nt2):
        ts0 = 256 * it
        xf = xf2[it % 2]
        s0t = s0t1
        ctx.dma("sp", lambda e, ts0=ts0, xf=xf: e.dma_start(out=xf[:], in_=fm_block_ap(g["xn0"], T, ts0, 256)),
                writes=[xf.buf], tag="ldxn")
        ctx.dma("act", lambda e, ts0=ts0, s0t=s0t: e.dma_start(out=s0t[:], in_=fm_block_ap(g["s0"], T, ts0, 256)),
                writes=[s0t.buf], tag="lds0")
        for hc in range(24):
            psg = psg2[hc % 2]; psu = psu2[hc % 2]; hsg = hsg2[hc % 2]; hup = hup2[hc % 2]
            for c in range(8):
                ctx.mm(lambda t, hc=hc, c=c, psg=psg, xf=xf: t.matmul(psg[:], wgT[c][:, 128 * hc:128 * (hc + 1)],
                                                      blk_slice(xf, c, 256), start=(c == 0), stop=(c == 7)),
                       [wgT[c].buf, xf.buf], psg.buf, acc_first=(c == 0), acc_last=(c == 7), tag="gate")
            for c in range(8):
                ctx.mm(lambda t, hc=hc, c=c, psu=psu, xf=xf: t.matmul(psu[:], wuT[c][:, 128 * hc:128 * (hc + 1)],
                                                      blk_slice(xf, c, 256), start=(c == 0), stop=(c == 7)),
                       [wuT[c].buf, xf.buf], psu.buf, acc_first=(c == 0), acc_last=(c == 7), tag="up")
            ctx.op("act", lambda s, psg=psg, hsg=hsg: s.activation(hsg[:], psg[:], AF.Silu, scale=1.0 / 127.0),
                   reads=[psg.buf], writes=[hsg.buf])
            ctx.op("dve", lambda v, psu=psu, hup=hup: v.tensor_scalar(out=hup[:], in0=psu[:], scalar1=1.0 / 127.0,
                                                    scalar2=None, op0=ALU.mult),
                   reads=[psu.buf], writes=[hup.buf])
            ctx.op("dve", lambda v, hc=hc, hsg=hsg, hup=hup: v.tensor_tensor(blk_slice(hhb, hc, 256), hsg[:], hup[:],
                                                           op=ALU.mult),
                   reads=[hsg.buf, hup.buf], writes=[hhb.buf])
        for dc in range(8):
            pd = pd2[dc % 2]; g0b = g0b2[dc % 2]; prodb = prodb2[dc % 2]
            for hc in range(24):
                ctx.mm(lambda t, hc=hc, dc=dc, pd=pd: t.matmul(pd[:], wdT[hc][:, 128 * dc:128 * (dc + 1)],
                                                        blk_slice(hhb, hc, 256),
                                                        start=(hc == 0), stop=(hc == 23)),
                       [wdT[hc].buf, hhb.buf], pd.buf, acc_first=(hc == 0),
                       acc_last=(hc == 23), tag="down")
            ctx.op("act", lambda s, dc=dc, pd=pd, g0b=g0b: s.activation(g0b[:], pd[:],
                                                        AF.Silu, scale=1.0 / 127.0),
                   reads=[pd.buf], writes=[g0b.buf], tag="g0")
            ctx.op("dve", lambda v, dc=dc, g0b=g0b, prodb=prodb, s0t=s0t: v.tensor_tensor(prodb[:], g0b[:], blk_slice(s0t, dc, 256),
                                                           op=ALU.mult),
                   reads=[g0b.buf, s0t.buf], writes=[prodb.buf])
            for sx in range(2):
                ptw = ptw2[sx % 2]
                ctx.op("pe", lambda t, sx=sx, prodb=prodb, ptw=ptw: t.transpose(ptw[:], prodb[:, 128 * sx:128 * (sx + 1)], idb[:]),
                       reads=[prodb.buf, idb.buf], writes=[ptw.buf])
                ctx.op("dve", lambda v, sx=sx, dc=dc, ptw=ptw: v.tensor_copy(xrow[sx][:, 128 * dc:128 * (dc + 1)], ptw[:]),
                       reads=[ptw.buf], writes=[xrow[sx].buf])
        for sx in range(2):
            t0 = ts0 + 128 * sx
            xt = xt1; x1f = x1f1
            ctx.dma("sp", lambda e, t0=t0, xt=xt: e.dma_start(out=xt[:], in_=g["x"].ap[t0:t0 + 128, :]),
                    writes=[xt.buf], tag="ldx")
            ctx.op("dve", lambda v, sx=sx, xt=xt, x1f=x1f: v.tensor_tensor(x1f[:], xt[:], xrow[sx][:], op=ALU.add),
                   reads=[xt.buf, xrow[sx].buf], writes=[x1f.buf])
            ctx.dma("sp", lambda e, t0=t0, x1f=x1f: e.dma_start(out=g["x1"].ap[t0:t0 + 128, :], in_=x1f[:]),
                    reads=[x1f.buf], tag="stx1")



def emit_ln1(ctx, nc, g, T):
    nt5 = T // 512
    idb = sbuf(ctx, "p4_idb", [128, 128], BF16)
    ctx.dma("sp", lambda e: e.dma_start(out=idb[:], in_=g["c_identb"].ap), writes=[idb.buf])
    xt2 = [sbuf(ctx, f"p4_xt{i}", [128, 1024], F32) for i in range(2)]
    xsq2 = [sbuf(ctx, f"p4_xsq{i}", [128, 1024], F32) for i in range(2)]
    mvec2 = [sbuf(ctx, f"p4_m{i}", [128, 1], F32) for i in range(2)]
    ssq2 = [sbuf(ctx, f"p4_ssq{i}", [128, 1], F32) for i in range(2)]
    var2 = [sbuf(ctx, f"p4_var{i}", [128, 1], F32) for i in range(2)]
    rstd2 = [sbuf(ctx, f"p4_rstd{i}", [128, 1], F32) for i in range(2)]
    xnb2 = [sbuf(ctx, f"p4_xnb{i}", [128, 1024], BF16) for i in range(2)]
    xnf2 = [[sbuf(ctx, f"p4_xnf{i}_{c}", [128, 512], BF16) for c in range(8)] for i in range(2)]
    ptr2 = [psum(ctx, f"p4_ptr{i}", [128, 128], BF16) for i in range(2)]
    for it in range(nt5):
        ts = slice(512 * it, 512 * (it + 1))
        xnf = xnf2[it % 2]
        for sub in range(4):
            p = sub % 2
            xt = xt2[p]; xsq = xsq2[p]; mvec = mvec2[p]; ssq = ssq2[p]
            var = var2[p]; rstd = rstd2[p]; xnb = xnb2[p]
            t0 = 512 * it + 128 * sub
            ctx.dma("sp", lambda e, t0=t0, xt=xt: e.dma_start(out=xt[:], in_=g["x1"].ap[t0:t0 + 128, :]),
                    writes=[xt.buf], tag="ldx1")
            ctx.op("dve", lambda v, xt=xt, mvec=mvec: v.reduce_sum(mvec[:], xt[:], axis=AX.X),
                   reads=[xt.buf], writes=[mvec.buf])
            ctx.op("dve", lambda v, mvec=mvec: v.tensor_scalar(out=mvec[:], in0=mvec[:], scalar1=1.0 / D,
                                                    scalar2=None, op0=ALU.mult),
                   reads=[mvec.buf], writes=[mvec.buf])
            ctx.op("dve", lambda v, xt=xt, xsq=xsq: v.tensor_tensor(xsq[:], xt[:], xt[:], op=ALU.mult),
                   reads=[xt.buf], writes=[xsq.buf])
            ctx.op("dve", lambda v, xsq=xsq, ssq=ssq: v.reduce_sum(ssq[:], xsq[:], axis=AX.X),
                   reads=[xsq.buf], writes=[ssq.buf])
            ctx.op("dve", lambda v, mvec=mvec, var=var: v.tensor_tensor(var[:], mvec[:], mvec[:], op=ALU.mult),
                   reads=[mvec.buf], writes=[var.buf])
            ctx.op("dve", lambda v, ssq=ssq, var=var: v.tensor_scalar(out=var[:], in0=ssq[:], scalar1=1.0 / D,
                                                    scalar2=var[:], op0=ALU.mult, op1=ALU.subtract),
                   reads=[ssq.buf, var.buf], writes=[var.buf])
            ctx.op("dve", lambda v, var=var: v.tensor_scalar(out=var[:], in0=var[:], scalar1=EPS,
                                                    scalar2=None, op0=ALU.add),
                   reads=[var.buf], writes=[var.buf])
            ctx.op("act", lambda s, var=var, rstd=rstd: s.activation(rstd[:], var[:], AF.Ln),
                   reads=[var.buf], writes=[rstd.buf])
            ctx.op("act", lambda s, rstd=rstd: s.activation(rstd[:], rstd[:], AF.Exp, scale=-0.5),
                   reads=[rstd.buf], writes=[rstd.buf])
            ctx.op("dve", lambda v, xt=xt, mvec=mvec, rstd=rstd, xnb=xnb: v.tensor_scalar(out=xnb[:], in0=xt[:], scalar1=mvec[:],
                                                    scalar2=rstd[:], op0=ALU.subtract, op1=ALU.mult),
                   reads=[xt.buf, mvec.buf, rstd.buf], writes=[xnb.buf])
            for c in range(8):
                ptr = ptr2[c % 2]
                ctx.op("pe", lambda t, c=c, xnb=xnb, ptr=ptr: t.transpose(ptr[:], xnb[:, 128 * c:128 * (c + 1)], idb[:]),
                       reads=[xnb.buf, idb.buf], writes=[ptr.buf])
                ctx.op("dve", lambda v, c=c, sub=sub, ptr=ptr, xnf=xnf: v.tensor_copy(xnf[c][:, 128 * sub:128 * (sub + 1)], ptr[:]),
                       reads=[ptr.buf], writes=[xnf[c].buf])
        for c in range(8):
            ctx.dma("act", lambda e, c=c, ts=ts, xnf=xnf: e.dma_start(out=g["xn1"].ap[128 * c:128 * (c + 1), ts],
                                                             in_=xnf[c][:]),
                    reads=[xnf[c].buf], tag="stxn1")



def qkv1_setup(ctx, nc, g, T):
    nt5 = T // 512
    wq1 = [sbuf(ctx, f"p5_wq{c}", [128, 4096], BF16) for c in range(8)]
    for c in range(8):
        ctx.dma("pool", lambda e, c=c: e.dma_start(out=wq1[c][:], in_=g["L1_attn_W_qkv"].ap[128 * c:128 * (c + 1), :]),
                writes=[wq1[c].buf], tag="ldw1")
    p1t = sbuf(ctx, "p5_p1t", [128, 128], BF16)
    ctx.dma("sp", lambda e: e.dma_start(out=p1t[:], in_=g["c_p1t"].ap), writes=[p1t.buf])
    dkbq = sbuf(ctx, "p5_dkbq", [128, 1024], BF16)
    dkbk = sbuf(ctx, "p5_dkbk", [128, 1024], BF16)
    ctx.dma("sp", lambda e: e.dma_start(out=dkbq[:], in_=g["dk_bq"].ap), writes=[dkbq.buf])
    ctx.dma("sp", lambda e: e.dma_start(out=dkbk[:], in_=g["dk_bk"].ap), writes=[dkbk.buf])
    tabs = {}
    for nm in ("x1cq", "x1sq", "x1ck", "x1sk"):
        tabs[nm] = sbuf(ctx, f"p5_{nm}", [128, 4096], BF16)

    xf2 = [sbuf(ctx, f"p5_xf{i}", [128, 8 * 512], BF16) for i in range(2)]
    qraw2 = [sbuf(ctx, f"p5_qraw{i}", [128, 512], BF16) for i in range(2)]
    tmp11 = sbuf(ctx, "p5_tmp1", [128, 512], F32)
    tmp12 = [tmp11, tmp11]
    sqv1 = sbuf(ctx, "p5_sqv", [128, 512], F32)
    sqv2 = [sqv1, sqv1]
    qrope1 = sbuf(ctx, "p5_qrope", [128, 512], F32)
    qrope2 = [qrope1, qrope1]
    qhat2 = [sbuf(ctx, f"p5_qhat{i}", [128, 512], BF16) for i in range(2)]
    vout1 = sbuf(ctx, "p5_vout", [128, 2048], BF16)
    vout2 = [vout1, vout1]
    pacc1 = psum(ctx, "p5_pacc", [128, 512], F32)
    pacc2 = [pacc1, pacc1]
    prot2 = [pacc1, pacc1]
    pv1 = psum(ctx, "p5_pv", [128, 512], F32)
    pv2 = [pv1, pv1]

    def qkv1_tile(it):
        ts0 = 512 * it
        xf = xf2[it % 2]
        ctx.dma("sp", lambda e, ts0=ts0, xf=xf: e.dma_start(out=xf[:], in_=fm_block_ap(g["xn1"], T, ts0, 512)),
                writes=[xf.buf], tag="ldxn1")
        for nm in ("x1cq", "x1sq", "x1ck", "x1sk"):
            ctx.dma("act", lambda e, nm=nm, ts0=ts0: e.dma_start(out=tabs[nm][:],
                                                                 in_=fm_block_ap(g["c_" + nm], T, ts0, 512)),
                    writes=[tabs[nm].buf], tag="ldtab")
        for f in range(16):
            is_q = f < 8
            h = f if is_q else f - 8
            pacc = pacc2[f % 2]; prot = prot2[f % 2]
            qraw = qraw2[f % 2]; tmp1 = tmp12[f % 2]; sqv = sqv2[f % 2]
            qrope = qrope2[f % 2]; qhat = qhat2[f % 2]
            for c in range(8):
                ctx.mm(lambda t, f=f, c=c, pacc=pacc, xf=xf: t.matmul(pacc[:], wq1[c][:, 128 * f:128 * (f + 1)],
                                                    blk_slice(xf, c, 512), start=(c == 0), stop=(c == 7)),
                       [wq1[c].buf, xf.buf], pacc.buf, acc_first=(c == 0), acc_last=(c == 7), tag="qk1")
            ctx.op("dve", lambda v, pacc=pacc, qraw=qraw: v.tensor_copy(qraw[:], pacc[:]), reads=[pacc.buf], writes=[qraw.buf])
            ctx.mm(lambda t, prot=prot, qraw=qraw: t.matmul(prot[:], p1t[:], qraw[:], start=True, stop=True),
                   [p1t.buf, qraw.buf], prot.buf, tag="rot1")
            ct = tabs["x1cq" if is_q else "x1ck"]
            stab = tabs["x1sq" if is_q else "x1sk"]
            ctx.op("dve", lambda v, ct=ct, h=h, qraw=qraw, tmp1=tmp1: v.tensor_tensor(tmp1[:], qraw[:], blk_slice(ct, h, 512),
                                                                op=ALU.mult),
                   reads=[qraw.buf, ct.buf], writes=[tmp1.buf])
            ctx.op("dve", lambda v, stab=stab, h=h, prot=prot, sqv=sqv: v.tensor_tensor(sqv[:], prot[:], blk_slice(stab, h, 512),
                                                                    op=ALU.mult),
                   reads=[prot.buf, stab.buf], writes=[sqv.buf])
            ctx.op("dve", lambda v, tmp1=tmp1, sqv=sqv, qrope=qrope: v.tensor_tensor(qrope[:], tmp1[:], sqv[:], op=ALU.add),
                   reads=[tmp1.buf, sqv.buf], writes=[qrope.buf])
            gam = dkbq if is_q else dkbk
            def gmul(v, gam=gam, h=h, qrope=qrope, qhat=qhat):
                src = qrope[:]
                s3 = bass.AP(src.tensor, src.offset, [list(src.ap[0]), [128, 4], [1, 128]])
                d = qhat[:]
                d3 = bass.AP(d.tensor, d.offset, [list(d.ap[0]), [128, 4], [1, 128]])
                ga = gam[:, 128 * h:128 * (h + 1)]
                g3 = bass.AP(ga.tensor, ga.offset, [list(ga.ap[0]), [0, 4], [1, 128]])
                return v.tensor_tensor(d3, s3, g3, op=ALU.mult)
            ctx.op("dve", gmul, reads=[qrope.buf, dkbq.buf if is_q else dkbk.buf], writes=[qhat.buf])
            dst = "q1t" if is_q else "k1t"
            ctx.dma("act", lambda e, dst=dst, h=h, ts0=ts0, qhat=qhat: e.dma_start(
                out=g[dst].ap[128 * h:128 * (h + 1), ts0:ts0 + 512], in_=qhat[:]),
                reads=[qhat.buf], writes=[g[dst].buf], tag="stqk1")
        for sub in range(4):
            vout = vout2[sub % 2]
            for vh in range(4):
                pv = pv2[vh % 2]
                for c in range(8):
                    ctx.mm(lambda t, sub=sub, vh=vh, c=c, pv=pv, xf=xf: t.matmul(
                        pv[:], xf[:, 512 * c + 128 * sub:512 * c + 128 * (sub + 1)],
                        wq1[c][:, 2048 + 512 * vh:2048 + 512 * (vh + 1)],
                        start=(c == 0), stop=(c == 7)),
                        [xf.buf, wq1[c].buf], pv.buf, acc_first=(c == 0), acc_last=(c == 7), tag="v1mm")
                ctx.op("dve", lambda v, vh=vh, pv=pv, vout=vout: v.tensor_copy(vout[:, 512 * vh:512 * (vh + 1)], pv[:]),
                       reads=[pv.buf], writes=[vout.buf])
            t0 = ts0 + 128 * sub
            ctx.dma("sp", lambda e, t0=t0, vout=vout: e.dma_start(out=g["v1"].ap[t0:t0 + 128, :], in_=vout[:]),
                    reads=[vout.buf], writes=[g["v1"].buf], tag="stv1")
    return qkv1_tile


def emit_qkv1(ctx, nc, g, T):
    tile = qkv1_setup(ctx, nc, g, T)
    for it in range(T // 512):
        tile(it)



def retention_setup(ctx, nc, g, T):
    ncnk = T // 128
    idb = sbuf(ctx, "p6_idb", [128, 128], BF16)
    csl = sbuf(ctx, "p6_csl", [128, 128], BF16)
    onc = sbuf(ctx, "p6_onc", [128, 1], F32)
    onrb = sbuf(ctx, "p6_onrb", [1, 128], BF16)
    g128 = sbuf(ctx, "p6_g128", [128, 8], F32)
    ctx.dma("sp", lambda e: e.dma_start(out=idb[:], in_=g["c_identb"].ap), writes=[idb.buf])
    ctx.dma("sp", lambda e: e.dma_start(out=csl[:], in_=g["c_causal"].ap), writes=[csl.buf])
    ctx.dma("sp", lambda e: e.dma_start(out=onc[:], in_=g["c_onescol"].ap), writes=[onc.buf])
    ctx.dma("sp", lambda e: e.dma_start(out=onrb[:], in_=g["c_onesrow_b"].ap), writes=[onrb.buf])
    ctx.dma("sp", lambda e: e.dma_start(out=g128[:], in_=g["dk_g128"].ap), writes=[g128.buf])

    oncb = sbuf(ctx, "p6_oncb", [128, 1], BF16)
    ctx.op("dve", lambda v: v.tensor_copy(oncb[:], onc[:]), reads=[onc.buf], writes=[oncb.buf])
    R = [sbuf(ctx, f"p6_R{h}", [128, 256], F32) for h in range(NH1)]
    Rb2 = [sbuf(ctx, f"p6_Rb{i}", [128, 256], BF16) for i in range(2)]
    for h in range(NH1):
        ctx.op("dve", lambda v, h=h: v.memset(R[h][:], 0.0), writes=[R[h].buf])

    qb2 = [sbuf(ctx, f"p6_qb{i}", [128, 1024], BF16) for i in range(2)]
    kb2 = [sbuf(ctx, f"p6_kb{i}", [128, 1024], BF16) for i in range(2)]
    vb2 = [sbuf(ctx, f"p6_vb{i}", [128, 2048], BF16) for i in range(2)]
    ktTs2 = [sbuf(ctx, f"p6_ktTs{i}", [128, 1024], BF16) for i in range(2)]
    aSs2 = [sbuf(ctx, f"p6_aSs{i}", [128, 1024], BF16) for i in range(2)]
    outb2 = [sbuf(ctx, f"p6_outb{i}", [128, 256], BF16) for i in range(2)]
    sqb2 = [sbuf(ctx, f"p6_sqb{i}", [128, 256], BF16) for i in range(2)]
    srows1 = sbuf(ctx, "p6_srows", [1, 2048], F32)
    srows2 = [srows1, srows1]
    stats1 = sbuf(ctx, "p6_stats", [1, 1024], F32)
    stats2 = [stats1, stats1]
    mu1 = sbuf(ctx, "p6_mu", [1, 1024], F32)
    mu2 = [mu1, mu1]
    rsd1 = sbuf(ctx, "p6_rsd", [1, 1024], F32)
    rsd2 = [rsd1, rsd1]
    arow1 = sbuf(ctx, "p6_arow", [1, 1024], BF16)
    arow2 = [arow1, arow1]
    brow1 = sbuf(ctx, "p6_brow", [1, 1024], BF16)
    brow2 = [brow1, brow1]
    brf1 = sbuf(ctx, "p6_brf", [1, 1024], F32)
    brf2 = [brf1, brf1]
    abf1 = sbuf(ctx, "p6_abf", [128, 1024], BF16)
    abf2 = [abf1, abf1]
    nmd2 = [sbuf(ctx, f"p6_nmd{i}", [128, 2048], BF16) for i in range(2)]
    tmpn1 = sbuf(ctx, "p6_tmpn", [128, 1024], BF16)
    tmpn2 = [tmpn1, tmpn1]
    psS2 = [psum(ctx, f"p6_psS{i}", [128, 128], F32) for i in range(2)]
    ptb = psum(ctx, "p6_ptb", [128, 128], BF16)
    pav1 = psum(ctx, "p6_pav", [128, 256], F32)
    pav2 = [pav1, pav1]
    pkv = psum(ctx, "p6_pkv", [128, 256], F32)
    pmix = psum(ctx, "p6_pmix", [128, 512], F32)
    pstat = Tile(pmix.t, "p6_pmix")
    pstat.buf = pmix.buf
    pbc = pmix

    def stageA(ck):
        p = ck % 2
        t0 = 128 * ck
        qb, kb, vb = qb2[p], kb2[p], vb2[p]
        ktTs, aSs = ktTs2[p], aSs2[p]
        ctx.dma("sp", lambda e, t0=t0, qb=qb: e.dma_start(out=qb[:], in_=fm_block_ap(g["q1t"], T, t0, 128)),
                reads=[g["q1t"].buf], writes=[qb.buf], tag="ldq1")
        ctx.dma("sp", lambda e, t0=t0, kb=kb: e.dma_start(out=kb[:], in_=fm_block_ap(g["k1t"], T, t0, 128)),
                reads=[g["k1t"].buf], writes=[kb.buf], tag="ldk1")
        ctx.dma("act", lambda e, t0=t0, vb=vb: e.dma_start(out=vb[:], in_=g["v1"].ap[t0:t0 + 128, :]),
                reads=[g["v1"].buf], writes=[vb.buf], tag="ldv1")
        for h in range(NH1):
            psS = psS2[h % 2]
            ctx.op("pe", lambda t, h=h, kb=kb: t.transpose(ptb[:], blk_slice(kb, h, 128), idb[:]),
                   reads=[kb.buf, idb.buf], writes=[ptb.buf])
            ctx.op("dve", lambda v, h=h, ktTs=ktTs: v.tensor_copy(blk_slice(ktTs, h, 128), ptb[:]),
                   reads=[ptb.buf], writes=[ktTs.buf])
            ctx.mm(lambda t, h=h, kb=kb, qb=qb, psS=psS: t.matmul(psS[:], blk_slice(kb, h, 128),
                                                                  blk_slice(qb, h, 128),
                                                                  start=True, stop=True),
                   [kb.buf, qb.buf], psS.buf, tag="ret_s")
            ctx.op("dve", lambda v, h=h, psS=psS, aSs=aSs: v.tensor_tensor(blk_slice(aSs, h, 128),
                                                                           psS[:], csl[:], op=ALU.mult),
                   reads=[psS.buf, csl.buf], writes=[aSs.buf])

    def stageB(ck):
        p = ck % 2
        t0 = 128 * ck
        qb, vb = qb2[p], vb2[p]
        ktTs, aSs = ktTs2[p], aSs2[p]
        srows = srows2[p]; nmd = nmd2[p]
        for h in range(NH1):
            Rb = Rb2[h % 2]; pav = pav2[h % 2]
            outb = outb2[h % 2]; sqb = sqb2[h % 2]
            ctx.op("dve", lambda v, h=h, Rb=Rb: v.tensor_copy(Rb[:], R[h][:]), reads=[R[h].buf], writes=[Rb.buf])
            for vc in range(2):
                ctx.mm(lambda t, h=h, vc=vc, vb=vb, aSs=aSs, pav=pav: t.matmul(
                    pav[:, 128 * vc:128 * (vc + 1)],
                    vb[:, 256 * h + 128 * vc:256 * h + 128 * (vc + 1)],
                    blk_slice(aSs, h, 128), start=True, stop=False),
                    [vb.buf, aSs.buf], pav.buf, acc_first=(vc == 0), acc_last=False, tag="ret_av")
                ctx.mm(lambda t, h=h, vc=vc, Rb=Rb, qb=qb, pav=pav: t.matmul(
                    pav[:, 128 * vc:128 * (vc + 1)], Rb[:, 128 * vc:128 * (vc + 1)],
                    blk_slice(qb, h, 128), start=False, stop=True),
                    [Rb.buf, qb.buf], pav.buf, acc_first=False, acc_last=(vc == 1), tag="ret_in")
            ctx.mm(lambda t, h=h, ktTs=ktTs, vb=vb: t.matmul(pkv[:], blk_slice(ktTs, h, 128),
                                                             vb[:, 256 * h:256 * (h + 1)],
                                                             start=True, stop=True),
                   [ktTs.buf, vb.buf], pkv.buf, tag="ret_kv")
            ctx.op("dve", lambda v, h=h: v.tensor_tensor(R[h][:], R[h][:], pkv[:], op=ALU.add),
                   reads=[R[h].buf, pkv.buf], writes=[R[h].buf])
            ctx.op("dve", lambda v, h=h: v.tensor_scalar(out=R[h][:], in0=R[h][:], scalar1=g128[:, h:h + 1],
                                                         scalar2=None, op0=ALU.mult),
                   reads=[R[h].buf, g128.buf], writes=[R[h].buf])
            ctx.op("act", lambda sc, pav=pav, outb=outb: sc.copy(outb[:], pav[:]),
                   reads=[pav.buf], writes=[outb.buf])
            ctx.op("dve", lambda v, outb=outb, sqb=sqb: v.tensor_tensor(sqb[:], outb[:], outb[:], op=ALU.mult),
                   reads=[outb.buf], writes=[sqb.buf])
            for vc in range(2):
                ctx.mm(lambda t, vc=vc, outb=outb: t.matmul(pstat[0:1, 0:128], oncb[:],
                                                            outb[:, 128 * vc:128 * (vc + 1)],
                                                            start=(vc == 0), stop=(vc == 1)),
                       [oncb.buf, outb.buf], pstat.buf, acc_first=(vc == 0),
                       acc_last=(vc == 1), tag="gn_s")
            for vc in range(2):
                ctx.mm(lambda t, vc=vc, sqb=sqb: t.matmul(pstat[0:1, 128:256], oncb[:],
                                                          sqb[:, 128 * vc:128 * (vc + 1)],
                                                          start=(vc == 0), stop=(vc == 1)),
                       [oncb.buf, sqb.buf], pstat.buf, acc_first=(vc == 0),
                       acc_last=(vc == 1), tag="gn_s2")
            ctx.op("dve", lambda v, h=h, srows=srows: v.tensor_copy(srows[0:1, 256 * h:256 * (h + 1)], pstat[0:1, 0:256]),
                   reads=[pstat.buf], writes=[srows.buf])
            ctx.op("dve", lambda v, h=h, outb=outb, nmd=nmd: v.tensor_copy(nmd[:, 256 * h:256 * (h + 1)], outb[:]),
                   reads=[outb.buf], writes=[nmd.buf])
        stats = stats2[p]; mu = mu2[p]; rsd = rsd2[p]
        arow = arow2[p]; brow = brow2[p]; brf = brf2[p]
        def sview(off, srows=srows):
            a = srows[:]
            return bass.AP(a.tensor, a.offset + off, [list(a.ap[0]), [256, 8], [1, 128]])
        def v1024(tile):
            a = tile[:]
            return bass.AP(a.tensor, a.offset, [list(a.ap[0]), [128, 8], [1, 128]])
        ctx.op("dve", lambda v, mu=mu, srows=srows: v.tensor_scalar(out=v1024(mu), in0=sview(0), scalar1=1.0 / 256.0,
                                                scalar2=None, op0=ALU.mult),
               reads=[srows.buf], writes=[mu.buf])
        ctx.op("dve", lambda v, rsd=rsd, srows=srows: v.tensor_scalar(out=v1024(rsd), in0=sview(128), scalar1=1.0 / 256.0,
                                                scalar2=None, op0=ALU.mult),
               reads=[srows.buf], writes=[rsd.buf])
        ctx.op("dve", lambda v, stats=stats, mu=mu: v.tensor_tensor(stats[:, 0:1024], mu[:], mu[:], op=ALU.mult),
               reads=[mu.buf], writes=[stats.buf])
        ctx.op("dve", lambda v, rsd=rsd, stats=stats: v.tensor_tensor(rsd[:], rsd[:], stats[:, 0:1024], op=ALU.subtract),
               reads=[rsd.buf, stats.buf], writes=[rsd.buf])
        ctx.op("dve", lambda v, rsd=rsd: v.tensor_scalar(out=rsd[:], in0=rsd[:], scalar1=EPS,
                                                scalar2=None, op0=ALU.add),
               reads=[rsd.buf], writes=[rsd.buf])
        ctx.op("act", lambda s, rsd=rsd: s.activation(rsd[:], rsd[:], AF.Ln),
               reads=[rsd.buf], writes=[rsd.buf])
        ctx.op("act", lambda s, rsd=rsd: s.activation(rsd[:], rsd[:], AF.Exp, scale=-0.5),
               reads=[rsd.buf], writes=[rsd.buf])
        ctx.op("dve", lambda v, arow=arow, rsd=rsd: v.tensor_copy(arow[:], rsd[:]), reads=[rsd.buf], writes=[arow.buf])
        ctx.op("dve", lambda v, brf=brf, mu=mu, rsd=rsd: v.tensor_tensor(brf[:], mu[:], rsd[:], op=ALU.mult),
               reads=[mu.buf, rsd.buf], writes=[brf.buf])
        ctx.op("dve", lambda v, brow=brow, brf=brf: v.tensor_scalar(out=brow[:], in0=brf[:], scalar1=-1.0, scalar2=None,
                                                op0=ALU.mult),
               reads=[brf.buf], writes=[brow.buf])
        Ab = abf2[p]; Bb = tmpn2[p]
        for j in range(2):
            ctx.mm(lambda t, j=j, arow=arow: t.matmul(pbc[:], onrb[:], arow[0:1, 512 * j:512 * (j + 1)],
                                           start=True, stop=True),
                   [onrb.buf, arow.buf], pbc.buf, tag="gn_bc")
            ctx.op("act", lambda sc, j=j, Ab=Ab: sc.copy(Ab[:, 512 * j:512 * (j + 1)], pbc[:]),
                   reads=[pbc.buf], writes=[Ab.buf])
            ctx.mm(lambda t, j=j, brow=brow: t.matmul(pbc[:], onrb[:], brow[0:1, 512 * j:512 * (j + 1)],
                                           start=True, stop=True),
                   [onrb.buf, brow.buf], pbc.buf, tag="gn_bc2")
            ctx.op("act", lambda sc, j=j, Bb=Bb: sc.copy(Bb[:, 512 * j:512 * (j + 1)], pbc[:]),
                   reads=[pbc.buf], writes=[Bb.buf])
        def vcview(tile, vc):
            a = tile[:]
            return bass.AP(a.tensor, a.offset + 128 * vc, [list(a.ap[0]), [256, 8], [1, 128]])
        def bview(tile):
            a = tile[:]
            return bass.AP(a.tensor, a.offset, [list(a.ap[0]), [128, 8], [1, 128]])
        for vc in range(2):
            ctx.op("dve", lambda v, vc=vc, nmd=nmd, Ab=Ab: v.tensor_tensor(
                vcview(nmd, vc), vcview(nmd, vc), bview(Ab), op=ALU.mult),
                reads=[nmd.buf, Ab.buf], writes=[nmd.buf])
            ctx.op("dve", lambda v, vc=vc, nmd=nmd, Bb=Bb: v.tensor_tensor(
                vcview(nmd, vc), vcview(nmd, vc), bview(Bb), op=ALU.add),
                reads=[nmd.buf, Bb.buf], writes=[nmd.buf])
        ctx.dma("sp", lambda e, t0=t0, nmd=nmd: e.dma_start(
            out=bass.AP(g["normed1"].ap.tensor, g["normed1"].ap.offset + t0,
                        [[T, 128], [128 * T, 16], [1, 128]]),
            in_=bass.AP(nmd[:].tensor, nmd[:].offset, [list(nmd[:].ap[0]), [128, 16], [1, 128]])),
            reads=[nmd.buf], tag="stnmd")

    return stageA, stageB


def emit_retention(ctx, nc, g, T):
    ncnk = T // 128
    stageA, stageB = retention_setup(ctx, nc, g, T)
    for ck in range(ncnk + 1):
        if ck < ncnk:
            stageA(ck)
        if ck >= 1:
            stageB(ck - 1)


def emit_qkv1ret(ctx, nc, g, T):
    qtile = qkv1_setup(ctx, nc, g, T)
    stageA, stageB = retention_setup(ctx, nc, g, T)
    ncnk = T // 128
    for it in range(T // 512):
        qtile(it)
        for ck in range(4 * it, 4 * it + 4):
            stageA(ck)
            if ck >= 1:
                stageB(ck - 1)
    stageB(ncnk - 1)



def emit_gateproj(ctx, nc, g, T):
    nt2 = T // 256
    wg1 = [sbuf(ctx, f"p7_wg{c}", [128, 2048], BF16) for c in range(8)]
    for c in range(8):
        ctx.dma("pool", lambda e, c=c: e.dma_start(out=wg1[c][:], in_=g["L1_attn_W_g"].ap[128 * c:128 * (c + 1), :]),
                writes=[wg1[c].buf], tag="ldwg1")
    wp = [sbuf(ctx, f"p7_wp{c}", [128, 1024], BF16) for c in range(16)]
    for c in range(16):
        ctx.dma("pool", lambda e, c=c: e.dma_start(out=wp[c][:], in_=g["L1_attn_W_proj"].ap[128 * c:128 * (c + 1), :]),
                writes=[wp[c].buf], tag="ldwp")
    xf2 = [sbuf(ctx, f"p7_xf{i}", [128, 8 * 256], BF16) for i in range(2)]
    nt2b = [sbuf(ctx, f"p7_nt{i}", [128, 16 * 256], BF16) for i in range(2)]
    gateb2 = [sbuf(ctx, f"p7_gateb{i}", [128, 256], BF16) for i in range(2)]
    gtb = sbuf(ctx, "p7_gtb", [128, 16 * 256], BF16)
    s1t2 = [sbuf(ctx, f"p7_s1t{i}", [128, 256], BF16) for i in range(2)]
    psg2 = [psum(ctx, f"p7_psg{i}", [128, 256], F32) for i in range(2)]
    pd2 = [psum(ctx, f"p7_pd{i}", [128, 256], F32) for i in range(2)]
    for it in range(nt2):
        ts0 = 256 * it
        xf = xf2[it % 2]; nt = nt2b[it % 2]
        ctx.dma("sp", lambda e, ts0=ts0, xf=xf: e.dma_start(out=xf[:], in_=fm_block_ap(g["xn1"], T, ts0, 256)),
                writes=[xf.buf], tag="ldxn1b")
        ctx.dma("pool", lambda e, ts0=ts0, nt=nt: e.dma_start(out=nt[:], in_=fm_block_ap(g["normed1"], T, ts0, 256, nchunk=16)),
                writes=[nt.buf], tag="ldnmd")
        for gc in range(16):
            psg = psg2[gc % 2]; gateb = gateb2[gc % 2]
            for c in range(8):
                ctx.mm(lambda t, gc=gc, c=c, psg=psg, xf=xf: t.matmul(psg[:], wg1[c][:, 128 * gc:128 * (gc + 1)],
                                                      blk_slice(xf, c, 256), start=(c == 0), stop=(c == 7)),
                       [wg1[c].buf, xf.buf], psg.buf, acc_first=(c == 0), acc_last=(c == 7), tag="g1mm")
            ctx.op("act", lambda s, psg=psg, gateb=gateb: s.activation(gateb[:], psg[:], AF.Silu),
                   reads=[psg.buf], writes=[gateb.buf])
            ctx.op("dve", lambda v, gc=gc, gateb=gateb, nt=nt: v.tensor_tensor(blk_slice(gtb, gc, 256), gateb[:],
                                                           blk_slice(nt, gc, 256), op=ALU.mult),
                   reads=[gateb.buf, nt.buf], writes=[gtb.buf])
        for dc in range(8):
            pd = pd2[dc % 2]; s1t = s1t2[dc % 2]
            for c in range(16):
                ctx.mm(lambda t, dc=dc, c=c, pd=pd: t.matmul(pd[:], wp[c][:, 128 * dc:128 * (dc + 1)],
                                                      blk_slice(gtb, c, 256), start=(c == 0), stop=(c == 15)),
                       [wp[c].buf, gtb.buf], pd.buf, acc_first=(c == 0), acc_last=(c == 15), tag="pjmm")
            ctx.op("dve", lambda v, pd=pd, s1t=s1t: v.tensor_copy(s1t[:], pd[:]), reads=[pd.buf], writes=[s1t.buf])
            ctx.dma("act", lambda e, dc=dc, ts0=ts0, s1t=s1t: e.dma_start(
                out=g["s1"].ap[128 * dc:128 * (dc + 1), ts0:ts0 + 256], in_=s1t[:]),
                reads=[s1t.buf], tag="sts1")



def emit_ffn1(ctx, nc, g, T):
    nt2 = T // 256
    idb = sbuf(ctx, "p8_idb", [128, 128], BF16)
    ctx.dma("sp", lambda e: e.dma_start(out=idb[:], in_=g["c_identb"].ap), writes=[idb.buf])
    w1t = [sbuf(ctx, f"p8_w1t{c}", [128, 4096], BF16) for c in range(8)]
    w2t = [sbuf(ctx, f"p8_w2t{c}", [128, 1024], BF16) for c in range(32)]
    wtmp = [sbuf(ctx, f"p8_wtmp{i}", [128, 4096], BF16) for i in range(2)]
    ptw2 = [psum(ctx, f"p8_ptw{i}", [128, 128], BF16) for i in range(2)]
    for fc in range(32):
        w = wtmp[fc % 2]
        ctx.dma("pool", lambda e, fc=fc, w=w: e.dma_start(out=w[:, 0:1024],
                                                          in_=g["w1n"].ap[128 * fc:128 * (fc + 1), :]),
                writes=[w.buf], tag="ldw1t")
        for c in range(8):
            ptw = ptw2[c % 2]
            ctx.op("pe", lambda t, w=w, c=c, ptw=ptw: t.transpose(ptw[:], w[:, 128 * c:128 * (c + 1)], idb[:]),
                   reads=[w.buf, idb.buf], writes=[ptw.buf])
            ctx.op("dve", lambda v, c=c, fc=fc, ptw=ptw: v.tensor_copy(w1t[c][:, 128 * fc:128 * (fc + 1)], ptw[:]),
                   reads=[ptw.buf], writes=[w1t[c].buf])
    for dc in range(8):
        w = wtmp[dc % 2]
        ctx.dma("pool", lambda e, dc=dc, w=w: e.dma_start(out=w[:], in_=g["w2n"].ap[128 * dc:128 * (dc + 1), :]),
                writes=[w.buf], tag="ldw2t")
        for fc in range(32):
            ptw = ptw2[fc % 2]
            ctx.op("pe", lambda t, w=w, fc=fc, ptw=ptw: t.transpose(ptw[:], w[:, 128 * fc:128 * (fc + 1)], idb[:]),
                   reads=[w.buf, idb.buf], writes=[ptw.buf])
            ctx.op("dve", lambda v, dc=dc, fc=fc, ptw=ptw: v.tensor_copy(w2t[fc][:, 128 * dc:128 * (dc + 1)], ptw[:]),
                   reads=[ptw.buf], writes=[w2t[fc].buf])
    b1 = sbuf(ctx, "p8_b1", [128, 32], F32)
    b2 = sbuf(ctx, "p8_b2", [128, 8], F32)
    ctx.dma("sp", lambda e: e.dma_start(out=b1[:], in_=g["L1_ffn_b1"].ap),
            writes=[b1.buf])
    ctx.dma("sp", lambda e: e.dma_start(out=b2[:], in_=g["L1_ffn_b2"].ap),
            writes=[b2.buf])
    xf2 = [sbuf(ctx, f"p8_xf{i}", [128, 8 * 512], BF16) for i in range(2)]
    hb = sbuf(ctx, "p8_hb", [128, 32 * 512], BF16)
    f1b2 = [sbuf(ctx, f"p8_f1b{i}", [128, 512], BF16) for i in range(2)]
    psh2 = [psum(ctx, f"p8_psh{i}", [128, 512], F32) for i in range(2)]
    pdn2 = [psum(ctx, f"p8_pdn{i}", [128, 512], F32) for i in range(2)]
    for it in range(T // 512):
        ts0 = 512 * it
        xf = xf2[it % 2]
        ctx.dma("sp", lambda e, ts0=ts0, xf=xf: e.dma_start(out=xf[:], in_=fm_block_ap(g["xn1"], T, ts0, 512)),
                writes=[xf.buf], tag="ldxn1c")
        for fc in range(32):
            psh = psh2[fc % 2]
            for c in range(8):
                ctx.mm(lambda t, fc=fc, c=c, psh=psh, xf=xf: t.matmul(psh[:], w1t[c][:, 128 * fc:128 * (fc + 1)],
                                                      blk_slice(xf, c, 512), start=(c == 0), stop=(c == 7)),
                       [w1t[c].buf, xf.buf], psh.buf, acc_first=(c == 0), acc_last=(c == 7), tag="h1mm")
            ctx.op("act", lambda s, fc=fc, psh=psh: s.activation(blk_slice(hb, fc, 512), psh[:], AF.Gelu,
                                                        bias=b1[:, fc:fc + 1]),
                   reads=[psh.buf, b1.buf], writes=[hb.buf], tag="gelu")
        for dc in range(8):
            pdn = pdn2[dc % 2]; f1b = f1b2[dc % 2]
            for fc in range(32):
                ctx.mm(lambda t, dc=dc, fc=fc, pdn=pdn: t.matmul(pdn[:], w2t[fc][:, 128 * dc:128 * (dc + 1)],
                                                        blk_slice(hb, fc, 512), start=(fc == 0), stop=(fc == 31)),
                       [w2t[fc].buf, hb.buf], pdn.buf, acc_first=(fc == 0), acc_last=(fc == 31), tag="f1mm")
            ctx.op("dve", lambda v, dc=dc, pdn=pdn, f1b=f1b: v.tensor_scalar(out=f1b[:], in0=pdn[:], scalar1=b2[:, dc:dc + 1],
                                                           scalar2=None, op0=ALU.add),
                   reads=[pdn.buf, b2.buf], writes=[f1b.buf])
            ctx.dma("act", lambda e, dc=dc, ts0=ts0, f1b=f1b: e.dma_start(
                out=g["f1out"].ap[128 * dc:128 * (dc + 1), ts0:ts0 + 512], in_=f1b[:]),
                reads=[f1b.buf], tag="stf1")



def emit_ternary(ctx, nc, g, T):
    onc = sbuf(ctx, "pt_onc", [128, 1], F32)
    onr = sbuf(ctx, "pt_onr", [1, 128], F32)
    ctx.dma("sp", lambda e: e.dma_start(out=onc[:], in_=g["c_onescol"].ap), writes=[onc.buf])
    ctx.dma("sp", lambda e: e.dma_start(out=onr[:], in_=g["c_onesrow"].ap), writes=[onr.buf])
    for (wname, oname, rows, cols) in (("L1_ffn_W1", "w1n", 4096, 1024),
                                       ("L1_ffn_W2", "w2n", 1024, 4096)):
        nrc = rows // 128
        wt2 = [sbuf(ctx, f"pt_wt_{oname}{i}", [128, cols], F32) for i in range(3)]
        rowsum2 = [sbuf(ctx, f"pt_rs_{oname}{i}", [128, 1], F32) for i in range(2)]
        racc = sbuf(ctx, f"pt_ra_{oname}", [128, 1], F32)
        thr = sbuf(ctx, f"pt_thr_{oname}", [1, 1], F32)
        thv = sbuf(ctx, f"pt_thv_{oname}", [128, 1], F32)
        sgn2 = [sbuf(ctx, f"pt_sg_{oname}{i}", [128, cols], F32) for i in range(2)]
        mag2 = [sbuf(ctx, f"pt_mg_{oname}{i}", [128, cols], F32) for i in range(2)]
        outw2 = [sbuf(ctx, f"pt_ow_{oname}{i}", [128, cols], BF16) for i in range(2)]
        pt1 = psum(ctx, f"pt_p1_{oname}", [1, 1], F32)
        pt2 = psum(ctx, f"pt_p2_{oname}", [128, 1], F32)
        ctx.op("dve", lambda v, racc=racc: v.memset(racc[:], 0.0), writes=[racc.buf])
        for rc in range(nrc):
            wt = wt2[rc % 3]
            rowsum = rowsum2[rc % 2]
            ctx.dma(("sp", "act", "pool")[rc % 3], lambda e, wname=wname, rc=rc, wt=wt: e.dma_start(
                out=wt[:], in_=g[wname].ap[128 * rc:128 * (rc + 1), :]),
                writes=[wt.buf], tag="tern_ld")
            ctx.op("dve", lambda v, wt=wt, rowsum=rowsum: v.reduce_sum(
                rowsum[:], wt[:], axis=AX.X, apply_absolute_value=True),
                reads=[wt.buf], writes=[rowsum.buf])
            ctx.op("dve", lambda v, racc=racc, rowsum=rowsum: v.tensor_tensor(
                racc[:], racc[:], rowsum[:], op=ALU.add),
                reads=[racc.buf, rowsum.buf], writes=[racc.buf])
        ctx.mm(lambda t, racc=racc, pt1=pt1: t.matmul(pt1[:], racc[:], onc[:], start=True, stop=True),
               [racc.buf, onc.buf], pt1.buf, tag="tern_tot")
        ctx.op("dve", lambda v, thr=thr, pt1=pt1: v.tensor_scalar(
            out=thr[:], in0=pt1[:], scalar1=1.0 / (rows * cols), scalar2=1e-8,
            op0=ALU.mult, op1=ALU.max),
            reads=[pt1.buf], writes=[thr.buf])
        ctx.op("dve", lambda v, thr=thr: v.tensor_scalar(
            out=thr[:], in0=thr[:], scalar1=0.5, scalar2=None, op0=ALU.mult),
            reads=[thr.buf], writes=[thr.buf])
        ctx.mm(lambda t, thr=thr, pt2=pt2: t.matmul(pt2[:], onr[:], thr[:], start=True, stop=True),
               [onr.buf, thr.buf], pt2.buf, tag="tern_bc")
        ctx.op("dve", lambda v, thv=thv, pt2=pt2: v.tensor_copy(thv[:], pt2[:]),
               reads=[pt2.buf], writes=[thv.buf])
        for rc in range(nrc):
            wt = wt2[rc % 3]
            sgn = sgn2[rc % 2]; mag = mag2[rc % 2]; outw = outw2[rc % 2]
            ctx.dma(("sp", "act", "pool")[rc % 3], lambda e, wname=wname, rc=rc, wt=wt: e.dma_start(
                out=wt[:], in_=g[wname].ap[128 * rc:128 * (rc + 1), :]),
                writes=[wt.buf], tag="tern_ld2")
            ctx.op("act", lambda s, sgn=sgn, wt=wt: s.sign(sgn[:], wt[:]),
                   reads=[wt.buf], writes=[sgn.buf])
            ctx.op("act", lambda s, mag=mag, wt=wt: s.activation(mag[:], wt[:], AF.Abs),
                   reads=[wt.buf], writes=[mag.buf])
            ctx.op("dve", lambda v, mag=mag, thv=thv: v.tensor_scalar(
                out=mag[:], in0=mag[:], scalar1=thv[:], scalar2=None, op0=ALU.subtract),
                reads=[mag.buf, thv.buf], writes=[mag.buf])
            ctx.op("act", lambda s, mag=mag: s.sign(mag[:], mag[:]),
                   reads=[mag.buf], writes=[mag.buf])
            ctx.op("dve", lambda v, mag=mag: v.tensor_scalar(
                out=mag[:], in0=mag[:], scalar1=0.0, scalar2=None, op0=ALU.max),
                reads=[mag.buf], writes=[mag.buf])
            ctx.op("dve", lambda v, outw=outw, sgn=sgn, mag=mag: v.tensor_tensor(
                outw[:], sgn[:], mag[:], op=ALU.mult),
                reads=[sgn.buf, mag.buf], writes=[outw.buf])
            ctx.dma(("act", "pool", "sp")[rc % 3], lambda e, oname=oname, rc=rc, outw=outw: e.dma_start(
                out=g[oname].ap[128 * rc:128 * (rc + 1), :], in_=outw[:]),
                reads=[outw.buf], tag="tern_st")



def emit_combine(ctx, nc, g, T):
    nt2 = T // 256
    idb = sbuf(ctx, "p9_idb", [128, 128], BF16)
    ctx.dma("sp", lambda e: e.dma_start(out=idb[:], in_=g["c_identb"].ap), writes=[idb.buf])
    f12 = [sbuf(ctx, f"p9_f1{i}", [128, 8 * 256], BF16) for i in range(2)]
    s12 = [sbuf(ctx, f"p9_s1{i}", [128, 8 * 256], BF16) for i in range(2)]
    g1b2 = [sbuf(ctx, f"p9_g1b{i}", [128, 256], F32) for i in range(2)]
    prodb2 = [sbuf(ctx, f"p9_prodb{i}", [128, 256], BF16) for i in range(2)]
    xrow2 = [[sbuf(ctx, f"p9_xrow{i}_{s}", [128, 1024], BF16) for s in range(2)] for i in range(2)]
    xt2 = [sbuf(ctx, f"p9_xt{i}", [128, 1024], F32) for i in range(2)]
    x2f2 = [sbuf(ctx, f"p9_x2f{i}", [128, 1024], F32) for i in range(2)]
    ptr2 = [psum(ctx, f"p9_ptr{i}", [128, 128], BF16) for i in range(2)]
    for it in range(nt2):
        ts0 = 256 * it
        f1 = f12[it % 2]; s1 = s12[it % 2]; xrow = xrow2[it % 2]
        ctx.dma("sp", lambda e, ts0=ts0, f1=f1: e.dma_start(out=f1[:], in_=fm_block_ap(g["f1out"], T, ts0, 256)),
                writes=[f1.buf], tag="ldf1")
        ctx.dma("pool", lambda e, ts0=ts0, s1=s1: e.dma_start(out=s1[:], in_=fm_block_ap(g["s1"], T, ts0, 256)),
                writes=[s1.buf], tag="lds1")
        for dc in range(8):
            g1b = g1b2[dc % 2]; prodb = prodb2[dc % 2]
            ctx.op("act", lambda s, dc=dc, f1=f1, g1b=g1b: s.activation(g1b[:], blk_slice(f1, dc, 256), AF.Silu),
                   reads=[f1.buf], writes=[g1b.buf])
            ctx.op("dve", lambda v, dc=dc, g1b=g1b, prodb=prodb, s1=s1: v.tensor_tensor(prodb[:], g1b[:], blk_slice(s1, dc, 256),
                                                           op=ALU.mult),
                   reads=[g1b.buf, s1.buf], writes=[prodb.buf])
            for sx in range(2):
                ptr = ptr2[sx]
                ctx.op("pe", lambda t, sx=sx, prodb=prodb, ptr=ptr: t.transpose(ptr[:], prodb[:, 128 * sx:128 * (sx + 1)], idb[:]),
                       reads=[prodb.buf, idb.buf], writes=[ptr.buf])
                ctx.op("dve", lambda v, sx=sx, dc=dc, ptr=ptr, xrow=xrow: v.tensor_copy(xrow[sx][:, 128 * dc:128 * (dc + 1)], ptr[:]),
                       reads=[ptr.buf], writes=[xrow[sx].buf])
        for sx in range(2):
            t0 = ts0 + 128 * sx
            xt = xt2[sx]; x2f = x2f2[sx]
            ctx.dma("sp", lambda e, t0=t0, xt=xt: e.dma_start(out=xt[:], in_=g["x1"].ap[t0:t0 + 128, :]),
                    writes=[xt.buf], tag="ldx1c")
            ctx.op("dve", lambda v, sx=sx, xt=xt, x2f=x2f, xrow=xrow: v.tensor_tensor(x2f[:], xt[:], xrow[sx][:], op=ALU.add),
                   reads=[xt.buf, xrow[sx].buf], writes=[x2f.buf])
            ctx.dma("sp", lambda e, t0=t0, x2f=x2f: e.dma_start(out=g["out"].ap[t0:t0 + 128, :], in_=x2f[:]),
                    reads=[x2f.buf], tag="stout")



def _shapes(T):
    sh = {
        "x": ([T, 1024], F32),
        "L0_attn_Wqkv": ([1024, 3072], F32),
        "L0_attn_bqkv": ([128, 24], F32),
        "L0_attn_Wout": ([1024, 1024], F32),
        "L0_attn_bout": ([128, 8], F32),
        "L0_ffn_w_gate": ([3072, 1024], I32),
        "L0_ffn_w_up": ([3072, 1024], I32),
        "L0_ffn_w_down": ([1024, 3072], I32),
        "L1_attn_W_qkv": ([1024, 4096], F32),
        "L1_attn_W_g": ([1024, 2048], F32),
        "L1_attn_W_proj": ([2048, 1024], F32),
        "L1_attn_decay": ([1, 8], F32),
        "L1_ffn_W1": ([4096, 1024], F32),
        "L1_ffn_b1": ([128, 32], F32),
        "L1_ffn_W2": ([1024, 4096], F32),
        "L1_ffn_b2": ([128, 8], F32),
        "xn0": ([1024, T], BF16), "q0h": ([1024, T], BF16), "k0r": ([1024, T], BF16),
        "v0p": ([T, 1040], BF16), "ch0": ([128, 16], F32),
        "s0": ([1024, T], BF16), "x1": ([T, 1024], F32), "xn1": ([1024, T], BF16),
        "q1t": ([1024, T], BF16), "k1t": ([1024, T], BF16), "v1": ([T, 2048], BF16),
        "normed1": ([2048, T], BF16), "s1": ([1024, T], BF16), "f1out": ([1024, T], BF16),
        "dk_bq": ([128, 1024], BF16), "dk_bk": ([128, 1024], BF16), "dk_g128": ([128, 8], F32),
        "w1n": ([4096, 1024], BF16), "w2n": ([1024, 4096], BF16),
        "out": ([T, 1024], F32),
    }
    for k, v in host_constants(T).items():
        sh["c_" + k] = (list(v.shape), {np.dtype(np.float32): F32, np.dtype(BF): BF16}[v.dtype])
    return sh


PHASES = [
    ("decay", emit_decayprep,
     ["L1_attn_decay", "c_iotar", "c_onesrow_b", "c_onesrow"],
     ["dk_bq", "dk_bk", "dk_g128"]),
    ("pre0", emit_pre0,
     ["x", "L0_attn_Wqkv", "L0_attn_bqkv", "c_r0cq", "c_r0sq", "c_r0ck", "c_r0sk",
      "c_p0t", "c_identb", "c_onescol", "c_onesrow"],
     ["xn0", "q0h", "k0r", "v0p", "ch0"]),
    ("attn0", emit_attn0,
     ["q0h", "k0r", "v0p", "ch0", "L0_attn_Wout", "L0_attn_bout", "L0_attn_bqkv",
      "c_onesrow_b"],
     ["s0"]),
    ("ffn0", emit_ffn0,
     ["xn0", "s0", "x", "L0_ffn_w_gate", "L0_ffn_w_up", "L0_ffn_w_down", "c_identb"],
     ["x1"]),
    ("ln1", emit_ln1, ["x1", "c_identb"], ["xn1"]),
    ("qkv1ret", emit_qkv1ret,
     ["xn1", "L1_attn_W_qkv", "c_x1cq", "c_x1sq", "c_x1ck", "c_x1sk", "c_p1t",
      "dk_bq", "dk_bk", "dk_g128", "c_causal", "c_identb", "c_onescol", "c_onesrow_b"],
     ["q1t", "k1t", "v1", "normed1"]),
    ("gp", emit_gateproj, ["xn1", "normed1", "L1_attn_W_g", "L1_attn_W_proj"], ["s1"]),
    ("tern", emit_ternary,
     ["L1_ffn_W1", "L1_ffn_W2", "c_onescol", "c_onesrow"], ["w1n", "w2n"]),
    ("ffn1", emit_ffn1, ["xn1", "w1n", "w2n", "L1_ffn_b1", "L1_ffn_b2", "c_identb"],
     ["f1out"]),
    ("comb", emit_combine, ["f1out", "s1", "x1", "c_identb"], ["out"]),
]


def _build_phase_nc(name, emit, ins, outs, T):
    from contextlib import ExitStack
    nc = bass.Bass("TRN2", target_bir_lowering=False)
    sh = _shapes(T)
    g = {}
    for nm in ins:
        g[nm] = dram(nc, nm, sh[nm][0], sh[nm][1], kind="ExternalInput")
    for nm in outs:
        g[nm] = dram(nc, nm, sh[nm][0], sh[nm][1], kind="ExternalOutput")
    with ExitStack() as stack:
        ctx = Ctx(nc, stack)
        emit(ctx, nc, g, T)
        ctx.barrier()
        sems = {}
        for k in list(ENGINES) + ALL_DMA_SEMS:
            sems[k] = stack.enter_context(nc.semaphore(f"sem_{k}"))
        blk = stack.enter_context(nc.Block())
        ctx.finalize(blk, sems)
    return nc


def prepare_host_inputs(inputs, T):
    d = {}
    d["L0_attn_Wqkv"] = np.ascontiguousarray(inputs["L0_attn_Wqkv"], np.float32)
    d["L0_attn_bqkv"] = np.ascontiguousarray(np.asarray(inputs["L0_attn_bqkv"], np.float32).reshape(24, 128).T)
    d["L0_attn_Wout"] = np.ascontiguousarray(inputs["L0_attn_Wout"], np.float32)
    d["L0_attn_bout"] = np.ascontiguousarray(np.asarray(inputs["L0_attn_bout"], np.float32).reshape(8, 128).T)
    for k in ("L0_ffn_w_gate", "L0_ffn_w_up", "L0_ffn_w_down"):
        d[k] = np.ascontiguousarray(inputs[k], np.int32)
    for k in ("L1_attn_W_qkv", "L1_attn_W_g", "L1_attn_W_proj", "L1_ffn_W1", "L1_ffn_W2"):
        d[k] = np.ascontiguousarray(inputs[k], np.float32)
    d["L1_attn_decay"] = np.ascontiguousarray(np.asarray(inputs["L1_attn_decay"], np.float32).reshape(1, 8))
    d["L1_ffn_b1"] = np.ascontiguousarray(np.asarray(inputs["L1_ffn_b1"], np.float32).reshape(32, 128).T)
    d["L1_ffn_b2"] = np.ascontiguousarray(np.asarray(inputs["L1_ffn_b2"], np.float32).reshape(8, 128).T)
    for k, v in host_constants(T).items():
        d["c_" + k] = np.ascontiguousarray(v)
    return d


def _build_fused_nc(T):
    from contextlib import ExitStack
    nc = bass.Bass("TRN2", target_bir_lowering=False)
    sh = _shapes(T)
    ext_in = set()
    for (_, _, ins, _) in PHASES:
        ext_in.update(ins)
    inter = set()
    for (_, _, _, outs) in PHASES:
        inter.update(o for o in outs if o != "out")
    ext_in -= inter
    ext_in.discard("out")
    g = {}
    for nm in sorted(ext_in):
        g[nm] = dram(nc, nm, sh[nm][0], sh[nm][1], kind="ExternalInput")
    for nm in sorted(inter):
        g[nm] = dram(nc, nm, sh[nm][0], sh[nm][1], kind="Internal")
    g["out"] = dram(nc, "out", sh["out"][0], sh["out"][1], kind="ExternalOutput")
    ctx = Ctx(nc, None)
    for i, (name, emit, ins, outs) in enumerate(PHASES):
        with ExitStack() as pstack:
            ctx.stack = pstack
            if i:
                ctx.barrier()
            emit(ctx, nc, g, T)
    ctx.barrier()
    with ExitStack() as st:
        sems = {}
        for k in list(ENGINES) + ALL_DMA_SEMS:
            sems[k] = st.enter_context(nc.semaphore(f"sem_{k}"))
        blk = st.enter_context(nc.Block())
        ctx.finalize(blk, sems)
    return nc, sorted(ext_in)


def kernel(**inputs):
    T = 4096
    ncores = 8
    shared = prepare_host_inputs(inputs, T)
    x = np.ascontiguousarray(np.asarray(inputs["L0_attn_x"], np.float32))
    nc, ext_in = _build_fused_nc(T)
    in_maps = []
    for i in range(ncores):
        m = {nm: shared[nm] for nm in ext_in if nm != "x"}
        m["x"] = np.ascontiguousarray(x[i])
        in_maps.append(m)
    res = run_bass_kernel_spmd(nc, in_maps, core_ids=list(range(ncores)))
    return np.stack([np.asarray(res.results[i]["out"]) for i in range(ncores)], axis=0)


def kernel_unfused(**inputs):
    T = 4096
    ncores = 8
    shared = prepare_host_inputs(inputs, T)
    x = np.ascontiguousarray(np.asarray(inputs["L0_attn_x"], np.float32))
    core_vals = [dict(shared) for _ in range(ncores)]
    for i in range(ncores):
        core_vals[i]["x"] = np.ascontiguousarray(x[i])
    for (name, emit, ins, outs) in PHASES:
        nc = _build_phase_nc(name, emit, ins, outs, T)
        in_maps = [{nm: core_vals[i][nm] for nm in ins} for i in range(ncores)]
        res = run_bass_kernel_spmd(nc, in_maps, core_ids=list(range(ncores)))
        for i in range(ncores):
            for nm in outs:
                core_vals[i][nm] = np.asarray(res.results[i][nm])
    return np.stack([core_vals[i]["out"] for i in range(ncores)], axis=0)


# version 15
# speedup vs baseline: 1.0363x; 1.0363x over previous
import math
import numpy as np
import ml_dtypes
import concourse.bass as bass
import concourse.mybir as mybir
from concourse.bass_utils import run_bass_kernel_spmd

F32 = mybir.dt.float32
BF16 = mybir.dt.bfloat16
I32 = mybir.dt.int32
AF = mybir.ActivationFunctionType
ALU = mybir.AluOpType
AX = mybir.AxisListType


ENGINES = ("pe", "act", "dve", "pool", "sp")
N_DMA_SEMS = 6
DMA_SEMS = {q: [f"d{q}{i}" for i in range(N_DMA_SEMS)] for q in ("sp", "act", "pool")}
ALL_DMA_SEMS = [k for v in DMA_SEMS.values() for k in v]


class Buf:
    __slots__ = ("name", "w", "r")

    def __init__(self, name):
        self.name = name
        self.w = None
        self.r = {}


class Op:
    __slots__ = ("eng", "fn", "waits", "inc", "tag")

    def __init__(self, eng, fn, waits, inc, tag):
        self.eng = eng
        self.fn = fn
        self.waits = waits
        self.inc = inc
        self.tag = tag


class Ctx:
    def __init__(self, nc, stack=None):
        self.nc = nc
        self.stack = stack
        self.ops = []
        self.counts = {}
        self.waited = {}
        for e in ENGINES:
            self.counts[e] = 0
        for ks in DMA_SEMS.values():
            for k in ks:
                self.counts[k] = 0
        self._dma_rot = {q: 0 for q in DMA_SEMS}

    def _collect_waits(self, eng, reads, writes):
        need = {}
        def add(tok):
            if tok is None:
                return
            k, v = tok
            if need.get(k, -1) < v:
                need[k] = v
        for b in reads:
            add(b.w)
        for b in writes:
            add(b.w)
            for k, v in b.r.items():
                add((k, v))
        waits = []
        for k, v in need.items():
            if k == eng == "pe":
                continue
            done = self.waited.get((eng, k), 0)
            if k == eng and v <= self.counts[eng]:
                pass
            if v > done:
                waits.append((k, v))
                self.waited[(eng, k)] = v
        return waits

    def op(self, eng, fn, reads=(), writes=(), tag="", inc=True):
        waits = self._collect_waits(eng, reads, writes)
        inc_tok = None
        if inc:
            self.counts[eng] += 1
            inc_tok = (eng, 1)
        self.ops.append(Op(eng, fn, waits, inc_tok, tag))
        tok = (eng, self.counts[eng])
        for b in writes:
            b.w = tok
            b.r = {}
        for b in reads:
            if b.r.get(eng, 0) < tok[1]:
                b.r[eng] = tok[1]
        return tok

    def mm(self, fn, reads, write, acc_first=True, acc_last=True, tag="mm"):
        waits = self._collect_waits("pe", reads, (write,) if acc_first else ())
        inc_tok = None
        if acc_last:
            self.counts["pe"] += 1
            inc_tok = ("pe", 1)
        self.ops.append(Op("pe", fn, waits, inc_tok, tag))
        if acc_last:
            tok = ("pe", self.counts["pe"])
            write.w = tok
            write.r = {}
        for b in reads:
            b.r["pe"] = self.counts["pe"] + (0 if acc_last else 1)

    def dma(self, queue, fn, reads=(), writes=(), tag="dma", sem=None):
        if sem is None:
            sem = self._dma_rot[queue]
            self._dma_rot[queue] = (sem + 1) % N_DMA_SEMS
        dkey = DMA_SEMS[queue][sem]
        waits = self._collect_waits(queue, reads, writes)
        prev = self.counts[dkey]
        if prev > 0 and self.waited.get((queue, dkey), 0) < prev:
            waits.append((dkey, prev))
            self.waited[(queue, dkey)] = prev
        self.counts[dkey] += 16
        self.ops.append(Op(queue, fn, waits, (dkey, 16), tag))
        tok = (dkey, self.counts[dkey])
        for b in writes:
            b.w = tok
            b.r = {}
        for b in reads:
            if b.r.get(dkey, 0) < tok[1]:
                b.r[dkey] = tok[1]

    def barrier(self):
        snapshot = [(k, v) for k, v in self.counts.items() if v > 0]
        for e in ENGINES:
            for k, v in snapshot:
                if k == e:
                    continue
                if self.waited.get((e, k), 0) < v:
                    self.waited[(e, k)] = v
                    self.ops.append(Op(e, None, [(k, v)], None, "barrier"))

    def finalize(self, blk, sems):
        per_eng = {e: [] for e in ENGINES}
        for o in self.ops:
            per_eng[o.eng].append(o)
        attr = {"pe": "tensor", "act": "scalar", "dve": "vector",
                "pool": "gpsimd", "sp": "sync"}

        def run_engine(ops_list):
            def body(eng):
                for o in ops_list:
                    for (k, v) in o.waits:
                        eng.wait_ge(sems[k], v)
                    if o.fn is None:
                        continue
                    ins = o.fn(eng)
                    if o.inc is not None:
                        ins.then_inc(sems[o.inc[0]], o.inc[1])
            return body

        for e in ENGINES:
            if per_eng[e]:
                getattr(blk, attr[e])(run_engine(per_eng[e]))



class Tile:
    __slots__ = ("t", "buf")

    def __init__(self, t, name):
        self.t = t
        self.buf = Buf(name)

    def __getitem__(self, idx):
        return self.t[idx]

    @property
    def ap(self):
        return self.t[:]


def sbuf(ctx, name, shape, dtype):
    return Tile(ctx.stack.enter_context(ctx.nc.sbuf_tensor(name, shape, dtype)), name)


def psum(ctx, name, shape, dtype=F32):
    return Tile(ctx.stack.enter_context(ctx.nc.psum_tensor(name, shape, dtype)), name)


def dram(nc, name, shape, dtype, kind="Internal"):
    return Tile(nc.dram_tensor(name, shape, dtype, kind=kind).ap(), name)



D = 1024
NH0 = 16
HD0 = 64
FF0 = 3072
QK1 = 1024
V1 = 2048
NH1 = 8
FF1 = 4096
EPS = 1e-5

BF = ml_dtypes.bfloat16



def host_constants(T):
    c = {}
    c["identb"] = np.eye(128, dtype=BF)
    c["onescol"] = np.ones((128, 1), np.float32)
    c["onesrow"] = np.ones((1, 128), np.float32)
    c["onesrow_b"] = np.ones((1, 128), BF)
    hd2 = HD0 // 2
    inv = 1.0 / 10000.0 ** (np.arange(0, HD0, 2, dtype=np.float64) / HD0)
    t = np.arange(T, dtype=np.float64)
    fr = np.outer(inv, t)
    cos = np.cos(fr); sin = np.sin(fr)
    C = np.concatenate([cos, cos], axis=0)
    S = np.concatenate([sin, sin], axis=0)
    C2 = np.concatenate([C, C], axis=0)
    S2 = np.concatenate([S, S], axis=0)
    sc = 1.0 / math.sqrt(HD0)
    c["r0cq"] = (C2 * sc).astype(BF); c["r0sq"] = (S2 * sc).astype(BF)
    c["r0ck"] = C2.astype(BF); c["r0sk"] = S2.astype(BF)
    P0 = np.zeros((128, 128), np.float32)
    for base in (0, 64):
        for r in range(hd2):
            P0[base + r, base + hd2 + r] = -1.0
            P0[base + hd2 + r, base + r] = 1.0
    c["p0t"] = P0.T.copy().astype(BF)
    dim_half = QK1 // 2
    base_scale = (np.arange(0, QK1, 2, dtype=np.float64) + 0.4 * QK1) / (1.4 * QK1)
    pos = np.arange(T, dtype=np.float64)
    scale = base_scale[:, None] ** (pos[None, :] / 512.0)
    inv1 = 1.0 / 10000.0 ** (np.arange(dim_half, dtype=np.float64) / dim_half)
    sinu = inv1[:, None] * pos[None, :]
    sn = np.sin(sinu); cs = np.cos(sinu)
    rep = lambda a: np.repeat(a, 2, axis=0)
    c["x1cq"] = rep(cs * scale).astype(BF)
    c["x1sq"] = rep(sn * scale).astype(BF)
    c["x1ck"] = rep(cs / scale).astype(BF)
    c["x1sk"] = rep(sn / scale).astype(BF)
    P1 = np.zeros((128, 128), np.float32)
    for i in range(64):
        P1[2 * i, 2 * i + 1] = -1.0
        P1[2 * i + 1, 2 * i] = 1.0
    c["p1t"] = P1.T.copy().astype(BF)
    cz = np.triu(np.ones((128, 128), np.float32)).astype(BF)
    c["causal"] = cz
    c["iotar"] = np.arange(128, dtype=np.float32)[None, :]
    return c



def emit_decayprep(ctx, nc, g, T):
    dk = sbuf(ctx, "dp_dk", [1, 8], F32)
    ln = sbuf(ctx, "dp_ln", [1, 8], F32)
    nln = sbuf(ctx, "dp_nln", [1, 8], F32)
    g128 = sbuf(ctx, "dp_g128", [1, 8], F32)
    rowq = sbuf(ctx, "dp_rowq", [1, 128], F32)
    rowk = sbuf(ctx, "dp_rowk", [1, 128], F32)
    rows_bq = sbuf(ctx, "dp_rbq", [1, 1024], BF16)
    rows_bk = sbuf(ctx, "dp_rbk", [1, 1024], BF16)
    iot = sbuf(ctx, "dp_iot", [1, 128], F32)
    onr = sbuf(ctx, "dp_onr", [1, 128], BF16)
    evac = sbuf(ctx, "dp_evac", [128, 128], BF16)
    evf = sbuf(ctx, "dp_evf", [128, 8], F32)
    ps = psum(ctx, "dp_ps", [128, 128], F32)

    ctx.dma("sp", lambda e: e.dma_start(out=dk[0:1, :], in_=g["L1_attn_decay"].ap),
            writes=[dk.buf])
    ctx.dma("sp", lambda e: e.dma_start(out=iot[:], in_=g["c_iotar"].ap), writes=[iot.buf])
    ctx.dma("sp", lambda e: e.dma_start(out=onr[:], in_=g["c_onesrow_b"].ap), writes=[onr.buf])
    ctx.op("act", lambda s: s.activation(ln[:], dk[:], AF.Ln), reads=[dk.buf], writes=[ln.buf])
    ctx.op("dve", lambda v: v.tensor_scalar(out=nln[:], in0=ln[:], scalar1=-1.0, scalar2=None, op0=ALU.mult),
           reads=[ln.buf], writes=[nln.buf])
    ctx.op("act", lambda s: s.activation(g128[:], ln[:], AF.Exp, scale=128.0),
           reads=[ln.buf], writes=[g128.buf])
    for h in range(NH1):
        ctx.op("act", lambda s, h=h: s.activation(rowq[:], iot[:], AF.Exp, scale=ln[0:1, h:h + 1]),
               reads=[iot.buf, ln.buf], writes=[rowq.buf])
        ctx.op("dve", lambda v: v.tensor_scalar(out=rowq[:], in0=rowq[:], scalar1=1.0 / math.sqrt(128.0),
                                                scalar2=None, op0=ALU.mult),
               reads=[rowq.buf], writes=[rowq.buf])
        ctx.op("act", lambda s, h=h: s.activation(rowk[:], iot[:], AF.Exp, scale=nln[0:1, h:h + 1]),
               reads=[iot.buf, nln.buf], writes=[rowk.buf])
        ctx.op("dve", lambda v, h=h: v.tensor_copy(rows_bq[0:1, 128 * h:128 * (h + 1)], rowq[:]),
               reads=[rowq.buf], writes=[rows_bq.buf])
        ctx.op("dve", lambda v, h=h: v.tensor_copy(rows_bk[0:1, 128 * h:128 * (h + 1)], rowk[:]),
               reads=[rowk.buf], writes=[rows_bk.buf])
    for (rows, out) in ((rows_bq, "dk_bq"), (rows_bk, "dk_bk")):
        for h in range(NH1):
            ctx.mm(lambda t, rows=rows, h=h: t.matmul(ps[:], onr[:], rows[0:1, 128 * h:128 * (h + 1)],
                                                      start=True, stop=True),
                   [onr.buf, rows.buf], ps.buf)
            ctx.op("dve", lambda v: v.tensor_copy(evac[:], ps[:]), reads=[ps.buf], writes=[evac.buf])
            ctx.dma("sp", lambda e, out=out, h=h: e.dma_start(out=g[out].ap[:, 128 * h:128 * (h + 1)],
                                                              in_=evac[:]),
                    reads=[evac.buf])
    onrf = sbuf(ctx, "dp_onrf", [1, 128], F32)
    ps8 = psum(ctx, "dp_ps8", [128, 8], F32)
    ctx.dma("sp", lambda e: e.dma_start(out=onrf[:], in_=g["c_onesrow"].ap), writes=[onrf.buf])
    ctx.mm(lambda t: t.matmul(ps8[:], onrf[:], g128[:], start=True, stop=True),
           [onrf.buf, g128.buf], ps8.buf)
    ctx.op("dve", lambda v: v.tensor_copy(evf[:], ps8[:]), reads=[ps8.buf], writes=[evf.buf])
    ctx.dma("sp", lambda e: e.dma_start(out=g["dk_g128"].ap, in_=evf[:]), reads=[evf.buf])



def emit_pre0(ctx, nc, g, T):
    nt5 = T // 512
    wq = [sbuf(ctx, f"p1_wq{c}", [128, 3072], BF16) for c in range(8)]
    for c in range(8):
        ctx.dma("pool", lambda e, c=c: e.dma_start(out=wq[c][:], in_=g["L0_attn_Wqkv"].ap[128 * c:128 * (c + 1), :]),
                writes=[wq[c].buf], tag="wq")
    bq = sbuf(ctx, "p1_bq", [128, 24], F32)
    ctx.dma("sp", lambda e: e.dma_start(out=bq[:], in_=g["L0_attn_bqkv"].ap),
            writes=[bq.buf])
    tabs = {}
    for nm in ("r0cq", "r0sq", "r0ck", "r0sk"):
        tabs[nm] = sbuf(ctx, f"p1_{nm}", [128, T], BF16)
        ctx.dma("sp", lambda e, nm=nm: e.dma_start(out=tabs[nm][:], in_=g["c_" + nm].ap),
                writes=[tabs[nm].buf])
    p0t = sbuf(ctx, "p1_p0t", [128, 128], BF16)
    idb = sbuf(ctx, "p1_idb", [128, 128], BF16)
    ctx.dma("sp", lambda e: e.dma_start(out=p0t[:], in_=g["c_p0t"].ap), writes=[p0t.buf])
    ctx.dma("sp", lambda e: e.dma_start(out=idb[:], in_=g["c_identb"].ap), writes=[idb.buf])
    onescol = sbuf(ctx, "p1_ones", [128, 1], F32)
    onesrow = sbuf(ctx, "p1_onesr", [1, 128], F32)
    ctx.dma("sp", lambda e: e.dma_start(out=onescol[:], in_=g["c_onescol"].ap), writes=[onescol.buf])
    ctx.dma("sp", lambda e: e.dma_start(out=onesrow[:], in_=g["c_onesrow"].ap), writes=[onesrow.buf])

    xt2 = [sbuf(ctx, f"p1_xt{i}", [128, 1024], F32) for i in range(2)]
    xsq2 = [sbuf(ctx, f"p1_xsq{i}", [128, 1024], F32) for i in range(2)]
    mvec2 = [sbuf(ctx, f"p1_m{i}", [128, 1], F32) for i in range(2)]
    ssq2 = [sbuf(ctx, f"p1_ssq{i}", [128, 1], F32) for i in range(2)]
    var2 = [sbuf(ctx, f"p1_var{i}", [128, 1], F32) for i in range(2)]
    rstd2 = [sbuf(ctx, f"p1_rstd{i}", [128, 1], F32) for i in range(2)]
    xnb2 = [sbuf(ctx, f"p1_xnb{i}", [128, 1024], BF16) for i in range(2)]
    xnf = [sbuf(ctx, f"p1_xnf{c}", [128, 512], BF16) for c in range(8)]
    qraw2 = [sbuf(ctx, f"p1_qraw{i}", [128, 512], BF16) for i in range(2)]
    tmp12 = [sbuf(ctx, f"p1_tmp1{i}", [128, 512], F32) for i in range(2)]
    qhat2 = [sbuf(ctx, f"p1_qhat{i}", [128, 512], BF16) for i in range(2)]
    sqv2 = [sbuf(ctx, f"p1_sqv{i}", [128, 512], F32) for i in range(2)]
    vout2 = [sbuf(ctx, f"p1_vout{i}", [128, 1040], BF16) for i in range(2)]
    mq = sbuf(ctx, "p1_mq", [1, 16], F32)
    mk = sbuf(ctx, "p1_mk", [1, 16], F32)
    mtmp = sbuf(ctx, "p1_mtmp", [1, 1], F32)
    pacc2 = [psum(ctx, f"p1_pacc{i}", [128, 512], F32) for i in range(2)]
    prot2 = [psum(ctx, f"p1_prot{i}", [128, 512], F32) for i in range(2)]
    ptr = psum(ctx, "p1_ptr", [128, 128], BF16)
    pn1 = psum(ctx, "p1_pn1", [1, 512], F32)
    pn2 = psum(ctx, "p1_pn2", [1, 512], F32)
    pv = psum(ctx, "p1_pv", [128, 512], F32)

    ctx.op("dve", lambda v: v.memset(mq[:], 0.0), writes=[mq.buf])
    ctx.op("dve", lambda v: v.memset(mk[:], 0.0), writes=[mk.buf])

    def norms_update(src, acc, base, psn, hidx):
        ctx.mm(lambda t: t.matmul(psn[:], onescol[base:base + 64, :], src[base:base + 64, :],
                                  start=True, stop=True),
               [onescol.buf, src.buf], psn.buf, tag="norm")
        ctx.op("dve", lambda v: v.reduce_max(mtmp[:], psn[:], axis=AX.X),
               reads=[psn.buf], writes=[mtmp.buf])
        ctx.op("dve", lambda v: v.tensor_tensor(acc[0:1, hidx:hidx + 1], acc[0:1, hidx:hidx + 1],
                                                mtmp[:], op=ALU.max),
               reads=[acc.buf, mtmp.buf], writes=[acc.buf])

    for it in range(nt5):
        ts = slice(512 * it, 512 * (it + 1))
        for sub in range(4):
            p = sub % 2
            xt = xt2[p]; xsq = xsq2[p]; mvec = mvec2[p]; ssq = ssq2[p]
            var = var2[p]; rstd = rstd2[p]; xnb = xnb2[p]
            t0 = 512 * it + 128 * sub
            ctx.dma("sp", lambda e, t0=t0, xt=xt: e.dma_start(out=xt[:], in_=g["x"].ap[t0:t0 + 128, :]),
                    writes=[xt.buf], tag="ldx")
            ctx.op("dve", lambda v, xt=xt, mvec=mvec: v.reduce_sum(mvec[:], xt[:], axis=AX.X),
                   reads=[xt.buf], writes=[mvec.buf])
            ctx.op("dve", lambda v, mvec=mvec: v.tensor_scalar(out=mvec[:], in0=mvec[:], scalar1=1.0 / D,
                                                    scalar2=None, op0=ALU.mult),
                   reads=[mvec.buf], writes=[mvec.buf])
            ctx.op("act", lambda sc, xt=xt, xsq=xsq: sc.activation(xsq[:], xt[:], AF.Square),
                   reads=[xt.buf], writes=[xsq.buf])
            ctx.op("dve", lambda v, xsq=xsq, ssq=ssq: v.reduce_sum(ssq[:], xsq[:], axis=AX.X),
                   reads=[xsq.buf], writes=[ssq.buf])
            ctx.op("dve", lambda v, mvec=mvec, var=var: v.tensor_tensor(var[:], mvec[:], mvec[:], op=ALU.mult),
                   reads=[mvec.buf], writes=[var.buf])
            ctx.op("dve", lambda v, ssq=ssq, var=var: v.tensor_scalar(out=var[:], in0=ssq[:], scalar1=1.0 / D,
                                                    scalar2=var[:], op0=ALU.mult, op1=ALU.subtract),
                   reads=[ssq.buf, var.buf], writes=[var.buf])
            ctx.op("dve", lambda v, var=var: v.tensor_scalar(out=var[:], in0=var[:], scalar1=EPS,
                                                    scalar2=None, op0=ALU.add),
                   reads=[var.buf], writes=[var.buf])
            ctx.op("act", lambda s, var=var, rstd=rstd: s.activation(rstd[:], var[:], AF.Ln),
                   reads=[var.buf], writes=[rstd.buf])
            ctx.op("act", lambda s, rstd=rstd: s.activation(rstd[:], rstd[:], AF.Exp, scale=-0.5),
                   reads=[rstd.buf], writes=[rstd.buf])
            ctx.op("dve", lambda v, xt=xt, mvec=mvec, rstd=rstd, xnb=xnb: v.tensor_scalar(out=xnb[:], in0=xt[:], scalar1=mvec[:],
                                                    scalar2=rstd[:], op0=ALU.subtract, op1=ALU.mult),
                   reads=[xt.buf, mvec.buf, rstd.buf], writes=[xnb.buf])
            for c in range(8):
                ctx.op("pe", lambda t, c=c, xnb=xnb: t.transpose(ptr[:], xnb[:, 128 * c:128 * (c + 1)], idb[:]),
                       reads=[xnb.buf, idb.buf], writes=[ptr.buf])
                ctx.op("dve", lambda v, c=c, sub=sub: v.tensor_copy(xnf[c][:, 128 * sub:128 * (sub + 1)], ptr[:]),
                       reads=[ptr.buf], writes=[xnf[c].buf])
        for c in range(8):
            ctx.dma("act", lambda e, c=c, ts=ts: e.dma_start(out=g["xn0"].ap[128 * c:128 * (c + 1), ts],
                                                             in_=xnf[c][:]),
                    reads=[xnf[c].buf], tag="stxn")
        for f in range(16):
            is_q = f < 8
            pacc = pacc2[f % 2]; prot = prot2[f % 2]
            qraw = qraw2[f % 2]; tmp1 = tmp12[f % 2]; sqv = sqv2[f % 2]; qhat = qhat2[f % 2]
            for c in range(8):
                ctx.mm(lambda t, f=f, c=c, pacc=pacc: t.matmul(pacc[:], wq[c][:, 128 * f:128 * (f + 1)], xnf[c][:],
                                                    start=(c == 0), stop=(c == 7)),
                       [wq[c].buf, xnf[c].buf], pacc.buf, acc_first=(c == 0), acc_last=(c == 7), tag="qk_mm")
            ctx.op("act", lambda sc, f=f, pacc=pacc, qraw=qraw: sc.activation(qraw[:], pacc[:], AF.Identity,
                                                                             bias=bq[:, f:f + 1]),
                   reads=[pacc.buf, bq.buf], writes=[qraw.buf])
            ctx.mm(lambda t, prot=prot, qraw=qraw: t.matmul(prot[:], p0t[:], qraw[:], start=True, stop=True),
                   [p0t.buf, qraw.buf], prot.buf, tag="rot")
            ct = tabs["r0cq" if is_q else "r0ck"]
            stab = tabs["r0sq" if is_q else "r0sk"]
            ctx.op("dve", lambda v, ct=ct, ts=ts, qraw=qraw, tmp1=tmp1: v.tensor_tensor(tmp1[:], qraw[:], ct[:, ts], op=ALU.mult),
                   reads=[qraw.buf, ct.buf], writes=[tmp1.buf])
            ctx.op("dve", lambda v, stab=stab, ts=ts, prot=prot, sqv=sqv: v.tensor_tensor(sqv[:], prot[:], stab[:, ts], op=ALU.mult),
                   reads=[prot.buf, stab.buf], writes=[sqv.buf])
            ctx.op("dve", lambda v, tmp1=tmp1, sqv=sqv, qhat=qhat: v.tensor_tensor(qhat[:], tmp1[:], sqv[:], op=ALU.add),
                   reads=[tmp1.buf, sqv.buf], writes=[qhat.buf])
            dst = "q0h" if is_q else "k0r"
            fc = f if is_q else f - 8
            ctx.dma("act", lambda e, dst=dst, fc=fc, ts=ts, qhat=qhat: e.dma_start(
                out=g[dst].ap[128 * fc:128 * (fc + 1), ts], in_=qhat[:]),
                reads=[qhat.buf], tag="stqk")
            ctx.op("dve", lambda v, qhat=qhat, sqv=sqv: v.tensor_tensor(sqv[:], qhat[:], qhat[:], op=ALU.mult),
                   reads=[qhat.buf], writes=[sqv.buf])
            acc = mq if is_q else mk
            norms_update(sqv, acc, 0, pn1, 2 * fc)
            norms_update(sqv, acc, 64, pn2, 2 * fc + 1)
        for sub in range(4):
            vout = vout2[sub % 2]
            for vh in range(2):
                for c in range(8):
                    ctx.mm(lambda t, sub=sub, vh=vh, c=c: t.matmul(
                        pv[:], xnf[c][:, 128 * sub:128 * (sub + 1)],
                        wq[c][:, 2048 + 512 * vh:2048 + 512 * (vh + 1)],
                        start=(c == 0), stop=(c == 7)),
                        [xnf[c].buf, wq[c].buf], pv.buf, acc_first=(c == 0), acc_last=(c == 7), tag="v_mm")
                ctx.op("act", lambda sc, vh=vh, vout=vout: sc.copy(
                    bass.AP(vout[:].tensor, vout[:].offset + 65 * 8 * vh,
                            [list(vout[:].ap[0]), [65, 8], [1, 64]]),
                    bass.AP(pv[:].tensor, pv[:].offset, [list(pv[:].ap[0]), [64, 8], [1, 64]])),
                    reads=[pv.buf], writes=[vout.buf], tag="v_evac")
            ctx.op("dve", lambda v, vout=vout: v.memset(
                bass.AP(vout[:].tensor, vout[:].offset + 64, [list(vout[:].ap[0]), [65, 16]]), 1.0),
                writes=[vout.buf])
            t0 = 512 * it + 128 * sub
            ctx.dma("sp", lambda e, t0=t0, vout=vout: e.dma_start(out=g["v0p"].ap[t0:t0 + 128, :], in_=vout[:]),
                    reads=[vout.buf], tag="stv")
    prod = sbuf(ctx, "p1_prod", [1, 16], F32)
    negc = sbuf(ctx, "p1_negc", [1, 16], F32)
    chb = sbuf(ctx, "p1_chb", [128, 16], F32)
    pch = pv
    ctx.op("dve", lambda v: v.tensor_tensor(prod[:], mq[:], mk[:], op=ALU.mult),
           reads=[mq.buf, mk.buf], writes=[prod.buf])
    ctx.op("dve", lambda v: v.tensor_scalar(out=prod[:], in0=prod[:], scalar1=1e-30,
                                            scalar2=None, op0=ALU.add),
           reads=[prod.buf], writes=[prod.buf])
    ctx.op("act", lambda s: s.activation(prod[:], prod[:], AF.Ln),
           reads=[prod.buf], writes=[prod.buf])
    ctx.op("act", lambda s: s.activation(negc[:], prod[:], AF.Exp, scale=0.5),
           reads=[prod.buf], writes=[negc.buf])
    ctx.op("dve", lambda v: v.tensor_scalar(out=negc[:], in0=negc[:], scalar1=-1.0, scalar2=None,
                                            op0=ALU.mult),
           reads=[negc.buf], writes=[negc.buf])
    ctx.mm(lambda t: t.matmul(pch[:, 0:16], onesrow[:], negc[:], start=True, stop=True),
           [onesrow.buf, negc.buf], pch.buf)
    ctx.op("dve", lambda v: v.tensor_copy(chb[:], pch[:, 0:16]), reads=[pch.buf], writes=[chb.buf])
    ctx.dma("sp", lambda e: e.dma_start(out=g["ch0"].ap, in_=chb[:]), reads=[chb.buf])


def fm_block_ap(dr, T, ts0, width, nchunk=8, rows=128):
    a = dr.ap
    return bass.AP(a.tensor, a.offset + ts0, [[T, rows], [rows * T, nchunk], [1, width]])


def blk_slice(tile, c, width):
    return tile[:, c * width:(c + 1) * width]



def emit_attn0(ctx, nc, g, T):
    ntk = T // 128
    nq = T // 512
    kf = [sbuf(ctx, f"p2_kf{c}", [128, T], BF16) for c in range(8)]
    for c in range(8):
        ctx.dma("sp", lambda e, c=c: e.dma_start(out=kf[c][:], in_=g["k0r"].ap[128 * c:128 * (c + 1), :]),
                writes=[kf[c].buf], tag="ldk")
    vp = [sbuf(ctx, f"p2_vp{i}", [128, 1040], BF16) for i in range(ntk)]
    for i in range(ntk):
        ctx.dma("act", lambda e, i=i: e.dma_start(out=vp[i][:], in_=g["v0p"].ap[128 * i:128 * (i + 1), :]),
                writes=[vp[i].buf], tag="ldv")
    wo = [sbuf(ctx, f"p2_wo{c}", [128, 1024], BF16) for c in range(8)]
    for c in range(8):
        ctx.dma("pool", lambda e, c=c: e.dma_start(out=wo[c][:], in_=g["L0_attn_Wout"].ap[128 * c:128 * (c + 1), :]),
                writes=[wo[c].buf], tag="ldwo")
    ch = sbuf(ctx, "p2_ch", [128, 16], F32)
    ctx.dma("sp", lambda e: e.dma_start(out=ch[:], in_=g["ch0"].ap), writes=[ch.buf])
    bq = sbuf(ctx, "p2_bq", [128, 24], F32)
    ctx.dma("sp", lambda e: e.dma_start(out=bq[:], in_=g["L0_attn_bqkv"].ap),
            writes=[bq.buf])
    bo = sbuf(ctx, "p2_bo", [128, 8], F32)
    ctx.dma("sp", lambda e: e.dma_start(out=bo[:], in_=g["L0_attn_bout"].ap),
            writes=[bo.buf])
    onesrb = sbuf(ctx, "p2_onesrb", [1, 128], BF16)
    ctx.dma("sp", lambda e: e.dma_start(out=onesrb[:], in_=g["c_onesrow_b"].ap), writes=[onesrb.buf])

    bqvb = sbuf(ctx, "p2_bqvb", [128, 8], BF16)
    ctx.op("dve", lambda v: v.tensor_copy(bqvb[:], bq[:, 16:24]), reads=[bq.buf], writes=[bqvb.buf])
    boutp = sbuf(ctx, "p2_boutp", [128, 8], F32)

    qf2 = [sbuf(ctx, f"p2_qf{i}", [128, 8 * 512], BF16) for i in range(2)]
    aTA2 = [sbuf(ctx, f"p2_aTA{i}", [128, 1024], BF16) for i in range(2)]
    aTB2 = [sbuf(ctx, f"p2_aTB{i}", [128, 1024], BF16) for i in range(2)]
    pvsb2 = [[sbuf(ctx, f"p2_pvsb{p}_{i}", [128, 512], F32) for i in range(2)] for p in range(2)]
    dclq = [[sbuf(ctx, f"p2_dcl{p}_{i}", [1, 512], F32) for i in range(2)] for p in range(2)]
    recbq = [[sbuf(ctx, f"p2_recb{p}_{i}", [1, 512], BF16) for i in range(2)] for p in range(2)]
    bcf2 = [sbuf(ctx, f"p2_bcf{i}", [64, 512], F32) for i in range(2)]
    atf = [sbuf(ctx, f"p2_atf{c}", [128, 512], BF16) for c in range(8)]
    s0t2 = [sbuf(ctx, f"p2_s0t{i}", [128, 512], BF16) for i in range(2)]
    pscA = psum(ctx, "p2_pscA", [128, 1024], F32)
    pscB = psum(ctx, "p2_pscB", [128, 1024], F32)
    pvA = psum(ctx, "p2_pvA", [128, 512], F32)
    pvB = psum(ctx, "p2_pvB", [128, 512], F32)
    pbc = psum(ctx, "p2_pbc", [64, 512], F32)
    po = psum(ctx, "p2_po", [128, 512], F32)

    for dc in range(8):
        for c in range(8):
            ctx.mm(lambda t, dc=dc, c=c: t.matmul(po[:, dc:dc + 1], wo[c][:, 128 * dc:128 * (dc + 1)],
                                                  bqvb[:, c:c + 1], start=(c == 0), stop=(c == 7)),
                   [wo[c].buf, bqvb.buf], po.buf, acc_first=(c == 0), acc_last=(c == 7), tag="boutp")
    ctx.op("dve", lambda v: v.tensor_tensor(boutp[:], bo[:], po[:, 0:8], op=ALU.add),
           reads=[bo.buf, po.buf], writes=[boutp.buf])

    wup = sbuf(ctx, "p2_wup", [128, 512], BF16)
    ctx.op("dve", lambda v: v.memset(wup[:], 0.0), writes=[wup.buf])
    def pe_warmup(n):
        for i in range(n):
            ctx.mm(lambda t: t.matmul(po[:], kf[0][:, 0:128], wup[:], start=True, stop=True),
                   [kf[0].buf, wup.buf], po.buf, tag="warm")

    nhalf = ntk // 2

    def pair_main(hp, qf):
        for tkp in range(nhalf + 1):
            if tkp < nhalf:
                aTA = aTA2[tkp % 2]; aTB = aTB2[tkp % 2]
                for half in range(2):
                    tk = 2 * tkp + half
                    ctx.mm(lambda t, hp=hp, tk=tk, half=half, qf=qf: t.matmul(
                        pscA[:, 512 * half:512 * (half + 1)],
                        kf[hp][0:64, 128 * tk:128 * (tk + 1)],
                        qf[0:64, 512 * hp:512 * (hp + 1)], start=True, stop=True),
                        [kf[hp].buf, qf.buf], pscA.buf,
                        acc_first=(half == 0), acc_last=(half == 1), tag="scoreA")
                    ctx.mm(lambda t, hp=hp, tk=tk, half=half, qf=qf: t.matmul(
                        pscB[:, 512 * half:512 * (half + 1)],
                        kf[hp][64:128, 128 * tk:128 * (tk + 1)],
                        qf[64:128, 512 * hp:512 * (hp + 1)], start=True, stop=True),
                        [kf[hp].buf, qf.buf], pscB.buf,
                        acc_first=(half == 0), acc_last=(half == 1), tag="scoreB")
                ctx.op("act", lambda s, hp=hp, aTA=aTA: s.activation(aTA[:], pscA[:], AF.Exp,
                                                                    bias=ch[:, 2 * hp:2 * hp + 1]),
                       reads=[pscA.buf, ch.buf], writes=[aTA.buf], tag="expA")
                ctx.op("act", lambda s, hp=hp, aTB=aTB: s.activation(aTB[:], pscB[:], AF.Exp,
                                                                    bias=ch[:, 2 * hp + 1:2 * hp + 2]),
                       reads=[pscB.buf, ch.buf], writes=[aTB.buf], tag="expB")
            if tkp >= 1:
                kp = tkp - 1
                aTA = aTA2[kp % 2]; aTB = aTB2[kp % 2]
                for half in range(2):
                    tk = 2 * kp + half
                    ctx.mm(lambda t, hp=hp, tk=tk, half=half, aTA=aTA: t.matmul(
                        pvA[0:65, :], vp[tk][:, 65 * (2 * hp):65 * (2 * hp) + 65],
                        aTA[:, 512 * half:512 * (half + 1)],
                        start=(tk == 0), stop=(tk == ntk - 1)),
                        [vp[tk].buf, aTA.buf], pvA.buf,
                        acc_first=(tk == 0), acc_last=(tk == ntk - 1), tag="avA")
                    ctx.mm(lambda t, hp=hp, tk=tk, half=half, aTB=aTB: t.matmul(
                        pvB[0:65, :], vp[tk][:, 65 * (2 * hp + 1):65 * (2 * hp + 1) + 65],
                        aTB[:, 512 * half:512 * (half + 1)],
                        start=(tk == 0), stop=(tk == ntk - 1)),
                        [vp[tk].buf, aTB.buf], pvB.buf,
                        acc_first=(tk == 0), acc_last=(tk == ntk - 1), tag="avB")
                ctx.mm(lambda t: t.matmul(po[:], kf[0][:, 0:128], wup[:], start=True, stop=True),
                       [kf[0].buf, wup.buf], po.buf, tag="warm2")
        pvsb = pvsb2[hp % 2]
        ctx.op("dve", lambda v, pvsb=pvsb: v.tensor_copy(pvsb[0][0:65, :], pvA[0:65, :]), reads=[pvA.buf], writes=[pvsb[0].buf])
        ctx.op("dve", lambda v, pvsb=pvsb: v.tensor_copy(pvsb[1][0:65, :], pvB[0:65, :]), reads=[pvB.buf], writes=[pvsb[1].buf])
        for i in range(2):
            dcl = dclq[hp % 2][i]; recb = recbq[hp % 2][i]
            ctx.op("dve", lambda v, i=i, dcl=dcl, pvsb=pvsb: v.tensor_scalar(out=dcl[:], in0=pvsb[i][64:65, :], scalar1=1e-35,
                                                    scalar2=None, op0=ALU.max),
                   reads=[pvsb[i].buf], writes=[dcl.buf])
            ctx.op("dve", lambda v, dcl=dcl: v.reciprocal(dcl[:], dcl[:]), reads=[dcl.buf], writes=[dcl.buf])
            ctx.op("dve", lambda v, dcl=dcl, recb=recb: v.tensor_copy(recb[:], dcl[:]), reads=[dcl.buf], writes=[recb.buf])

    def pair_tail(hp):
        pvsb = pvsb2[hp % 2]
        for i in range(2):
            hb = i * 64
            recb = recbq[hp % 2][i]; bcf = bcf2[i]
            ctx.mm(lambda t, recb=recb: t.matmul(pbc[:], onesrb[0:1, 0:64], recb[:], start=True, stop=True),
                   [onesrb.buf, recb.buf], pbc.buf, tag="bcast")
            ctx.op("dve", lambda v, bcf=bcf: v.tensor_copy(bcf[:], pbc[:]), reads=[pbc.buf], writes=[bcf.buf])
            ctx.op("dve", lambda v, hp=hp, hb=hb, i=i, bcf=bcf, pvsb=pvsb: v.tensor_tensor(
                atf[hp][hb:hb + 64, :], pvsb[i][0:64, :], bcf[:], op=ALU.mult),
                reads=[pvsb[i].buf, bcf.buf], writes=[atf[hp].buf], tag="norm")

    for iq in range(nq):
        ts0 = 512 * iq
        qf = qf2[iq % 2]
        ctx.dma("sp", lambda e, ts0=ts0, qf=qf: e.dma_start(out=qf[:], in_=fm_block_ap(g["q0h"], T, ts0, 512)),
                writes=[qf.buf], tag="ldq")
        pe_warmup(24)
        for hp in range(9):
            if hp < 8:
                pair_main(hp, qf)
            if hp >= 1:
                pair_tail(hp - 1)
        for dc in range(8):
            s0t = s0t2[dc % 2]
            for c in range(8):
                ctx.mm(lambda t, dc=dc, c=c: t.matmul(po[:], wo[c][:, 128 * dc:128 * (dc + 1)], atf[c][:],
                                                      start=(c == 0), stop=(c == 7)),
                       [wo[c].buf, atf[c].buf], po.buf, acc_first=(c == 0), acc_last=(c == 7), tag="out0")
            ctx.op("dve", lambda v, dc=dc, s0t=s0t: v.tensor_scalar(out=s0t[:], in0=po[:], scalar1=boutp[:, dc:dc + 1],
                                                           scalar2=None, op0=ALU.add),
                   reads=[po.buf, boutp.buf], writes=[s0t.buf])
            ctx.dma("act", lambda e, dc=dc, ts0=ts0, s0t=s0t: e.dma_start(
                out=g["s0"].ap[128 * dc:128 * (dc + 1), ts0:ts0 + 512], in_=s0t[:]),
                reads=[s0t.buf], tag="sts0")



def emit_ffn0(ctx, nc, g, T):
    nt2 = T // 256
    idb = sbuf(ctx, "p3_idb", [128, 128], BF16)
    ctx.dma("sp", lambda e: e.dma_start(out=idb[:], in_=g["c_identb"].ap), writes=[idb.buf])
    wgT = [sbuf(ctx, f"p3_wgT{c}", [128, 3072], BF16) for c in range(8)]
    wuT = [sbuf(ctx, f"p3_wuT{c}", [128, 3072], BF16) for c in range(8)]
    wdT = [sbuf(ctx, f"p3_wdT{h}", [128, 1024], BF16) for h in range(24)]
    wtmp = [sbuf(ctx, f"p3_wtmp{i}", [128, 3072], BF16) for i in range(2)]
    ptw2 = [psum(ctx, f"p3_ptw{i}", [128, 128], BF16) for i in range(2)]
    for (wname, dstT) in (("L0_ffn_w_gate", wgT), ("L0_ffn_w_up", wuT)):
        for hc in range(24):
            w = wtmp[hc % 2]
            ctx.dma("pool", lambda e, wname=wname, hc=hc, w=w: e.dma_start(
                out=w[:, 0:1024], in_=g[wname].ap[128 * hc:128 * (hc + 1), :]),
                writes=[w.buf], tag="ldwg")
            for c in range(8):
                ptw = ptw2[c % 2]
                ctx.op("pe", lambda t, w=w, c=c, ptw=ptw: t.transpose(ptw[:], w[:, 128 * c:128 * (c + 1)], idb[:]),
                       reads=[w.buf, idb.buf], writes=[ptw.buf])
                ctx.op("dve", lambda v, c=c, hc=hc, dstT=dstT, ptw=ptw: v.tensor_copy(
                    dstT[c][:, 128 * hc:128 * (hc + 1)], ptw[:]),
                    reads=[ptw.buf], writes=[dstT[c].buf])
    for dc in range(8):
        w = wtmp[dc % 2]
        ctx.dma("pool", lambda e, dc=dc, w=w: e.dma_start(
            out=w[:], in_=g["L0_ffn_w_down"].ap[128 * dc:128 * (dc + 1), :]),
            writes=[w.buf], tag="ldwd")
        for hc in range(24):
            ptw = ptw2[hc % 2]
            ctx.op("pe", lambda t, w=w, hc=hc, ptw=ptw: t.transpose(ptw[:], w[:, 128 * hc:128 * (hc + 1)], idb[:]),
                   reads=[w.buf, idb.buf], writes=[ptw.buf])
            ctx.op("dve", lambda v, dc=dc, hc=hc, ptw=ptw: v.tensor_copy(wdT[hc][:, 128 * dc:128 * (dc + 1)], ptw[:]),
                   reads=[ptw.buf], writes=[wdT[hc].buf])

    xf2 = [sbuf(ctx, f"p3_xf{i}", [128, 8 * 256], BF16) for i in range(2)]
    hsg2 = [sbuf(ctx, f"p3_hsg{i}", [128, 256], BF16) for i in range(2)]
    hup2 = [sbuf(ctx, f"p3_hup{i}", [128, 256], BF16) for i in range(2)]
    hhb = sbuf(ctx, "p3_hhb", [128, 24 * 256], BF16)
    g0b2 = [sbuf(ctx, f"p3_g0b{i}", [128, 256], BF16) for i in range(2)]
    s0t1 = sbuf(ctx, "p3_s0t", [128, 8 * 256], BF16)
    prodb2 = [sbuf(ctx, f"p3_prodb{i}", [128, 256], BF16) for i in range(2)]
    xrow = [sbuf(ctx, f"p3_xrow{s}", [128, 1024], BF16) for s in range(2)]
    xt1 = sbuf(ctx, "p3_xt", [128, 1024], F32)
    x1f1 = sbuf(ctx, "p3_x1f", [128, 1024], F32)
    psg2 = [psum(ctx, f"p3_psg{i}", [128, 256], F32) for i in range(2)]
    psu2 = [psum(ctx, f"p3_psu{i}", [128, 256], F32) for i in range(2)]
    pd2 = [psum(ctx, f"p3_pd{i}", [128, 256], F32) for i in range(2)]

    for it in range(nt2):
        ts0 = 256 * it
        xf = xf2[it % 2]
        s0t = s0t1
        ctx.dma("sp", lambda e, ts0=ts0, xf=xf: e.dma_start(out=xf[:], in_=fm_block_ap(g["xn0"], T, ts0, 256)),
                writes=[xf.buf], tag="ldxn")
        ctx.dma("act", lambda e, ts0=ts0, s0t=s0t: e.dma_start(out=s0t[:], in_=fm_block_ap(g["s0"], T, ts0, 256)),
                writes=[s0t.buf], tag="lds0")
        for hc in range(24):
            psg = psg2[hc % 2]; psu = psu2[hc % 2]; hsg = hsg2[hc % 2]; hup = hup2[hc % 2]
            for c in range(8):
                ctx.mm(lambda t, hc=hc, c=c, psg=psg, xf=xf: t.matmul(psg[:], wgT[c][:, 128 * hc:128 * (hc + 1)],
                                                      blk_slice(xf, c, 256), start=(c == 0), stop=(c == 7)),
                       [wgT[c].buf, xf.buf], psg.buf, acc_first=(c == 0), acc_last=(c == 7), tag="gate")
            for c in range(8):
                ctx.mm(lambda t, hc=hc, c=c, psu=psu, xf=xf: t.matmul(psu[:], wuT[c][:, 128 * hc:128 * (hc + 1)],
                                                      blk_slice(xf, c, 256), start=(c == 0), stop=(c == 7)),
                       [wuT[c].buf, xf.buf], psu.buf, acc_first=(c == 0), acc_last=(c == 7), tag="up")
            ctx.op("act", lambda s, psg=psg, hsg=hsg: s.activation(hsg[:], psg[:], AF.Silu, scale=1.0 / 127.0),
                   reads=[psg.buf], writes=[hsg.buf])
            ctx.op("dve", lambda v, psu=psu, hup=hup: v.tensor_scalar(out=hup[:], in0=psu[:], scalar1=1.0 / 127.0,
                                                    scalar2=None, op0=ALU.mult),
                   reads=[psu.buf], writes=[hup.buf])
            ctx.op("dve", lambda v, hc=hc, hsg=hsg, hup=hup: v.tensor_tensor(blk_slice(hhb, hc, 256), hsg[:], hup[:],
                                                           op=ALU.mult),
                   reads=[hsg.buf, hup.buf], writes=[hhb.buf])
        for dc in range(8):
            pd = pd2[dc % 2]; g0b = g0b2[dc % 2]; prodb = prodb2[dc % 2]
            for hc in range(24):
                ctx.mm(lambda t, hc=hc, dc=dc, pd=pd: t.matmul(pd[:], wdT[hc][:, 128 * dc:128 * (dc + 1)],
                                                        blk_slice(hhb, hc, 256),
                                                        start=(hc == 0), stop=(hc == 23)),
                       [wdT[hc].buf, hhb.buf], pd.buf, acc_first=(hc == 0),
                       acc_last=(hc == 23), tag="down")
            ctx.op("act", lambda s, dc=dc, pd=pd, g0b=g0b: s.activation(g0b[:], pd[:],
                                                        AF.Silu, scale=1.0 / 127.0),
                   reads=[pd.buf], writes=[g0b.buf], tag="g0")
            ctx.op("dve", lambda v, dc=dc, g0b=g0b, prodb=prodb, s0t=s0t: v.tensor_tensor(prodb[:], g0b[:], blk_slice(s0t, dc, 256),
                                                           op=ALU.mult),
                   reads=[g0b.buf, s0t.buf], writes=[prodb.buf])
            for sx in range(2):
                ptw = ptw2[sx % 2]
                ctx.op("pe", lambda t, sx=sx, prodb=prodb, ptw=ptw: t.transpose(ptw[:], prodb[:, 128 * sx:128 * (sx + 1)], idb[:]),
                       reads=[prodb.buf, idb.buf], writes=[ptw.buf])
                ctx.op("dve", lambda v, sx=sx, dc=dc, ptw=ptw: v.tensor_copy(xrow[sx][:, 128 * dc:128 * (dc + 1)], ptw[:]),
                       reads=[ptw.buf], writes=[xrow[sx].buf])
        for sx in range(2):
            t0 = ts0 + 128 * sx
            xt = xt1; x1f = x1f1
            ctx.dma("sp", lambda e, t0=t0, xt=xt: e.dma_start(out=xt[:], in_=g["x"].ap[t0:t0 + 128, :]),
                    writes=[xt.buf], tag="ldx")
            ctx.op("dve", lambda v, sx=sx, xt=xt, x1f=x1f: v.tensor_tensor(x1f[:], xt[:], xrow[sx][:], op=ALU.add),
                   reads=[xt.buf, xrow[sx].buf], writes=[x1f.buf])
            ctx.dma("sp", lambda e, t0=t0, x1f=x1f: e.dma_start(out=g["x1"].ap[t0:t0 + 128, :], in_=x1f[:]),
                    reads=[x1f.buf], tag="stx1")



def emit_ln1(ctx, nc, g, T):
    nt5 = T // 512
    idb = sbuf(ctx, "p4_idb", [128, 128], BF16)
    ctx.dma("sp", lambda e: e.dma_start(out=idb[:], in_=g["c_identb"].ap), writes=[idb.buf])
    xt2 = [sbuf(ctx, f"p4_xt{i}", [128, 1024], F32) for i in range(2)]
    xsq2 = [sbuf(ctx, f"p4_xsq{i}", [128, 1024], F32) for i in range(2)]
    mvec2 = [sbuf(ctx, f"p4_m{i}", [128, 1], F32) for i in range(2)]
    ssq2 = [sbuf(ctx, f"p4_ssq{i}", [128, 1], F32) for i in range(2)]
    var2 = [sbuf(ctx, f"p4_var{i}", [128, 1], F32) for i in range(2)]
    rstd2 = [sbuf(ctx, f"p4_rstd{i}", [128, 1], F32) for i in range(2)]
    xnb2 = [sbuf(ctx, f"p4_xnb{i}", [128, 1024], BF16) for i in range(2)]
    xnf2 = [[sbuf(ctx, f"p4_xnf{i}_{c}", [128, 512], BF16) for c in range(8)] for i in range(2)]
    ptr2 = [psum(ctx, f"p4_ptr{i}", [128, 128], BF16) for i in range(2)]
    for it in range(nt5):
        ts = slice(512 * it, 512 * (it + 1))
        xnf = xnf2[it % 2]
        for sub in range(4):
            p = sub % 2
            xt = xt2[p]; xsq = xsq2[p]; mvec = mvec2[p]; ssq = ssq2[p]
            var = var2[p]; rstd = rstd2[p]; xnb = xnb2[p]
            t0 = 512 * it + 128 * sub
            ctx.dma("sp", lambda e, t0=t0, xt=xt: e.dma_start(out=xt[:], in_=g["x1"].ap[t0:t0 + 128, :]),
                    writes=[xt.buf], tag="ldx1")
            ctx.op("dve", lambda v, xt=xt, mvec=mvec: v.reduce_sum(mvec[:], xt[:], axis=AX.X),
                   reads=[xt.buf], writes=[mvec.buf])
            ctx.op("dve", lambda v, mvec=mvec: v.tensor_scalar(out=mvec[:], in0=mvec[:], scalar1=1.0 / D,
                                                    scalar2=None, op0=ALU.mult),
                   reads=[mvec.buf], writes=[mvec.buf])
            ctx.op("dve", lambda v, xt=xt, xsq=xsq: v.tensor_tensor(xsq[:], xt[:], xt[:], op=ALU.mult),
                   reads=[xt.buf], writes=[xsq.buf])
            ctx.op("dve", lambda v, xsq=xsq, ssq=ssq: v.reduce_sum(ssq[:], xsq[:], axis=AX.X),
                   reads=[xsq.buf], writes=[ssq.buf])
            ctx.op("dve", lambda v, mvec=mvec, var=var: v.tensor_tensor(var[:], mvec[:], mvec[:], op=ALU.mult),
                   reads=[mvec.buf], writes=[var.buf])
            ctx.op("dve", lambda v, ssq=ssq, var=var: v.tensor_scalar(out=var[:], in0=ssq[:], scalar1=1.0 / D,
                                                    scalar2=var[:], op0=ALU.mult, op1=ALU.subtract),
                   reads=[ssq.buf, var.buf], writes=[var.buf])
            ctx.op("dve", lambda v, var=var: v.tensor_scalar(out=var[:], in0=var[:], scalar1=EPS,
                                                    scalar2=None, op0=ALU.add),
                   reads=[var.buf], writes=[var.buf])
            ctx.op("act", lambda s, var=var, rstd=rstd: s.activation(rstd[:], var[:], AF.Ln),
                   reads=[var.buf], writes=[rstd.buf])
            ctx.op("act", lambda s, rstd=rstd: s.activation(rstd[:], rstd[:], AF.Exp, scale=-0.5),
                   reads=[rstd.buf], writes=[rstd.buf])
            ctx.op("dve", lambda v, xt=xt, mvec=mvec, rstd=rstd, xnb=xnb: v.tensor_scalar(out=xnb[:], in0=xt[:], scalar1=mvec[:],
                                                    scalar2=rstd[:], op0=ALU.subtract, op1=ALU.mult),
                   reads=[xt.buf, mvec.buf, rstd.buf], writes=[xnb.buf])
            for c in range(8):
                ptr = ptr2[c % 2]
                ctx.op("pe", lambda t, c=c, xnb=xnb, ptr=ptr: t.transpose(ptr[:], xnb[:, 128 * c:128 * (c + 1)], idb[:]),
                       reads=[xnb.buf, idb.buf], writes=[ptr.buf])
                ctx.op("dve", lambda v, c=c, sub=sub, ptr=ptr, xnf=xnf: v.tensor_copy(xnf[c][:, 128 * sub:128 * (sub + 1)], ptr[:]),
                       reads=[ptr.buf], writes=[xnf[c].buf])
        for c in range(8):
            ctx.dma("act", lambda e, c=c, ts=ts, xnf=xnf: e.dma_start(out=g["xn1"].ap[128 * c:128 * (c + 1), ts],
                                                             in_=xnf[c][:]),
                    reads=[xnf[c].buf], tag="stxn1")



def emit_qkv1(ctx, nc, g, T):
    nt5 = T // 512
    wq1 = [sbuf(ctx, f"p5_wq{c}", [128, 4096], BF16) for c in range(8)]
    for c in range(8):
        ctx.dma("pool", lambda e, c=c: e.dma_start(out=wq1[c][:], in_=g["L1_attn_W_qkv"].ap[128 * c:128 * (c + 1), :]),
                writes=[wq1[c].buf], tag="ldw1")
    p1t = sbuf(ctx, "p5_p1t", [128, 128], BF16)
    ctx.dma("sp", lambda e: e.dma_start(out=p1t[:], in_=g["c_p1t"].ap), writes=[p1t.buf])
    dkbq = sbuf(ctx, "p5_dkbq", [128, 1024], BF16)
    dkbk = sbuf(ctx, "p5_dkbk", [128, 1024], BF16)
    ctx.dma("sp", lambda e: e.dma_start(out=dkbq[:], in_=g["dk_bq"].ap), writes=[dkbq.buf])
    ctx.dma("sp", lambda e: e.dma_start(out=dkbk[:], in_=g["dk_bk"].ap), writes=[dkbk.buf])
    tabs = {}
    for nm in ("x1cq", "x1sq", "x1ck", "x1sk"):
        tabs[nm] = sbuf(ctx, f"p5_{nm}", [128, 4096], BF16)

    xf2 = [sbuf(ctx, f"p5_xf{i}", [128, 8 * 512], BF16) for i in range(2)]
    qraw2 = [sbuf(ctx, f"p5_qraw{i}", [128, 512], BF16) for i in range(2)]
    tmp12 = [sbuf(ctx, f"p5_tmp1{i}", [128, 512], F32) for i in range(2)]
    sqv2 = [sbuf(ctx, f"p5_sqv{i}", [128, 512], F32) for i in range(2)]
    qrope2 = [sbuf(ctx, f"p5_qrope{i}", [128, 512], F32) for i in range(2)]
    qhat2 = [sbuf(ctx, f"p5_qhat{i}", [128, 512], BF16) for i in range(2)]
    vout2 = [sbuf(ctx, f"p5_vout{i}", [128, 2048], BF16) for i in range(2)]
    pacc2 = [psum(ctx, f"p5_pacc{i}", [128, 512], F32) for i in range(2)]
    prot2 = [psum(ctx, f"p5_prot{i}", [128, 512], F32) for i in range(2)]
    pv2 = [psum(ctx, f"p5_pv{i}", [128, 512], F32) for i in range(2)]

    for it in range(nt5):
        ts0 = 512 * it
        xf = xf2[it % 2]
        ctx.dma("sp", lambda e, ts0=ts0, xf=xf: e.dma_start(out=xf[:], in_=fm_block_ap(g["xn1"], T, ts0, 512)),
                writes=[xf.buf], tag="ldxn1")
        for nm in ("x1cq", "x1sq", "x1ck", "x1sk"):
            ctx.dma("act", lambda e, nm=nm, ts0=ts0: e.dma_start(out=tabs[nm][:],
                                                                 in_=fm_block_ap(g["c_" + nm], T, ts0, 512)),
                    writes=[tabs[nm].buf], tag="ldtab")
        for f in range(16):
            is_q = f < 8
            h = f if is_q else f - 8
            pacc = pacc2[f % 2]; prot = prot2[f % 2]
            qraw = qraw2[f % 2]; tmp1 = tmp12[f % 2]; sqv = sqv2[f % 2]
            qrope = qrope2[f % 2]; qhat = qhat2[f % 2]
            for c in range(8):
                ctx.mm(lambda t, f=f, c=c, pacc=pacc, xf=xf: t.matmul(pacc[:], wq1[c][:, 128 * f:128 * (f + 1)],
                                                    blk_slice(xf, c, 512), start=(c == 0), stop=(c == 7)),
                       [wq1[c].buf, xf.buf], pacc.buf, acc_first=(c == 0), acc_last=(c == 7), tag="qk1")
            ctx.op("dve", lambda v, pacc=pacc, qraw=qraw: v.tensor_copy(qraw[:], pacc[:]), reads=[pacc.buf], writes=[qraw.buf])
            ctx.mm(lambda t, prot=prot, qraw=qraw: t.matmul(prot[:], p1t[:], qraw[:], start=True, stop=True),
                   [p1t.buf, qraw.buf], prot.buf, tag="rot1")
            ct = tabs["x1cq" if is_q else "x1ck"]
            stab = tabs["x1sq" if is_q else "x1sk"]
            ctx.op("dve", lambda v, ct=ct, h=h, qraw=qraw, tmp1=tmp1: v.tensor_tensor(tmp1[:], qraw[:], blk_slice(ct, h, 512),
                                                                op=ALU.mult),
                   reads=[qraw.buf, ct.buf], writes=[tmp1.buf])
            ctx.op("dve", lambda v, stab=stab, h=h, prot=prot, sqv=sqv: v.tensor_tensor(sqv[:], prot[:], blk_slice(stab, h, 512),
                                                                    op=ALU.mult),
                   reads=[prot.buf, stab.buf], writes=[sqv.buf])
            ctx.op("dve", lambda v, tmp1=tmp1, sqv=sqv, qrope=qrope: v.tensor_tensor(qrope[:], tmp1[:], sqv[:], op=ALU.add),
                   reads=[tmp1.buf, sqv.buf], writes=[qrope.buf])
            gam = dkbq if is_q else dkbk
            def gmul(v, gam=gam, h=h, qrope=qrope, qhat=qhat):
                src = qrope[:]
                s3 = bass.AP(src.tensor, src.offset, [list(src.ap[0]), [128, 4], [1, 128]])
                d = qhat[:]
                d3 = bass.AP(d.tensor, d.offset, [list(d.ap[0]), [128, 4], [1, 128]])
                ga = gam[:, 128 * h:128 * (h + 1)]
                g3 = bass.AP(ga.tensor, ga.offset, [list(ga.ap[0]), [0, 4], [1, 128]])
                return v.tensor_tensor(d3, s3, g3, op=ALU.mult)
            ctx.op("dve", gmul, reads=[qrope.buf, dkbq.buf if is_q else dkbk.buf], writes=[qhat.buf])
            dst = "q1t" if is_q else "k1t"
            ctx.dma("act", lambda e, dst=dst, h=h, ts0=ts0, qhat=qhat: e.dma_start(
                out=g[dst].ap[128 * h:128 * (h + 1), ts0:ts0 + 512], in_=qhat[:]),
                reads=[qhat.buf], tag="stqk1")
        for sub in range(4):
            vout = vout2[sub % 2]
            for vh in range(4):
                pv = pv2[vh % 2]
                for c in range(8):
                    ctx.mm(lambda t, sub=sub, vh=vh, c=c, pv=pv, xf=xf: t.matmul(
                        pv[:], xf[:, 512 * c + 128 * sub:512 * c + 128 * (sub + 1)],
                        wq1[c][:, 2048 + 512 * vh:2048 + 512 * (vh + 1)],
                        start=(c == 0), stop=(c == 7)),
                        [xf.buf, wq1[c].buf], pv.buf, acc_first=(c == 0), acc_last=(c == 7), tag="v1mm")
                ctx.op("dve", lambda v, vh=vh, pv=pv, vout=vout: v.tensor_copy(vout[:, 512 * vh:512 * (vh + 1)], pv[:]),
                       reads=[pv.buf], writes=[vout.buf])
            t0 = ts0 + 128 * sub
            ctx.dma("sp", lambda e, t0=t0, vout=vout: e.dma_start(out=g["v1"].ap[t0:t0 + 128, :], in_=vout[:]),
                    reads=[vout.buf], tag="stv1")



def emit_retention(ctx, nc, g, T):
    ncnk = T // 128
    idb = sbuf(ctx, "p6_idb", [128, 128], BF16)
    csl = sbuf(ctx, "p6_csl", [128, 128], BF16)
    onc = sbuf(ctx, "p6_onc", [128, 1], F32)
    onrb = sbuf(ctx, "p6_onrb", [1, 128], BF16)
    g128 = sbuf(ctx, "p6_g128", [128, 8], F32)
    ctx.dma("sp", lambda e: e.dma_start(out=idb[:], in_=g["c_identb"].ap), writes=[idb.buf])
    ctx.dma("sp", lambda e: e.dma_start(out=csl[:], in_=g["c_causal"].ap), writes=[csl.buf])
    ctx.dma("sp", lambda e: e.dma_start(out=onc[:], in_=g["c_onescol"].ap), writes=[onc.buf])
    ctx.dma("sp", lambda e: e.dma_start(out=onrb[:], in_=g["c_onesrow_b"].ap), writes=[onrb.buf])
    ctx.dma("sp", lambda e: e.dma_start(out=g128[:], in_=g["dk_g128"].ap), writes=[g128.buf])

    oncb = sbuf(ctx, "p6_oncb", [128, 1], BF16)
    ctx.op("dve", lambda v: v.tensor_copy(oncb[:], onc[:]), reads=[onc.buf], writes=[oncb.buf])
    R = [sbuf(ctx, f"p6_R{h}", [128, 256], F32) for h in range(NH1)]
    Rb2 = [sbuf(ctx, f"p6_Rb{i}", [128, 256], BF16) for i in range(2)]
    for h in range(NH1):
        ctx.op("dve", lambda v, h=h: v.memset(R[h][:], 0.0), writes=[R[h].buf])

    qb2 = [sbuf(ctx, f"p6_qb{i}", [128, 1024], BF16) for i in range(2)]
    kb2 = [sbuf(ctx, f"p6_kb{i}", [128, 1024], BF16) for i in range(2)]
    vb2 = [sbuf(ctx, f"p6_vb{i}", [128, 2048], BF16) for i in range(2)]
    ktTs2 = [sbuf(ctx, f"p6_ktTs{i}", [128, 1024], BF16) for i in range(2)]
    aSs2 = [sbuf(ctx, f"p6_aSs{i}", [128, 1024], BF16) for i in range(2)]
    outb2 = [sbuf(ctx, f"p6_outb{i}", [128, 256], BF16) for i in range(2)]
    sqb2 = [sbuf(ctx, f"p6_sqb{i}", [128, 256], BF16) for i in range(2)]
    srows2 = [sbuf(ctx, f"p6_srows{i}", [1, 2048], F32) for i in range(2)]
    stats2 = [sbuf(ctx, f"p6_stats{i}", [1, 1024], F32) for i in range(2)]
    mu2 = [sbuf(ctx, f"p6_mu{i}", [1, 1024], F32) for i in range(2)]
    rsd2 = [sbuf(ctx, f"p6_rsd{i}", [1, 1024], F32) for i in range(2)]
    arow2 = [sbuf(ctx, f"p6_arow{i}", [1, 1024], BF16) for i in range(2)]
    brow2 = [sbuf(ctx, f"p6_brow{i}", [1, 1024], BF16) for i in range(2)]
    brf2 = [sbuf(ctx, f"p6_brf{i}", [1, 1024], F32) for i in range(2)]
    abf2 = [sbuf(ctx, f"p6_abf{i}", [128, 1024], BF16) for i in range(2)]
    nmd2 = [sbuf(ctx, f"p6_nmd{i}", [128, 2048], BF16) for i in range(2)]
    tmpn2 = [sbuf(ctx, f"p6_tmpn{i}", [128, 1024], BF16) for i in range(2)]
    psS2 = [psum(ctx, f"p6_psS{i}", [128, 128], F32) for i in range(2)]
    ptb = psum(ctx, "p6_ptb", [128, 128], BF16)
    pav2 = [psum(ctx, f"p6_pav{i}", [128, 256], F32) for i in range(2)]
    pkv = psum(ctx, "p6_pkv", [128, 256], F32)
    pstat = psum(ctx, "p6_pstat", [1, 256], F32)
    pbc = psum(ctx, "p6_pbc", [128, 512], F32)

    def stageA(ck):
        p = ck % 2
        t0 = 128 * ck
        qb, kb, vb = qb2[p], kb2[p], vb2[p]
        ktTs, aSs = ktTs2[p], aSs2[p]
        ctx.dma("sp", lambda e, t0=t0, qb=qb: e.dma_start(out=qb[:], in_=fm_block_ap(g["q1t"], T, t0, 128)),
                writes=[qb.buf], tag="ldq1")
        ctx.dma("sp", lambda e, t0=t0, kb=kb: e.dma_start(out=kb[:], in_=fm_block_ap(g["k1t"], T, t0, 128)),
                writes=[kb.buf], tag="ldk1")
        ctx.dma("act", lambda e, t0=t0, vb=vb: e.dma_start(out=vb[:], in_=g["v1"].ap[t0:t0 + 128, :]),
                writes=[vb.buf], tag="ldv1")
        for h in range(NH1):
            psS = psS2[h % 2]
            ctx.op("pe", lambda t, h=h, kb=kb: t.transpose(ptb[:], blk_slice(kb, h, 128), idb[:]),
                   reads=[kb.buf, idb.buf], writes=[ptb.buf])
            ctx.op("dve", lambda v, h=h, ktTs=ktTs: v.tensor_copy(blk_slice(ktTs, h, 128), ptb[:]),
                   reads=[ptb.buf], writes=[ktTs.buf])
            ctx.mm(lambda t, h=h, kb=kb, qb=qb, psS=psS: t.matmul(psS[:], blk_slice(kb, h, 128),
                                                                  blk_slice(qb, h, 128),
                                                                  start=True, stop=True),
                   [kb.buf, qb.buf], psS.buf, tag="ret_s")
            ctx.op("dve", lambda v, h=h, psS=psS, aSs=aSs: v.tensor_tensor(blk_slice(aSs, h, 128),
                                                                           psS[:], csl[:], op=ALU.mult),
                   reads=[psS.buf, csl.buf], writes=[aSs.buf])

    def stageB(ck):
        p = ck % 2
        t0 = 128 * ck
        qb, vb = qb2[p], vb2[p]
        ktTs, aSs = ktTs2[p], aSs2[p]
        srows = srows2[p]; nmd = nmd2[p]
        for h in range(NH1):
            Rb = Rb2[h % 2]; pav = pav2[h % 2]
            outb = outb2[h % 2]; sqb = sqb2[h % 2]
            ctx.op("dve", lambda v, h=h, Rb=Rb: v.tensor_copy(Rb[:], R[h][:]), reads=[R[h].buf], writes=[Rb.buf])
            for vc in range(2):
                ctx.mm(lambda t, h=h, vc=vc, vb=vb, aSs=aSs, pav=pav: t.matmul(
                    pav[:, 128 * vc:128 * (vc + 1)],
                    vb[:, 256 * h + 128 * vc:256 * h + 128 * (vc + 1)],
                    blk_slice(aSs, h, 128), start=True, stop=False),
                    [vb.buf, aSs.buf], pav.buf, acc_first=(vc == 0), acc_last=False, tag="ret_av")
                ctx.mm(lambda t, h=h, vc=vc, Rb=Rb, qb=qb, pav=pav: t.matmul(
                    pav[:, 128 * vc:128 * (vc + 1)], Rb[:, 128 * vc:128 * (vc + 1)],
                    blk_slice(qb, h, 128), start=False, stop=True),
                    [Rb.buf, qb.buf], pav.buf, acc_first=False, acc_last=(vc == 1), tag="ret_in")
            ctx.mm(lambda t, h=h, ktTs=ktTs, vb=vb: t.matmul(pkv[:], blk_slice(ktTs, h, 128),
                                                             vb[:, 256 * h:256 * (h + 1)],
                                                             start=True, stop=True),
                   [ktTs.buf, vb.buf], pkv.buf, tag="ret_kv")
            ctx.op("dve", lambda v, h=h: v.tensor_tensor(R[h][:], R[h][:], pkv[:], op=ALU.add),
                   reads=[R[h].buf, pkv.buf], writes=[R[h].buf])
            ctx.op("dve", lambda v, h=h: v.tensor_scalar(out=R[h][:], in0=R[h][:], scalar1=g128[:, h:h + 1],
                                                         scalar2=None, op0=ALU.mult),
                   reads=[R[h].buf, g128.buf], writes=[R[h].buf])
            ctx.op("act", lambda sc, pav=pav, outb=outb: sc.copy(outb[:], pav[:]),
                   reads=[pav.buf], writes=[outb.buf])
            ctx.op("dve", lambda v, outb=outb, sqb=sqb: v.tensor_tensor(sqb[:], outb[:], outb[:], op=ALU.mult),
                   reads=[outb.buf], writes=[sqb.buf])
            for vc in range(2):
                ctx.mm(lambda t, vc=vc, outb=outb: t.matmul(pstat[0:1, 0:128], oncb[:],
                                                            outb[:, 128 * vc:128 * (vc + 1)],
                                                            start=(vc == 0), stop=(vc == 1)),
                       [oncb.buf, outb.buf], pstat.buf, acc_first=(vc == 0),
                       acc_last=(vc == 1), tag="gn_s")
            for vc in range(2):
                ctx.mm(lambda t, vc=vc, sqb=sqb: t.matmul(pstat[0:1, 128:256], oncb[:],
                                                          sqb[:, 128 * vc:128 * (vc + 1)],
                                                          start=(vc == 0), stop=(vc == 1)),
                       [oncb.buf, sqb.buf], pstat.buf, acc_first=(vc == 0),
                       acc_last=(vc == 1), tag="gn_s2")
            ctx.op("dve", lambda v, h=h, srows=srows: v.tensor_copy(srows[0:1, 256 * h:256 * (h + 1)], pstat[:]),
                   reads=[pstat.buf], writes=[srows.buf])
            ctx.op("dve", lambda v, h=h, outb=outb, nmd=nmd: v.tensor_copy(nmd[:, 256 * h:256 * (h + 1)], outb[:]),
                   reads=[outb.buf], writes=[nmd.buf])
        stats = stats2[p]; mu = mu2[p]; rsd = rsd2[p]
        arow = arow2[p]; brow = brow2[p]; brf = brf2[p]
        def sview(off, srows=srows):
            a = srows[:]
            return bass.AP(a.tensor, a.offset + off, [list(a.ap[0]), [256, 8], [1, 128]])
        def v1024(tile):
            a = tile[:]
            return bass.AP(a.tensor, a.offset, [list(a.ap[0]), [128, 8], [1, 128]])
        ctx.op("dve", lambda v, mu=mu, srows=srows: v.tensor_scalar(out=v1024(mu), in0=sview(0), scalar1=1.0 / 256.0,
                                                scalar2=None, op0=ALU.mult),
               reads=[srows.buf], writes=[mu.buf])
        ctx.op("dve", lambda v, rsd=rsd, srows=srows: v.tensor_scalar(out=v1024(rsd), in0=sview(128), scalar1=1.0 / 256.0,
                                                scalar2=None, op0=ALU.mult),
               reads=[srows.buf], writes=[rsd.buf])
        ctx.op("dve", lambda v, stats=stats, mu=mu: v.tensor_tensor(stats[:, 0:1024], mu[:], mu[:], op=ALU.mult),
               reads=[mu.buf], writes=[stats.buf])
        ctx.op("dve", lambda v, rsd=rsd, stats=stats: v.tensor_tensor(rsd[:], rsd[:], stats[:, 0:1024], op=ALU.subtract),
               reads=[rsd.buf, stats.buf], writes=[rsd.buf])
        ctx.op("dve", lambda v, rsd=rsd: v.tensor_scalar(out=rsd[:], in0=rsd[:], scalar1=EPS,
                                                scalar2=None, op0=ALU.add),
               reads=[rsd.buf], writes=[rsd.buf])
        ctx.op("act", lambda s, rsd=rsd: s.activation(rsd[:], rsd[:], AF.Ln),
               reads=[rsd.buf], writes=[rsd.buf])
        ctx.op("act", lambda s, rsd=rsd: s.activation(rsd[:], rsd[:], AF.Exp, scale=-0.5),
               reads=[rsd.buf], writes=[rsd.buf])
        ctx.op("dve", lambda v, arow=arow, rsd=rsd: v.tensor_copy(arow[:], rsd[:]), reads=[rsd.buf], writes=[arow.buf])
        ctx.op("dve", lambda v, brf=brf, mu=mu, rsd=rsd: v.tensor_tensor(brf[:], mu[:], rsd[:], op=ALU.mult),
               reads=[mu.buf, rsd.buf], writes=[brf.buf])
        ctx.op("dve", lambda v, brow=brow, brf=brf: v.tensor_scalar(out=brow[:], in0=brf[:], scalar1=-1.0, scalar2=None,
                                                op0=ALU.mult),
               reads=[brf.buf], writes=[brow.buf])
        Ab = abf2[p]; Bb = tmpn2[p]
        for j in range(2):
            ctx.mm(lambda t, j=j, arow=arow: t.matmul(pbc[:], onrb[:], arow[0:1, 512 * j:512 * (j + 1)],
                                           start=True, stop=True),
                   [onrb.buf, arow.buf], pbc.buf, tag="gn_bc")
            ctx.op("act", lambda sc, j=j, Ab=Ab: sc.copy(Ab[:, 512 * j:512 * (j + 1)], pbc[:]),
                   reads=[pbc.buf], writes=[Ab.buf])
            ctx.mm(lambda t, j=j, brow=brow: t.matmul(pbc[:], onrb[:], brow[0:1, 512 * j:512 * (j + 1)],
                                           start=True, stop=True),
                   [onrb.buf, brow.buf], pbc.buf, tag="gn_bc2")
            ctx.op("act", lambda sc, j=j, Bb=Bb: sc.copy(Bb[:, 512 * j:512 * (j + 1)], pbc[:]),
                   reads=[pbc.buf], writes=[Bb.buf])
        def vcview(tile, vc):
            a = tile[:]
            return bass.AP(a.tensor, a.offset + 128 * vc, [list(a.ap[0]), [256, 8], [1, 128]])
        def bview(tile):
            a = tile[:]
            return bass.AP(a.tensor, a.offset, [list(a.ap[0]), [128, 8], [1, 128]])
        for vc in range(2):
            ctx.op("dve", lambda v, vc=vc, nmd=nmd, Ab=Ab: v.tensor_tensor(
                vcview(nmd, vc), vcview(nmd, vc), bview(Ab), op=ALU.mult),
                reads=[nmd.buf, Ab.buf], writes=[nmd.buf])
            ctx.op("dve", lambda v, vc=vc, nmd=nmd, Bb=Bb: v.tensor_tensor(
                vcview(nmd, vc), vcview(nmd, vc), bview(Bb), op=ALU.add),
                reads=[nmd.buf, Bb.buf], writes=[nmd.buf])
        ctx.dma("sp", lambda e, t0=t0, nmd=nmd: e.dma_start(
            out=bass.AP(g["normed1"].ap.tensor, g["normed1"].ap.offset + t0,
                        [[T, 128], [128 * T, 16], [1, 128]]),
            in_=bass.AP(nmd[:].tensor, nmd[:].offset, [list(nmd[:].ap[0]), [128, 16], [1, 128]])),
            reads=[nmd.buf], tag="stnmd")

    for ck in range(ncnk + 1):
        if ck < ncnk:
            stageA(ck)
        if ck >= 1:
            stageB(ck - 1)



def emit_gateproj(ctx, nc, g, T):
    nt2 = T // 256
    wg1 = [sbuf(ctx, f"p7_wg{c}", [128, 2048], BF16) for c in range(8)]
    for c in range(8):
        ctx.dma("pool", lambda e, c=c: e.dma_start(out=wg1[c][:], in_=g["L1_attn_W_g"].ap[128 * c:128 * (c + 1), :]),
                writes=[wg1[c].buf], tag="ldwg1")
    wp = [sbuf(ctx, f"p7_wp{c}", [128, 1024], BF16) for c in range(16)]
    for c in range(16):
        ctx.dma("pool", lambda e, c=c: e.dma_start(out=wp[c][:], in_=g["L1_attn_W_proj"].ap[128 * c:128 * (c + 1), :]),
                writes=[wp[c].buf], tag="ldwp")
    xf2 = [sbuf(ctx, f"p7_xf{i}", [128, 8 * 256], BF16) for i in range(2)]
    nt2b = [sbuf(ctx, f"p7_nt{i}", [128, 16 * 256], BF16) for i in range(2)]
    gateb2 = [sbuf(ctx, f"p7_gateb{i}", [128, 256], BF16) for i in range(2)]
    gtb = sbuf(ctx, "p7_gtb", [128, 16 * 256], BF16)
    s1t2 = [sbuf(ctx, f"p7_s1t{i}", [128, 256], BF16) for i in range(2)]
    psg2 = [psum(ctx, f"p7_psg{i}", [128, 256], F32) for i in range(2)]
    pd2 = [psum(ctx, f"p7_pd{i}", [128, 256], F32) for i in range(2)]
    for it in range(nt2):
        ts0 = 256 * it
        xf = xf2[it % 2]; nt = nt2b[it % 2]
        ctx.dma("sp", lambda e, ts0=ts0, xf=xf: e.dma_start(out=xf[:], in_=fm_block_ap(g["xn1"], T, ts0, 256)),
                writes=[xf.buf], tag="ldxn1b")
        ctx.dma("pool", lambda e, ts0=ts0, nt=nt: e.dma_start(out=nt[:], in_=fm_block_ap(g["normed1"], T, ts0, 256, nchunk=16)),
                writes=[nt.buf], tag="ldnmd")
        for gc in range(16):
            psg = psg2[gc % 2]; gateb = gateb2[gc % 2]
            for c in range(8):
                ctx.mm(lambda t, gc=gc, c=c, psg=psg, xf=xf: t.matmul(psg[:], wg1[c][:, 128 * gc:128 * (gc + 1)],
                                                      blk_slice(xf, c, 256), start=(c == 0), stop=(c == 7)),
                       [wg1[c].buf, xf.buf], psg.buf, acc_first=(c == 0), acc_last=(c == 7), tag="g1mm")
            ctx.op("act", lambda s, psg=psg, gateb=gateb: s.activation(gateb[:], psg[:], AF.Silu),
                   reads=[psg.buf], writes=[gateb.buf])
            ctx.op("dve", lambda v, gc=gc, gateb=gateb, nt=nt: v.tensor_tensor(blk_slice(gtb, gc, 256), gateb[:],
                                                           blk_slice(nt, gc, 256), op=ALU.mult),
                   reads=[gateb.buf, nt.buf], writes=[gtb.buf])
        for dc in range(8):
            pd = pd2[dc % 2]; s1t = s1t2[dc % 2]
            for c in range(16):
                ctx.mm(lambda t, dc=dc, c=c, pd=pd: t.matmul(pd[:], wp[c][:, 128 * dc:128 * (dc + 1)],
                                                      blk_slice(gtb, c, 256), start=(c == 0), stop=(c == 15)),
                       [wp[c].buf, gtb.buf], pd.buf, acc_first=(c == 0), acc_last=(c == 15), tag="pjmm")
            ctx.op("dve", lambda v, pd=pd, s1t=s1t: v.tensor_copy(s1t[:], pd[:]), reads=[pd.buf], writes=[s1t.buf])
            ctx.dma("act", lambda e, dc=dc, ts0=ts0, s1t=s1t: e.dma_start(
                out=g["s1"].ap[128 * dc:128 * (dc + 1), ts0:ts0 + 256], in_=s1t[:]),
                reads=[s1t.buf], tag="sts1")



def emit_ffn1(ctx, nc, g, T):
    nt2 = T // 256
    idb = sbuf(ctx, "p8_idb", [128, 128], BF16)
    ctx.dma("sp", lambda e: e.dma_start(out=idb[:], in_=g["c_identb"].ap), writes=[idb.buf])
    w1t = [sbuf(ctx, f"p8_w1t{c}", [128, 4096], BF16) for c in range(8)]
    w2t = [sbuf(ctx, f"p8_w2t{c}", [128, 1024], BF16) for c in range(32)]
    wtmp = [sbuf(ctx, f"p8_wtmp{i}", [128, 4096], BF16) for i in range(2)]
    ptw2 = [psum(ctx, f"p8_ptw{i}", [128, 128], BF16) for i in range(2)]
    for fc in range(32):
        w = wtmp[fc % 2]
        ctx.dma("pool", lambda e, fc=fc, w=w: e.dma_start(out=w[:, 0:1024],
                                                          in_=g["w1n"].ap[128 * fc:128 * (fc + 1), :]),
                writes=[w.buf], tag="ldw1t")
        for c in range(8):
            ptw = ptw2[c % 2]
            ctx.op("pe", lambda t, w=w, c=c, ptw=ptw: t.transpose(ptw[:], w[:, 128 * c:128 * (c + 1)], idb[:]),
                   reads=[w.buf, idb.buf], writes=[ptw.buf])
            ctx.op("dve", lambda v, c=c, fc=fc, ptw=ptw: v.tensor_copy(w1t[c][:, 128 * fc:128 * (fc + 1)], ptw[:]),
                   reads=[ptw.buf], writes=[w1t[c].buf])
    for dc in range(8):
        w = wtmp[dc % 2]
        ctx.dma("pool", lambda e, dc=dc, w=w: e.dma_start(out=w[:], in_=g["w2n"].ap[128 * dc:128 * (dc + 1), :]),
                writes=[w.buf], tag="ldw2t")
        for fc in range(32):
            ptw = ptw2[fc % 2]
            ctx.op("pe", lambda t, w=w, fc=fc, ptw=ptw: t.transpose(ptw[:], w[:, 128 * fc:128 * (fc + 1)], idb[:]),
                   reads=[w.buf, idb.buf], writes=[ptw.buf])
            ctx.op("dve", lambda v, dc=dc, fc=fc, ptw=ptw: v.tensor_copy(w2t[fc][:, 128 * dc:128 * (dc + 1)], ptw[:]),
                   reads=[ptw.buf], writes=[w2t[fc].buf])
    b1 = sbuf(ctx, "p8_b1", [128, 32], F32)
    b2 = sbuf(ctx, "p8_b2", [128, 8], F32)
    ctx.dma("sp", lambda e: e.dma_start(out=b1[:], in_=g["L1_ffn_b1"].ap),
            writes=[b1.buf])
    ctx.dma("sp", lambda e: e.dma_start(out=b2[:], in_=g["L1_ffn_b2"].ap),
            writes=[b2.buf])
    xf2 = [sbuf(ctx, f"p8_xf{i}", [128, 8 * 512], BF16) for i in range(2)]
    hb = sbuf(ctx, "p8_hb", [128, 32 * 512], BF16)
    f1b2 = [sbuf(ctx, f"p8_f1b{i}", [128, 512], BF16) for i in range(2)]
    psh2 = [psum(ctx, f"p8_psh{i}", [128, 512], F32) for i in range(2)]
    pdn2 = [psum(ctx, f"p8_pdn{i}", [128, 512], F32) for i in range(2)]
    for it in range(T // 512):
        ts0 = 512 * it
        xf = xf2[it % 2]
        ctx.dma("sp", lambda e, ts0=ts0, xf=xf: e.dma_start(out=xf[:], in_=fm_block_ap(g["xn1"], T, ts0, 512)),
                writes=[xf.buf], tag="ldxn1c")
        for fc in range(32):
            psh = psh2[fc % 2]
            for c in range(8):
                ctx.mm(lambda t, fc=fc, c=c, psh=psh, xf=xf: t.matmul(psh[:], w1t[c][:, 128 * fc:128 * (fc + 1)],
                                                      blk_slice(xf, c, 512), start=(c == 0), stop=(c == 7)),
                       [w1t[c].buf, xf.buf], psh.buf, acc_first=(c == 0), acc_last=(c == 7), tag="h1mm")
            ctx.op("act", lambda s, fc=fc, psh=psh: s.activation(blk_slice(hb, fc, 512), psh[:], AF.Gelu,
                                                        bias=b1[:, fc:fc + 1]),
                   reads=[psh.buf, b1.buf], writes=[hb.buf], tag="gelu")
        for dc in range(8):
            pdn = pdn2[dc % 2]; f1b = f1b2[dc % 2]
            for fc in range(32):
                ctx.mm(lambda t, dc=dc, fc=fc, pdn=pdn: t.matmul(pdn[:], w2t[fc][:, 128 * dc:128 * (dc + 1)],
                                                        blk_slice(hb, fc, 512), start=(fc == 0), stop=(fc == 31)),
                       [w2t[fc].buf, hb.buf], pdn.buf, acc_first=(fc == 0), acc_last=(fc == 31), tag="f1mm")
            ctx.op("dve", lambda v, dc=dc, pdn=pdn, f1b=f1b: v.tensor_scalar(out=f1b[:], in0=pdn[:], scalar1=b2[:, dc:dc + 1],
                                                           scalar2=None, op0=ALU.add),
                   reads=[pdn.buf, b2.buf], writes=[f1b.buf])
            ctx.dma("act", lambda e, dc=dc, ts0=ts0, f1b=f1b: e.dma_start(
                out=g["f1out"].ap[128 * dc:128 * (dc + 1), ts0:ts0 + 512], in_=f1b[:]),
                reads=[f1b.buf], tag="stf1")



def emit_ternary(ctx, nc, g, T):
    onc = sbuf(ctx, "pt_onc", [128, 1], F32)
    onr = sbuf(ctx, "pt_onr", [1, 128], F32)
    ctx.dma("sp", lambda e: e.dma_start(out=onc[:], in_=g["c_onescol"].ap), writes=[onc.buf])
    ctx.dma("sp", lambda e: e.dma_start(out=onr[:], in_=g["c_onesrow"].ap), writes=[onr.buf])
    for (wname, oname, rows, cols) in (("L1_ffn_W1", "w1n", 4096, 1024),
                                       ("L1_ffn_W2", "w2n", 1024, 4096)):
        nrc = rows // 128
        wt2 = [sbuf(ctx, f"pt_wt_{oname}{i}", [128, cols], F32) for i in range(3)]
        rowsum2 = [sbuf(ctx, f"pt_rs_{oname}{i}", [128, 1], F32) for i in range(2)]
        racc = sbuf(ctx, f"pt_ra_{oname}", [128, 1], F32)
        thr = sbuf(ctx, f"pt_thr_{oname}", [1, 1], F32)
        thv = sbuf(ctx, f"pt_thv_{oname}", [128, 1], F32)
        sgn2 = [sbuf(ctx, f"pt_sg_{oname}{i}", [128, cols], F32) for i in range(2)]
        mag2 = [sbuf(ctx, f"pt_mg_{oname}{i}", [128, cols], F32) for i in range(2)]
        outw2 = [sbuf(ctx, f"pt_ow_{oname}{i}", [128, cols], BF16) for i in range(2)]
        pt1 = psum(ctx, f"pt_p1_{oname}", [1, 1], F32)
        pt2 = psum(ctx, f"pt_p2_{oname}", [128, 1], F32)
        ctx.op("dve", lambda v, racc=racc: v.memset(racc[:], 0.0), writes=[racc.buf])
        for rc in range(nrc):
            wt = wt2[rc % 3]
            rowsum = rowsum2[rc % 2]
            ctx.dma(("sp", "act", "pool")[rc % 3], lambda e, wname=wname, rc=rc, wt=wt: e.dma_start(
                out=wt[:], in_=g[wname].ap[128 * rc:128 * (rc + 1), :]),
                writes=[wt.buf], tag="tern_ld")
            ctx.op("dve", lambda v, wt=wt, rowsum=rowsum: v.reduce_sum(
                rowsum[:], wt[:], axis=AX.X, apply_absolute_value=True),
                reads=[wt.buf], writes=[rowsum.buf])
            ctx.op("dve", lambda v, racc=racc, rowsum=rowsum: v.tensor_tensor(
                racc[:], racc[:], rowsum[:], op=ALU.add),
                reads=[racc.buf, rowsum.buf], writes=[racc.buf])
        ctx.mm(lambda t, racc=racc, pt1=pt1: t.matmul(pt1[:], racc[:], onc[:], start=True, stop=True),
               [racc.buf, onc.buf], pt1.buf, tag="tern_tot")
        ctx.op("dve", lambda v, thr=thr, pt1=pt1: v.tensor_scalar(
            out=thr[:], in0=pt1[:], scalar1=1.0 / (rows * cols), scalar2=1e-8,
            op0=ALU.mult, op1=ALU.max),
            reads=[pt1.buf], writes=[thr.buf])
        ctx.op("dve", lambda v, thr=thr: v.tensor_scalar(
            out=thr[:], in0=thr[:], scalar1=0.5, scalar2=None, op0=ALU.mult),
            reads=[thr.buf], writes=[thr.buf])
        ctx.mm(lambda t, thr=thr, pt2=pt2: t.matmul(pt2[:], onr[:], thr[:], start=True, stop=True),
               [onr.buf, thr.buf], pt2.buf, tag="tern_bc")
        ctx.op("dve", lambda v, thv=thv, pt2=pt2: v.tensor_copy(thv[:], pt2[:]),
               reads=[pt2.buf], writes=[thv.buf])
        for rc in range(nrc):
            wt = wt2[rc % 3]
            sgn = sgn2[rc % 2]; mag = mag2[rc % 2]; outw = outw2[rc % 2]
            ctx.dma(("sp", "act", "pool")[rc % 3], lambda e, wname=wname, rc=rc, wt=wt: e.dma_start(
                out=wt[:], in_=g[wname].ap[128 * rc:128 * (rc + 1), :]),
                writes=[wt.buf], tag="tern_ld2")
            ctx.op("act", lambda s, sgn=sgn, wt=wt: s.sign(sgn[:], wt[:]),
                   reads=[wt.buf], writes=[sgn.buf])
            ctx.op("act", lambda s, mag=mag, wt=wt: s.activation(mag[:], wt[:], AF.Abs),
                   reads=[wt.buf], writes=[mag.buf])
            ctx.op("dve", lambda v, mag=mag, thv=thv: v.tensor_scalar(
                out=mag[:], in0=mag[:], scalar1=thv[:], scalar2=None, op0=ALU.subtract),
                reads=[mag.buf, thv.buf], writes=[mag.buf])
            ctx.op("act", lambda s, mag=mag: s.sign(mag[:], mag[:]),
                   reads=[mag.buf], writes=[mag.buf])
            ctx.op("dve", lambda v, mag=mag: v.tensor_scalar(
                out=mag[:], in0=mag[:], scalar1=0.0, scalar2=None, op0=ALU.max),
                reads=[mag.buf], writes=[mag.buf])
            ctx.op("dve", lambda v, outw=outw, sgn=sgn, mag=mag: v.tensor_tensor(
                outw[:], sgn[:], mag[:], op=ALU.mult),
                reads=[sgn.buf, mag.buf], writes=[outw.buf])
            ctx.dma(("act", "pool", "sp")[rc % 3], lambda e, oname=oname, rc=rc, outw=outw: e.dma_start(
                out=g[oname].ap[128 * rc:128 * (rc + 1), :], in_=outw[:]),
                reads=[outw.buf], tag="tern_st")



def emit_combine(ctx, nc, g, T):
    nt2 = T // 256
    idb = sbuf(ctx, "p9_idb", [128, 128], BF16)
    ctx.dma("sp", lambda e: e.dma_start(out=idb[:], in_=g["c_identb"].ap), writes=[idb.buf])
    f12 = [sbuf(ctx, f"p9_f1{i}", [128, 8 * 256], BF16) for i in range(2)]
    s12 = [sbuf(ctx, f"p9_s1{i}", [128, 8 * 256], BF16) for i in range(2)]
    g1b2 = [sbuf(ctx, f"p9_g1b{i}", [128, 256], F32) for i in range(2)]
    prodb2 = [sbuf(ctx, f"p9_prodb{i}", [128, 256], BF16) for i in range(2)]
    xrow2 = [[sbuf(ctx, f"p9_xrow{i}_{s}", [128, 1024], BF16) for s in range(2)] for i in range(2)]
    xt2 = [sbuf(ctx, f"p9_xt{i}", [128, 1024], F32) for i in range(2)]
    x2f2 = [sbuf(ctx, f"p9_x2f{i}", [128, 1024], F32) for i in range(2)]
    ptr2 = [psum(ctx, f"p9_ptr{i}", [128, 128], BF16) for i in range(2)]
    for it in range(nt2):
        ts0 = 256 * it
        f1 = f12[it % 2]; s1 = s12[it % 2]; xrow = xrow2[it % 2]
        ctx.dma("sp", lambda e, ts0=ts0, f1=f1: e.dma_start(out=f1[:], in_=fm_block_ap(g["f1out"], T, ts0, 256)),
                writes=[f1.buf], tag="ldf1")
        ctx.dma("pool", lambda e, ts0=ts0, s1=s1: e.dma_start(out=s1[:], in_=fm_block_ap(g["s1"], T, ts0, 256)),
                writes=[s1.buf], tag="lds1")
        for dc in range(8):
            g1b = g1b2[dc % 2]; prodb = prodb2[dc % 2]
            ctx.op("act", lambda s, dc=dc, f1=f1, g1b=g1b: s.activation(g1b[:], blk_slice(f1, dc, 256), AF.Silu),
                   reads=[f1.buf], writes=[g1b.buf])
            ctx.op("dve", lambda v, dc=dc, g1b=g1b, prodb=prodb, s1=s1: v.tensor_tensor(prodb[:], g1b[:], blk_slice(s1, dc, 256),
                                                           op=ALU.mult),
                   reads=[g1b.buf, s1.buf], writes=[prodb.buf])
            for sx in range(2):
                ptr = ptr2[sx]
                ctx.op("pe", lambda t, sx=sx, prodb=prodb, ptr=ptr: t.transpose(ptr[:], prodb[:, 128 * sx:128 * (sx + 1)], idb[:]),
                       reads=[prodb.buf, idb.buf], writes=[ptr.buf])
                ctx.op("dve", lambda v, sx=sx, dc=dc, ptr=ptr, xrow=xrow: v.tensor_copy(xrow[sx][:, 128 * dc:128 * (dc + 1)], ptr[:]),
                       reads=[ptr.buf], writes=[xrow[sx].buf])
        for sx in range(2):
            t0 = ts0 + 128 * sx
            xt = xt2[sx]; x2f = x2f2[sx]
            ctx.dma("sp", lambda e, t0=t0, xt=xt: e.dma_start(out=xt[:], in_=g["x1"].ap[t0:t0 + 128, :]),
                    writes=[xt.buf], tag="ldx1c")
            ctx.op("dve", lambda v, sx=sx, xt=xt, x2f=x2f, xrow=xrow: v.tensor_tensor(x2f[:], xt[:], xrow[sx][:], op=ALU.add),
                   reads=[xt.buf, xrow[sx].buf], writes=[x2f.buf])
            ctx.dma("sp", lambda e, t0=t0, x2f=x2f: e.dma_start(out=g["out"].ap[t0:t0 + 128, :], in_=x2f[:]),
                    reads=[x2f.buf], tag="stout")



def _shapes(T):
    sh = {
        "x": ([T, 1024], F32),
        "L0_attn_Wqkv": ([1024, 3072], F32),
        "L0_attn_bqkv": ([128, 24], F32),
        "L0_attn_Wout": ([1024, 1024], F32),
        "L0_attn_bout": ([128, 8], F32),
        "L0_ffn_w_gate": ([3072, 1024], I32),
        "L0_ffn_w_up": ([3072, 1024], I32),
        "L0_ffn_w_down": ([1024, 3072], I32),
        "L1_attn_W_qkv": ([1024, 4096], F32),
        "L1_attn_W_g": ([1024, 2048], F32),
        "L1_attn_W_proj": ([2048, 1024], F32),
        "L1_attn_decay": ([1, 8], F32),
        "L1_ffn_W1": ([4096, 1024], F32),
        "L1_ffn_b1": ([128, 32], F32),
        "L1_ffn_W2": ([1024, 4096], F32),
        "L1_ffn_b2": ([128, 8], F32),
        "xn0": ([1024, T], BF16), "q0h": ([1024, T], BF16), "k0r": ([1024, T], BF16),
        "v0p": ([T, 1040], BF16), "ch0": ([128, 16], F32),
        "s0": ([1024, T], BF16), "x1": ([T, 1024], F32), "xn1": ([1024, T], BF16),
        "q1t": ([1024, T], BF16), "k1t": ([1024, T], BF16), "v1": ([T, 2048], BF16),
        "normed1": ([2048, T], BF16), "s1": ([1024, T], BF16), "f1out": ([1024, T], BF16),
        "dk_bq": ([128, 1024], BF16), "dk_bk": ([128, 1024], BF16), "dk_g128": ([128, 8], F32),
        "w1n": ([4096, 1024], BF16), "w2n": ([1024, 4096], BF16),
        "out": ([T, 1024], F32),
    }
    for k, v in host_constants(T).items():
        sh["c_" + k] = (list(v.shape), {np.dtype(np.float32): F32, np.dtype(BF): BF16}[v.dtype])
    return sh


PHASES = [
    ("decay", emit_decayprep,
     ["L1_attn_decay", "c_iotar", "c_onesrow_b", "c_onesrow"],
     ["dk_bq", "dk_bk", "dk_g128"]),
    ("pre0", emit_pre0,
     ["x", "L0_attn_Wqkv", "L0_attn_bqkv", "c_r0cq", "c_r0sq", "c_r0ck", "c_r0sk",
      "c_p0t", "c_identb", "c_onescol", "c_onesrow"],
     ["xn0", "q0h", "k0r", "v0p", "ch0"]),
    ("attn0", emit_attn0,
     ["q0h", "k0r", "v0p", "ch0", "L0_attn_Wout", "L0_attn_bout", "L0_attn_bqkv",
      "c_onesrow_b"],
     ["s0"]),
    ("ffn0", emit_ffn0,
     ["xn0", "s0", "x", "L0_ffn_w_gate", "L0_ffn_w_up", "L0_ffn_w_down", "c_identb"],
     ["x1"]),
    ("ln1", emit_ln1, ["x1", "c_identb"], ["xn1"]),
    ("qkv1", emit_qkv1,
     ["xn1", "L1_attn_W_qkv", "c_x1cq", "c_x1sq", "c_x1ck", "c_x1sk", "c_p1t",
      "dk_bq", "dk_bk"],
     ["q1t", "k1t", "v1"]),
    ("ret", emit_retention,
     ["q1t", "k1t", "v1", "dk_g128", "c_causal", "c_identb", "c_onescol", "c_onesrow_b"],
     ["normed1"]),
    ("gp", emit_gateproj, ["xn1", "normed1", "L1_attn_W_g", "L1_attn_W_proj"], ["s1"]),
    ("tern", emit_ternary,
     ["L1_ffn_W1", "L1_ffn_W2", "c_onescol", "c_onesrow"], ["w1n", "w2n"]),
    ("ffn1", emit_ffn1, ["xn1", "w1n", "w2n", "L1_ffn_b1", "L1_ffn_b2", "c_identb"],
     ["f1out"]),
    ("comb", emit_combine, ["f1out", "s1", "x1", "c_identb"], ["out"]),
]


def _build_phase_nc(name, emit, ins, outs, T):
    from contextlib import ExitStack
    nc = bass.Bass("TRN2", target_bir_lowering=False)
    sh = _shapes(T)
    g = {}
    for nm in ins:
        g[nm] = dram(nc, nm, sh[nm][0], sh[nm][1], kind="ExternalInput")
    for nm in outs:
        g[nm] = dram(nc, nm, sh[nm][0], sh[nm][1], kind="ExternalOutput")
    with ExitStack() as stack:
        ctx = Ctx(nc, stack)
        emit(ctx, nc, g, T)
        ctx.barrier()
        sems = {}
        for k in list(ENGINES) + ALL_DMA_SEMS:
            sems[k] = stack.enter_context(nc.semaphore(f"sem_{k}"))
        blk = stack.enter_context(nc.Block())
        ctx.finalize(blk, sems)
    return nc


def prepare_host_inputs(inputs, T):
    d = {}
    d["L0_attn_Wqkv"] = np.ascontiguousarray(inputs["L0_attn_Wqkv"], np.float32)
    d["L0_attn_bqkv"] = np.ascontiguousarray(np.asarray(inputs["L0_attn_bqkv"], np.float32).reshape(24, 128).T)
    d["L0_attn_Wout"] = np.ascontiguousarray(inputs["L0_attn_Wout"], np.float32)
    d["L0_attn_bout"] = np.ascontiguousarray(np.asarray(inputs["L0_attn_bout"], np.float32).reshape(8, 128).T)
    for k in ("L0_ffn_w_gate", "L0_ffn_w_up", "L0_ffn_w_down"):
        d[k] = np.ascontiguousarray(inputs[k], np.int32)
    for k in ("L1_attn_W_qkv", "L1_attn_W_g", "L1_attn_W_proj", "L1_ffn_W1", "L1_ffn_W2"):
        d[k] = np.ascontiguousarray(inputs[k], np.float32)
    d["L1_attn_decay"] = np.ascontiguousarray(np.asarray(inputs["L1_attn_decay"], np.float32).reshape(1, 8))
    d["L1_ffn_b1"] = np.ascontiguousarray(np.asarray(inputs["L1_ffn_b1"], np.float32).reshape(32, 128).T)
    d["L1_ffn_b2"] = np.ascontiguousarray(np.asarray(inputs["L1_ffn_b2"], np.float32).reshape(8, 128).T)
    for k, v in host_constants(T).items():
        d["c_" + k] = np.ascontiguousarray(v)
    return d


def _build_fused_nc(T):
    from contextlib import ExitStack
    nc = bass.Bass("TRN2", target_bir_lowering=False)
    sh = _shapes(T)
    ext_in = set()
    for (_, _, ins, _) in PHASES:
        ext_in.update(ins)
    inter = set()
    for (_, _, _, outs) in PHASES:
        inter.update(o for o in outs if o != "out")
    ext_in -= inter
    ext_in.discard("out")
    g = {}
    for nm in sorted(ext_in):
        g[nm] = dram(nc, nm, sh[nm][0], sh[nm][1], kind="ExternalInput")
    for nm in sorted(inter):
        g[nm] = dram(nc, nm, sh[nm][0], sh[nm][1], kind="Internal")
    g["out"] = dram(nc, "out", sh["out"][0], sh["out"][1], kind="ExternalOutput")
    ctx = Ctx(nc, None)
    for i, (name, emit, ins, outs) in enumerate(PHASES):
        with ExitStack() as pstack:
            ctx.stack = pstack
            if i:
                ctx.barrier()
            emit(ctx, nc, g, T)
    ctx.barrier()
    with ExitStack() as st:
        sems = {}
        for k in list(ENGINES) + ALL_DMA_SEMS:
            sems[k] = st.enter_context(nc.semaphore(f"sem_{k}"))
        blk = st.enter_context(nc.Block())
        ctx.finalize(blk, sems)
    return nc, sorted(ext_in)


def kernel(**inputs):
    T = 4096
    ncores = 8
    shared = prepare_host_inputs(inputs, T)
    x = np.ascontiguousarray(np.asarray(inputs["L0_attn_x"], np.float32))
    nc, ext_in = _build_fused_nc(T)
    in_maps = []
    for i in range(ncores):
        m = {nm: shared[nm] for nm in ext_in if nm != "x"}
        m["x"] = np.ascontiguousarray(x[i])
        in_maps.append(m)
    res = run_bass_kernel_spmd(nc, in_maps, core_ids=list(range(ncores)))
    return np.stack([np.asarray(res.results[i]["out"]) for i in range(ncores)], axis=0)


def kernel_unfused(**inputs):
    T = 4096
    ncores = 8
    shared = prepare_host_inputs(inputs, T)
    x = np.ascontiguousarray(np.asarray(inputs["L0_attn_x"], np.float32))
    core_vals = [dict(shared) for _ in range(ncores)]
    for i in range(ncores):
        core_vals[i]["x"] = np.ascontiguousarray(x[i])
    for (name, emit, ins, outs) in PHASES:
        nc = _build_phase_nc(name, emit, ins, outs, T)
        in_maps = [{nm: core_vals[i][nm] for nm in ins} for i in range(ncores)]
        res = run_bass_kernel_spmd(nc, in_maps, core_ids=list(range(ncores)))
        for i in range(ncores):
            for nm in outs:
                core_vals[i][nm] = np.asarray(res.results[i][nm])
    return np.stack([core_vals[i]["out"] for i in range(ncores)], axis=0)
